# version 06
# speedup vs baseline: 1.0932x; 1.0108x over previous
import math
from contextlib import ExitStack

import numpy as np
import concourse.bass as bass
import concourse.mybir as mybir
from concourse.bass_utils import run_bass_kernel_spmd

F32 = mybir.dt.float32
BF16 = mybir.dt.bfloat16
I32 = mybir.dt.int32
AF = mybir.ActivationFunctionType
ALU = mybir.AluOpType
AX = mybir.AxisListType

N_CORES = 8
N_TOK = 4096
D = 1024
NT = N_TOK // 128
DC = D // 128



class Sem:
    def __init__(self, nc, stack, name):
        self.h = stack.enter_context(nc.semaphore(name))
        self.n = 0

    def inc(self, instr, k=1):
        instr.then_inc(self.h, k)
        self.n += k
        return self.n

    def wait(self, eng, n=None):
        eng.wait_ge(self.h, self.n if n is None else n)


class Cons:

    def __init__(self, engine, fn, deps=(), reads_bank=True, dma=False):
        self.engine = engine
        self.fn = fn
        self.deps = deps
        self.reads_bank = reads_bank
        self.inc = 16 if dma else 1


class Prog:
    def __init__(self, nc, stack):
        self.nc = nc
        self.stack = stack
        self.sems = {}
        self._uid = 0

    def uid(self):
        self._uid += 1
        return self._uid

    def sem(self, name):
        if name not in self.sems:
            self.sems[name] = Sem(self.nc, self.stack, name)
        return self.sems[name]

    def sbuf(self, name, shape, dtype, stack=None):
        return (stack or self.stack).enter_context(
            self.nc.sbuf_tensor(name, shape, dtype))

    def psum(self, name, shape, dtype, stack=None):
        return (stack or self.stack).enter_context(
            self.nc.psum_tensor(name, shape, dtype))

    def seq(self, e, instr):
        sem = self.sem("chain")
        sem.inc(instr)
        sem.wait(e)
        return instr

    def dbg_dump(self, name, sb_ap, shape):
        if not getattr(self, "dbg", False):
            return
        nc = self.nc
        t = nc.dram_tensor(f"dbg_{name}", list(shape), sb_ap.dtype,
                           kind="ExternalOutput").ap()
        sd = self.sem("dbg_sd")
        with nc.Block() as b:
            @b.sync
            def _(e):
                sd.inc(e.dma_start(t, sb_ap), 16)
        with nc.Block() as b:
            @b.sync
            def _(e):
                sd.wait(e)

    def pipeline(self, items, banks, post=None, post_lag=2,
                 post_wait_streams=None):
        nc = self.nc
        mm = self.sem("pl_mm")
        pp = self.sem("pl_pp")
        nb = len(banks)
        n_items = len(items)
        sig = items[0][1]
        ns = len(sig)
        for _mmfn, cons in items:
            assert len(cons) == ns and all(
                c.engine == s.engine for c, s in zip(cons, sig)), \
                "pipeline items must share one stream signature"
        ssem = [self.sem(f"pl_s{k}") for k in range(ns)]
        mm_base = mm.n
        pp_base = pp.n
        s_base = [s.n for s in ssem]
        bank_readers = [k for k in range(ns) if sig[k].reads_bank]
        if post_wait_streams is None:
            post_wait_streams = list(range(ns))

        with nc.Block() as b:
            @b.tensor
            def _(e):
                def do_post(pi):
                    for k in post_wait_streams:
                        e.wait_ge(ssem[k].h, s_base[k] + (pi + 1) * sig[k].inc)
                    pp.inc(post(e, pi))
                for i, (mmfn, _cons) in enumerate(items):
                    if i >= nb:
                        j = i - nb
                        for k in bank_readers:
                            e.wait_ge(ssem[k].h,
                                      s_base[k] + (j + 1) * sig[k].inc)
                    mm.inc(mmfn(e, banks[i % nb]))
                    if post is not None and i >= post_lag:
                        do_post(i - post_lag)
                if post is not None:
                    for pi in range(max(0, n_items - post_lag), n_items):
                        do_post(pi)

            by_engine = {}
            for k, c in enumerate(sig):
                by_engine.setdefault(c.engine, []).append(k)

            def mk_body(ks):
                def body(e):
                    for i, (_mmfn, cons) in enumerate(items):
                        for k in ks:
                            c = cons[k]
                            e.wait_ge(mm.h, mm_base + i + 1)
                            for (dk, lag) in c.deps:
                                need = i + 1 - lag
                                if need > 0:
                                    if dk == "post":
                                        e.wait_ge(pp.h, pp_base + need)
                                    else:
                                        e.wait_ge(ssem[dk].h,
                                                  s_base[dk] + need * sig[dk].inc)
                            last = c.fn(e, banks[i % nb])
                            ssem[k].inc(last, sig[k].inc)
                return body

            for ename, ks in by_engine.items():
                getattr(b, ename)(mk_body(ks))




CONST_COLS = (1e-5, 1e-6)


def setup_common(p: Prog):
    nc = p.nc
    p.ident_bf = p.sbuf("ident_bf", [128, 128], BF16)
    p.iota_jp = p.sbuf("iota_jp", [128, 128], F32)
    p.ones_row_bf = p.sbuf("ones_row_bf", [1, 512], BF16)
    p.ones_col_bf = p.sbuf("ones_col_bf", [128, 1], BF16)
    p.cc = {v: p.sbuf(f"cc_{i}", [128, 1], F32) for i, v in enumerate(CONST_COLS)}
    with nc.Block() as b:
        @b.gpsimd
        def _(e):
            e.iota(p.iota_jp[:], [[1, 128]], base=0, channel_multiplier=-1,
                   allow_small_or_imprecise_dtypes=True)
            e.memset(p.ones_row_bf[:], 1.0)
            e.memset(p.ones_col_bf[:], 1.0)
            for v, t in p.cc.items():
                e.memset(t[:], v)
    with nc.Block() as b:
        @b.vector
        def _(e):
            e.tensor_single_scalar(p.ident_bf[:], p.iota_jp[:], 0.0, ALU.is_equal)


def setup_zsel(p: Prog):
    nc = p.nc
    p.zsel = p.sbuf("zsel", [2, 128], BF16)
    p.iota2 = p.sbuf("iota2", [2, 128], F32)
    with nc.Block() as b:
        @b.gpsimd
        def _(e):
            e.iota(p.iota2[:], [[1, 128]], base=0, channel_multiplier=-64,
                   allow_small_or_imprecise_dtypes=True)
    with nc.Block() as b:
        @b.vector
        def _(e):
            p.seq(e, e.tensor_scalar(p.zsel[:], p.iota2[:], 0.0, 64.0,
                                     ALU.is_ge, ALU.min))
            e.scalar_tensor_tensor(p.zsel[:], p.iota2[:], 64.0, p.zsel[:],
                                   ALU.is_lt, ALU.mult)


def setup_ones_f32(p: Prog):
    nc = p.nc
    p.ones_row_f32 = p.sbuf("ones_row_f32", [1, 512], F32)
    with nc.Block() as b:
        @b.gpsimd
        def _(e):
            e.memset(p.ones_row_f32[:], 1.0)


def load_weight_bf16(p, name, dram_ap, rows, cols, sem, eng, stack=None):
    t = p.sbuf(name, [128, rows // 128, cols], BF16, stack)
    src = dram_ap.rearrange("(c p) f -> p c f", p=128)
    sem.inc(eng.dma_start(t[:], src), 16)
    return t


def load_bias_row_bf16(p, name, dram_ap, n, sem, eng, stack=None):
    t = p.sbuf(name, [1, n], BF16, stack)
    sem.inc(eng.dma_start(t[:], dram_ap[None, :]), 16)
    return t


def load_bias_col_f32(p, name, dram_ap, n, sem, eng, stack=None):
    t = p.sbuf(name, [128, n // 128], F32, stack)
    src = dram_ap.rearrange("(c p) -> p c", p=128)
    sem.inc(eng.dma_start(t[:], src, allow_slow_non_contiguous=True), 16)
    return t


def ln_phase(p: Prog, x_src_dram, x_ch, xn_t, xnT, stats, CT, xd: Sem,
             tp_banks, c):
    nc = p.nc
    s1, ssq, mean, rstd, scr = stats
    CN = CT * 128
    lns1 = p.sem("ln_s1")
    lnv1 = p.sem("ln_v1")
    lns2 = p.sem("ln_s2")
    lnxn = p.sem("ln_xn")
    lntp = p.sem("ln_tp")
    lnev = p.sem("ln_ev")
    b_s1, b_v1, b_s2 = lns1.n, lnv1.n, lns2.n
    b_xn, b_tp, b_ev = lnxn.n, lntp.n, lnev.n
    b_xd = xd.n
    with nc.Block() as b:
        @b.sync
        def _(e):
            src = x_src_dram[c * CN:(c + 1) * CN, :].rearrange(
                "(t p) f -> p t f", p=128)
            xd.inc(e.dma_start(x_ch[:, 0:CT, :], src), 16)

        @b.scalar
        def _(e):
            e.wait_ge(xd.h, b_xd + 16)
            last = None
            for t in range(CT):
                e.activation(xn_t[:, t, :], x_ch[:, t, :], AF.Square,
                             accum_out=ssq[:, t:t + 1])
                last = e.activation(xn_t[:, t, :], x_ch[:, t, :], AF.Copy,
                                    accum_out=s1[:, t:t + 1])
            lns1.inc(last)
            e.wait_ge(lnv1.h, b_v1 + 1)
            lns2.inc(e.activation(rstd[:, 0:CT], rstd[:, 0:CT], AF.Sqrt,
                                  bias=p.cc[1e-5][:]))
            tau = 0
            for t in range(CT):
                for d in range(DC):
                    e.wait_ge(lntp.h, b_tp + tau + 1)
                    lnev.inc(e.activation(xnT[:, d, t * 128:(t + 1) * 128],
                                          tp_banks[tau % 2][:], AF.Copy))
                    tau += 1

        @b.vector
        def _(e):
            e.wait_ge(lns1.h, b_s1 + 1)
            p.seq(e, e.tensor_scalar(mean[:, 0:CT], s1[:, 0:CT], 1.0 / D,
                                     None, ALU.mult))
            p.seq(e, e.tensor_tensor(rstd[:, 0:CT], mean[:, 0:CT],
                                     mean[:, 0:CT], ALU.mult))
            lnv1.inc(e.scalar_tensor_tensor(rstd[:, 0:CT], ssq[:, 0:CT],
                                            1.0 / D, rstd[:, 0:CT],
                                            ALU.mult, ALU.subtract))
            e.wait_ge(lns2.h, b_s2 + 1)
            p.seq(e, e.reciprocal(rstd[:, 0:CT], rstd[:, 0:CT]))
            for t in range(CT):
                lnxn.inc(e.tensor_scalar(
                    xn_t[:, t, :], x_ch[:, t, :], mean[:, t:t + 1],
                    rstd[:, t:t + 1], ALU.subtract, ALU.mult))

        @b.tensor
        def _(e):
            tau = 0
            for t in range(CT):
                e.wait_ge(lnxn.h, b_xn + t + 1)
                for d in range(DC):
                    if tau >= 2:
                        e.wait_ge(lnev.h, b_ev + tau - 1)
                    lntp.inc(e.transpose(
                        tp_banks[tau % 2][:],
                        xn_t[:, t, d * 128:(d + 1) * 128], p.ident_bf[:]))
                    tau += 1




def build_l0_attn(p: Prog, x_in, x_out, W):
    nc = p.nc
    CH = 4
    CT = NT // CH
    CN = CT * 128

    ld = p.sem("l0a_ld")
    xd = p.sem("l0a_xd")
    st = p.sem("l0a_st")

    with ExitStack() as st_all:
        x_ch = p.sbuf("l0_x_ch", [128, CT, D], F32, st_all)
        xn_t = p.sbuf("l0_xn_t", [128, CT, D], BF16, st_all)
        xnT = p.sbuf("l0_xnT", [128, DC, CN], BF16, st_all)
        s1 = p.sbuf("l0_s1", [128, CT], F32, st_all)
        ssq = p.sbuf("l0_ssq", [128, CT], F32, st_all)
        mean = p.sbuf("l0_mean", [128, CT], F32, st_all)
        rstd = p.sbuf("l0_rstd", [128, CT], F32, st_all)
        scr = p.sbuf("l0_scr", [128, D], F32, st_all)
        kv_sb = p.sbuf("l0_kv", [128, 8, 65], F32, st_all)
        kv_bd = p.sbuf("l0_kv_bd", [128, 8, 128], BF16, st_all)
        klhs = p.sbuf("l0_klhs", [128, 8, 2], BF16, st_all)
        stats = (s1, ssq, mean, rstd, scr)

        pb = [p.psum(f"l0_pb{i}", [128, 512], F32, st_all) for i in range(4)]
        ptr = [p.psum(f"l0_ptr{i}", [128, 128], BF16, st_all) for i in range(2)]

        with ExitStack() as st_p1:
            with nc.Block() as b:
                @b.gpsimd
                def _(e):
                    p.wk = load_weight_bf16(p, "l0_wk", W["wk_w"], D, D, ld, e, st_p1)
                    p.wv = load_weight_bf16(p, "l0_wv", W["wv_w"], D, D, ld, e, st_p1)
                    p.wk_br = load_bias_row_bf16(p, "l0_wk_b", W["wk_b"], D, ld, e, st_p1)
                    p.wv_br = load_bias_row_bf16(p, "l0_wv_b", W["wv_b"], D, ld, e, st_p1)
            kt_ch = p.sbuf("l0_kt_ch", [128, CT, D], BF16, st_p1)
            km_ch = p.sbuf("l0_km_ch", [128, CT, D], BF16, st_p1)
            vt_ch = p.sbuf("l0_vt_ch", [128, CT, D], BF16, st_p1)

            for c in range(CH):
                ln_phase(p, x_in, x_ch, xn_t, xnT, stats, CT, xd, ptr, c)
                items = []
                for t in range(CT):
                    for half in range(2):
                        for wi in range(2):
                            def mmfn(e, bank, t=t, half=half, wi=wi, c=c):
                                if c == 0 and t == 0 and half == 0 and wi == 0:
                                    ld.wait(e)
                                w_sb = p.wk if wi == 0 else p.wv
                                brow = p.wk_br if wi == 0 else p.wv_br
                                for d in range(DC):
                                    e.matmul(bank[:],
                                             xnT[:, d, t * 128:(t + 1) * 128],
                                             w_sb[:, d, half * 512:(half + 1) * 512],
                                             start=(d == 0), stop=False)
                                return e.matmul(
                                    bank[:], p.ones_row_bf[0:1, 0:128],
                                    brow[0:1, half * 512:(half + 1) * 512],
                                    start=False, stop=True)

                            if wi == 0:
                                def evfn(e, bank, t=t, half=half):
                                    sl = slice(half * 512, (half + 1) * 512)
                                    e.tensor_scalar(km_ch[:, t, sl], bank[:],
                                                    0.0, None, ALU.min)
                                    return e.tensor_scalar(kt_ch[:, t, sl],
                                                           bank[:], 0.0, None,
                                                           ALU.max)
                                items.append((mmfn, [Cons("vector", evfn)]))
                            else:
                                def evfn(e, bank, t=t, half=half):
                                    sl = slice(half * 512, (half + 1) * 512)
                                    return e.tensor_copy(vt_ch[:, t, sl], bank[:])
                                items.append((mmfn, [Cons("vector", evfn)]))
                p.pipeline(items, pb)
                with nc.Block() as b:
                    @b.scalar
                    def _(e):
                        for t in range(CT):
                            e.activation(km_ch[:, t, :], km_ch[:, t, :], AF.Exp)
                with nc.Block() as b:
                    @b.vector
                    def _(e):
                        for t in range(CT):
                            e.tensor_add(kt_ch[:, t, :], kt_ch[:, t, :],
                                         km_ch[:, t, :])
                items = []
                for t in range(CT):
                    for hp in range(8):
                        def mmfn(e, bank, t=t, hp=hp):
                            e.matmul(bank[:, 0:128],
                                     kt_ch[:, t, hp * 128:(hp + 1) * 128],
                                     vt_ch[:, t, hp * 128:(hp + 1) * 128],
                                     start=True, stop=False)
                            return e.matmul(bank[:, 128:129],
                                            kt_ch[:, t, hp * 128:(hp + 1) * 128],
                                            p.ones_col_bf[:],
                                            start=True, stop=True)

                        def evfn(e, bank, t=t, hp=hp, c=c):
                            first = (c == 0 and t == 0)
                            if first:
                                e.tensor_copy(kv_sb[0:64, hp, 0:64],
                                              bank[0:64, 0:64])
                                e.tensor_copy(kv_sb[64:128, hp, 0:64],
                                              bank[64:128, 64:128])
                                return e.tensor_copy(kv_sb[:, hp, 64:65],
                                                     bank[:, 128:129])
                            e.tensor_add(kv_sb[0:64, hp, 0:64],
                                         kv_sb[0:64, hp, 0:64], bank[0:64, 0:64])
                            e.tensor_add(kv_sb[64:128, hp, 0:64],
                                         kv_sb[64:128, hp, 0:64],
                                         bank[64:128, 64:128])
                            return e.tensor_add(kv_sb[:, hp, 64:65],
                                                kv_sb[:, hp, 64:65],
                                                bank[:, 128:129])
                        items.append((mmfn, [Cons("vector", evfn)]))
                p.pipeline(items, pb)

        p.dbg_dump("xnT", xnT[:], [128, DC, CN])
        p.dbg_dump("kv_sb", kv_sb[:], [128, 8, 65])
        with nc.Block() as b:
            @b.vector
            def _(e):
                e.memset(kv_bd[:], 0.0)
                e.memset(klhs[:], 0.0)
                for h in range(16):
                    hp, lo = h // 2, (h % 2) * 64
                    e.tensor_copy(kv_bd[lo:lo + 64, hp, lo:lo + 64],
                                  kv_sb[lo:lo + 64, hp, 0:64])
                for cc in range(8):
                    e.tensor_copy(klhs[0:64, cc, 0:1], kv_sb[0:64, cc, 64:65])
                    e.tensor_copy(klhs[64:128, cc, 1:2], kv_sb[64:128, cc, 64:65])

        with ExitStack() as st_p2:
            with nc.Block() as b:
                @b.gpsimd
                def _(e):
                    p.wq = load_weight_bf16(p, "l0_wq", W["wq_w"], D, D, ld, e, st_p2)
                    p.wg = load_weight_bf16(p, "l0_wg", W["wg_w"], D, D, ld, e, st_p2)
                    p.ow = load_weight_bf16(p, "l0_ow", W["out_w"], D, D, ld, e, st_p2)
                    p.wq_bc = load_bias_col_f32(p, "l0_wq_bc", W["wq_b"], D, ld, e, st_p2)
                    p.wg_bc = load_bias_col_f32(p, "l0_wg_bc", W["wg_b"], D, ld, e, st_p2)
                    p.ow_br = load_bias_row_bf16(p, "l0_ow_b", W["out_b"], D, ld, e, st_p2)
            qT = p.sbuf("l0_qT", [128, DC, CN], BF16, st_p2)
            qmin = p.sbuf("l0_qmin", [128, DC, CN], BF16, st_p2)
            gT_d = p.sbuf("l0_gT_d", [128, 2, 512], BF16, st_p2)
            zb_d = p.sbuf("l0_zb_d", [128, 2, 512], F32, st_p2)
            gsT = p.sbuf("l0_gsT", [128, DC, CN], BF16, st_p2)
            zT = p.sbuf("l0_zT", [2, DC, CN], BF16, st_p2)
            scrZ = p.sbuf("l0_scrZ", [2, 512], F32, st_p2)
            sc_f32 = p.sbuf("l0_sc", [128, 2, 512], F32, st_p2)
            zps = p.psum("l0_zps", [2, 512], F32, st_p2)
            zps2 = p.psum("l0_zps2", [2, 512], F32, st_p2)

            for c in range(CH):
                ln_phase(p, x_in, x_ch, xn_t, xnT, stats, CT, xd, ptr, c)
                items = []
                for dout in range(DC):
                    for nh in range(2):
                        def mmfn(e, bank, dout=dout, nh=nh, c=c):
                            if c == 0 and dout == 0 and nh == 0:
                                ld.wait(e)
                            last = None
                            for d in range(DC):
                                last = e.matmul(
                                    bank[:],
                                    p.wq[:, d, dout * 128:(dout + 1) * 128],
                                    xnT[:, d, nh * 512:(nh + 1) * 512],
                                    start=(d == 0), stop=(d == DC - 1))
                            return last

                        def evfn(e, bank, dout=dout, nh=nh):
                            sl = slice(nh * 512, (nh + 1) * 512)
                            e.tensor_scalar(qmin[:, dout, sl], bank[:],
                                            p.wq_bc[:, dout:dout + 1], 0.0,
                                            ALU.add, ALU.min)
                            return e.tensor_scalar(qT[:, dout, sl], bank[:],
                                                   p.wq_bc[:, dout:dout + 1],
                                                   0.0, ALU.add, ALU.max)
                        items.append((mmfn, [Cons("vector", evfn)]))
                p.pipeline(items, pb)
                with nc.Block() as b:
                    @b.scalar
                    def _(e):
                        for dout in range(DC):
                            e.activation(qmin[:, dout, :], qmin[:, dout, :],
                                         AF.Exp)
                with nc.Block() as b:
                    @b.vector
                    def _(e):
                        for dout in range(DC):
                            e.tensor_add(qT[:, dout, :], qT[:, dout, :],
                                         qmin[:, dout, :])
                items = []
                for nh in range(2):
                    for hp in range(DC):
                        def mmfn(e, bank, nh=nh, hp=hp):
                            return e.matmul(bank[0:2, 0:512], klhs[:, hp, :],
                                            qT[:, hp, nh * 512:(nh + 1) * 512],
                                            start=True, stop=True)

                        def evfn(e, bank, nh=nh, hp=hp):
                            p.seq(e, e.tensor_scalar(scrZ[:], bank[0:2, 0:512],
                                                     1e-6, None, ALU.add))
                            p.seq(e, e.reciprocal(scrZ[:], scrZ[:]))
                            return e.tensor_copy(
                                zT[0:2, hp, nh * 512:(nh + 1) * 512], scrZ[:])
                        items.append((mmfn, [Cons("vector", evfn)]))
                p.pipeline(items, [zps, zps2])
                if c == 0:
                    p.dbg_dump("qT0", qT[:], [128, DC, CN])
                    p.dbg_dump("zT0", zT[:], [2, DC, CN])
                items = []
                for dout in range(DC):
                    for nh in range(2):
                        def mmfn(e, bg, dout=dout, nh=nh):
                            bank_g, bank_z, bank_s = bg
                            for d in range(DC):
                                e.matmul(bank_g[:],
                                         p.wg[:, d, dout * 128:(dout + 1) * 128],
                                         xnT[:, d, nh * 512:(nh + 1) * 512],
                                         start=(d == 0), stop=(d == DC - 1))
                            e.matmul(bank_z[:], p.zsel[:],
                                     zT[0:2, dout, nh * 512:(nh + 1) * 512],
                                     start=True, stop=True)
                            return e.matmul(bank_s[:], kv_bd[:, dout, :],
                                            qT[:, dout, nh * 512:(nh + 1) * 512],
                                            start=True, stop=True)

                        def ev_g(e, bg, dout=dout, nh=nh):
                            return e.activation(gT_d[:, nh, 0:512], bg[0][:],
                                                AF.Sigmoid,
                                                bias=p.wg_bc[:, dout:dout + 1])

                        def ev_z(e, bg, nh=nh):
                            return e.tensor_copy(zb_d[:, nh, 0:512], bg[1][:])

                        def ev_s(e, bg, dout=dout, nh=nh):
                            p.seq(e, e.tensor_tensor(sc_f32[:, nh, :], bg[2][:],
                                                     zb_d[:, nh, 0:512],
                                                     ALU.mult))
                            return e.tensor_tensor(
                                gsT[:, dout, nh * 512:(nh + 1) * 512],
                                sc_f32[:, nh, :], gT_d[:, nh, 0:512], ALU.mult)
                        items.append((mmfn, [
                            Cons("scalar", ev_g),
                            Cons("vector", ev_z),
                            Cons("vector", ev_s, deps=[(0, 0)]),
                        ]))
                p.pipeline(items, [(pb[0], pb[1], pb[2])])

                if c == 0:
                    p.dbg_dump("gsT0", gsT[:], [128, DC, CN])
                items = []
                for t in range(CT):
                    for half in range(2):
                        def mmfn(e, bank, t=t, half=half):
                            for d in range(DC):
                                e.matmul(bank[:],
                                         gsT[:, d, t * 128:(t + 1) * 128],
                                         p.ow[:, d, half * 512:(half + 1) * 512],
                                         start=(d == 0), stop=False)
                            return e.matmul(
                                bank[:], p.ones_row_bf[0:1, 0:128],
                                p.ow_br[0:1, half * 512:(half + 1) * 512],
                                start=False, stop=True)

                        def evfn(e, bank, t=t, half=half):
                            sl = slice(half * 512, (half + 1) * 512)
                            return e.tensor_add(x_ch[:, t, sl], x_ch[:, t, sl],
                                                bank[:])
                        items.append((mmfn, [Cons("vector", evfn)]))
                p.pipeline(items, pb)
                with nc.Block() as b:
                    @b.sync
                    def _(e):
                        dst = x_out[c * CN:(c + 1) * CN, :].rearrange(
                            "(t p) f -> p t f", p=128)
                        st.inc(e.dma_start(dst, x_ch[:, 0:CT, :]), 16)
    with nc.Block() as b:
        @b.sync
        def _(e):
            st.wait(e)




def _build_l0a_program():
    stack = ExitStack()
    nc = bass.Bass("TRN2", target_bir_lowering=False)
    x_in = nc.dram_tensor("x_in", [N_TOK, D], F32, kind="ExternalInput").ap()
    names = ["wg_w", "wg_b", "wq_w", "wq_b", "wk_w", "wk_b", "wv_w", "wv_b",
             "out_w", "out_b"]
    W = {}
    for n in names:
        shape = [D, D] if n.endswith("_w") else [D]
        W[n] = nc.dram_tensor(n, shape, F32, kind="ExternalInput").ap()
    x_out = nc.dram_tensor("x_out", [N_TOK, D], F32, kind="ExternalOutput").ap()
    p = Prog(nc, stack)
    setup_common(p)
    setup_zsel(p)
    build_l0_attn(p, x_in, x_out, W)
    stack.close()
    nc.finalize()
    return nc


_PROGRAMS = {}


def _get_prog(name, builder):
    if name not in _PROGRAMS:
        _PROGRAMS[name] = builder()
    return _PROGRAMS[name]


def kernel_multi(**inputs):
    x = np.ascontiguousarray(inputs["L0_attn_x"], dtype=np.float32)
    cores = list(range(N_CORES))

    nc = _get_prog("l0a", _build_l0a_program)
    wn = ["wg_w", "wg_b", "wq_w", "wq_b", "wk_w", "wk_b", "wv_w", "wv_b",
          "out_w", "out_b"]
    in_maps = []
    for c in cores:
        m = {"x_in": x[c]}
        for n in wn:
            m[n] = np.ascontiguousarray(inputs["L0_attn_" + n], np.float32)
        in_maps.append(m)
    res = run_bass_kernel_spmd(nc, in_maps, core_ids=cores)
    x1 = [res.results[c]["x_out"] for c in cores]

    nc = _get_prog("l0f", _build_l0f_program)
    in_maps = []
    for c in cores:
        m = {"x_in": x1[c],
             "w_gate": np.ascontiguousarray(inputs["L0_ffn_w_gate"], np.float32),
             "w_up": np.ascontiguousarray(inputs["L0_ffn_w_up"], np.float32),
             "w_down": np.ascontiguousarray(inputs["L0_ffn_w_down"], np.float32)}
        in_maps.append(m)
    res = run_bass_kernel_spmd(nc, in_maps, core_ids=cores)
    x2 = [res.results[c]["x_out"] for c in cores]

    nc = _get_prog("l1a", _build_l1a_program)
    wn = ["Wq", "Wkv", "Wk", "Wv", "Wo", "q_g", "k_g"]
    in_maps = []
    for c in cores:
        m = {"x_in": x2[c]}
        for n in wn:
            m[n] = np.ascontiguousarray(inputs["L1_attn_" + n], np.float32)
        in_maps.append(m)
    res = run_bass_kernel_spmd(nc, in_maps, core_ids=cores)
    x3 = [res.results[c]["x_out"] for c in cores]

    nc = _get_prog("l1f", _build_l1f_program)
    wn = ["Wq", "bq", "Wk", "bk", "Wv", "bv", "We", "be", "Wsw", "bsw",
          "gamma", "beta"]
    in_maps = []
    for c in cores:
        m = {"x_in": x3[c]}
        for n in wn:
            m[n] = np.ascontiguousarray(inputs["L1_ffn_" + n], np.float32)
        rk = np.zeros((128, 1), np.float32)
        for r in range(c):
            rk[16 * r, 0] = 1.0
        m["rank_mask"] = rk
        in_maps.append(m)
    res = run_bass_kernel_spmd(nc, in_maps, core_ids=cores)
    out = np.stack([np.asarray(res.results[c]["x_out"], dtype=np.float32)
                    for c in cores])
    return out


def emit_quant(p, e, src, dst, W, scr, rstd_col=None):
    nb = W // 16
    B2, B3, B4 = scr["B2"], scr["B3"], scr["B4"]
    A, S, R = scr["A"], scr["S"], scr["R"]
    sq = lambda i: p.seq(e, i)
    I = lambda ap: ap.bitcast(I32)
    v3 = lambda t: t[:, 0:W].rearrange("p (b k) -> p b k", k=16)
    npart = src.shape[0]
    sq(e.tensor_single_scalar(I(B2[:, 0:W]), I(src), 0x7FFFFFFF,
                              ALU.bitwise_and))
    sq(e.tensor_reduce(A[:, 0:nb], v3(B2), AX.X, ALU.max))
    if rstd_col is not None:
        sq(e.tensor_scalar(S[:, 0:nb], A[:, 0:nb], rstd_col, 1.0 / 6.0,
                           ALU.mult, ALU.mult))
    else:
        sq(e.tensor_scalar(S[:, 0:nb], A[:, 0:nb], 1.0 / 6.0, None, ALU.mult))
    sq(e.tensor_single_scalar(S[:, 0:nb], S[:, 0:nb], 1e-12, ALU.max))
    sq(e.reciprocal(R[:, 0:nb], S[:, 0:nb]))
    if rstd_col is not None:
        sq(e.tensor_scalar(R[:, 0:nb], R[:, 0:nb], rstd_col, None, ALU.mult))
    Rb = R[0:npart, 0:nb].unsqueeze(2).broadcast_to([npart, nb, 16])
    Sb = S[0:npart, 0:nb].unsqueeze(2).broadcast_to([npart, nb, 16])
    sq(e.tensor_tensor(v3(B3), v3(B2), Rb, ALU.mult))
    sq(e.tensor_single_scalar(B4[:, 0:W], B3[:, 0:W], 0.25, ALU.is_gt))
    for t in (0.75, 1.25, 1.75):
        sq(e.scalar_tensor_tensor(B4[:, 0:W], B3[:, 0:W], t, B4[:, 0:W],
                                  ALU.is_gt, ALU.add))
    sq(e.tensor_single_scalar(B2[:, 0:W], B3[:, 0:W], 2.5, ALU.is_gt))
    sq(e.scalar_tensor_tensor(B2[:, 0:W], B3[:, 0:W], 3.5, B2[:, 0:W],
                              ALU.is_gt, ALU.add))
    sq(e.scalar_tensor_tensor(B4[:, 0:W], B4[:, 0:W], 0.5, B2[:, 0:W],
                              ALU.mult, ALU.add))
    sq(e.tensor_scalar(B2[:, 0:W], B3[:, 0:W], 5.0, 2.0, ALU.is_gt, ALU.mult))
    sq(e.tensor_add(B4[:, 0:W], B4[:, 0:W], B2[:, 0:W]))
    sq(e.tensor_tensor(v3(B4), v3(B4), Sb, ALU.mult))
    sq(e.tensor_single_scalar(I(B2[:, 0:W]), I(src), -2147483648,
                              ALU.bitwise_and))
    sq(e.tensor_tensor(I(B4[:, 0:W]), I(B4[:, 0:W]), I(B2[:, 0:W]),
                       ALU.bitwise_or))
    return e.tensor_copy(dst, B4[:, 0:W])


def emit_quant_srv(p, e, src, W, Asl, Ssl, Rsl, rstd_col=None):
    nb = W // 16
    B2 = p.f2_B2s
    sq = lambda i: p.seq(e, i)
    I = lambda ap: ap.bitcast(I32)
    sq(e.tensor_single_scalar(I(B2[:, 0:W]), I(src), 0x7FFFFFFF,
                              ALU.bitwise_and))
    v3 = B2[:, 0:W].rearrange("p (b k) -> p b k", k=16)
    sq(e.tensor_reduce(Asl, v3, AX.X, ALU.max))
    if rstd_col is not None:
        sq(e.tensor_scalar(Ssl, Asl, rstd_col, 1.0 / 6.0, ALU.mult, ALU.mult))
    else:
        sq(e.tensor_scalar(Ssl, Asl, 1.0 / 6.0, None, ALU.mult))
    sq(e.tensor_single_scalar(Ssl, Ssl, 1e-12, ALU.max))
    return e.reciprocal(Rsl, Ssl)


def emit_quant_pool(p, e, src, dst, W, Asl, Ssl, Rsl):
    nb = W // 16
    B2, B3, B4, S1 = p.f2_B2p, p.f2_B3p, p.f2_B4p, p.f2_Ps1
    sqp = p.sem("chainP")

    def sq(i):
        sqp.inc(i)
        sqp.wait(e)
        return i
    v3 = lambda t: t[:, 0:W].rearrange("p (b k) -> p b k", k=16)
    sq(e.tensor_scalar(B3[:, 0:W], src, 0.0, 2.0, ALU.is_ge, ALU.mult))
    sq(e.tensor_scalar(S1[:, 0:W], B3[:, 0:W], 1.0, None, ALU.subtract))
    sq(e.tensor_tensor(B2[:, 0:W], src, S1[:, 0:W], ALU.mult))
    Rb = Rsl.unsqueeze(2).broadcast_to([128, nb, 16])
    Sb = Ssl.unsqueeze(2).broadcast_to([128, nb, 16])
    sq(e.tensor_tensor(v3(B3), v3(B2), Rb, ALU.mult))
    sq(e.tensor_single_scalar(B4[:, 0:W], B3[:, 0:W], 0.25, ALU.is_gt))
    for t in (0.75, 1.25, 1.75):
        sq(e.tensor_single_scalar(B2[:, 0:W], B3[:, 0:W], t, ALU.is_gt))
        sq(e.tensor_add(B4[:, 0:W], B4[:, 0:W], B2[:, 0:W]))
    sq(e.tensor_scalar(B4[:, 0:W], B4[:, 0:W], 0.5, None, ALU.mult))
    for t in (2.5, 3.5):
        sq(e.tensor_single_scalar(B2[:, 0:W], B3[:, 0:W], t, ALU.is_gt))
        sq(e.tensor_add(B4[:, 0:W], B4[:, 0:W], B2[:, 0:W]))
    sq(e.tensor_scalar(B2[:, 0:W], B3[:, 0:W], 5.0, 2.0, ALU.is_gt, ALU.mult))
    sq(e.tensor_add(B4[:, 0:W], B4[:, 0:W], B2[:, 0:W]))
    sq(e.tensor_tensor(v3(B4), v3(B4), Sb, ALU.mult))
    return e.tensor_tensor(dst, B4[:, 0:W], S1[:, 0:W], ALU.mult)


def build_l0_ffn(p: Prog, x_in, x_out, W):
    nc = p.nc
    I_DIM = 2816
    IC = I_DIM // 128
    CH = 8
    CT = NT // CH
    CN = CT * 128

    with ExitStack() as st_all:
        QW = 512
        scr = {
            "B2": p.sbuf("f2_B2", [128, QW], F32, st_all),
            "B3": p.sbuf("f2_B3", [128, QW], F32, st_all),
            "B4": p.sbuf("f2_B4", [128, QW], F32, st_all),
            "A": p.sbuf("f2_A", [128, QW // 16], F32, st_all),
            "S": p.sbuf("f2_S", [128, QW // 16], F32, st_all),
            "R": p.sbuf("f2_R", [128, QW // 16], F32, st_all),
        }

        wgT = p.sbuf("f2_wgT", [128, DC, I_DIM], BF16, st_all)
        wuT = p.sbuf("f2_wuT", [128, DC, I_DIM], BF16, st_all)
        wdT = p.sbuf("f2_wdT", [128, IC, D], BF16, st_all)

        st_w = ExitStack()
        slab = [p.sbuf(f"f2_slab{i}", [128, I_DIM], F32, st_w) for i in range(3)]
        qsl = [p.sbuf(f"f2_qsl{i}", [128, I_DIM], BF16, st_w) for i in range(2)]
        ptrW = [p.psum(f"f2_ptrw{i}", [128, 128], BF16, st_w) for i in range(2)]

        slabs = []
        for r in range(I_DIM // 128):
            slabs.append((W["w_gate"][r * 128:(r + 1) * 128, :], D, wgT, r))
        for r in range(I_DIM // 128):
            slabs.append((W["w_up"][r * 128:(r + 1) * 128, :], D, wuT, r))
        for r in range(D // 128):
            slabs.append((W["w_down"][r * 128:(r + 1) * 128, :], I_DIM, wdT, r))
        gk = [0]
        sched = []
        for (dram, cols, dstT, r) in slabs:
            ch = []
            for off in range(0, cols, QW):
                wlen = min(QW, cols - off)
                ch.append((off, wlen, "V"))
                gk[0] += 1
            sched.append(ch)
        nsl = len(slabs)
        cumP = [0] * (nsl + 1)
        cumTP = [0] * (nsl + 1)
        for s in range(nsl):
            cumP[s + 1] = cumP[s] + sum(1 for c in sched[s] if c[2] == "P")
            cumTP[s + 1] = cumTP[s] + slabs[s][1] // 128

        wd = p.sem("f2_wd")
        wvq = p.sem("f2_wvq")
        wtp = p.sem("f2_wtp")
        wse = p.sem("f2_wse")
        b_wd, b_wvq = wd.n, wvq.n
        b_wtp, b_wse = wtp.n, wse.n

        with nc.Block() as b:
            @b.sync
            def _(e):
                for s in range(nsl):
                    if s >= 3:
                        e.wait_ge(wvq.h, b_wvq + s - 2)
                    dram, cols, dstT, r = slabs[s]
                    wd.inc(e.dma_start(slab[s % 3][:, 0:cols], dram), 16)

            @b.vector
            def _(e):
                for s in range(nsl):
                    dram, cols, dstT, r = slabs[s]
                    e.wait_ge(wd.h, b_wd + 16 * (s + 1))
                    if s >= 2:
                        e.wait_ge(wtp.h, b_wtp + cumTP[s - 1])
                    last = None
                    for (off, wlen, who) in sched[s]:
                        last = emit_quant(p, e, slab[s % 3][:, off:off + wlen],
                                          qsl[s % 2][:, off:off + wlen],
                                          wlen, scr)
                    wvq.inc(last)

            @b.tensor
            def _(e):
                tau = 0
                for s in range(nsl):
                    dram, cols, dstT, r = slabs[s]
                    e.wait_ge(wvq.h, b_wvq + s + 1)
                    for k in range(cols // 128):
                        if tau >= 2:
                            e.wait_ge(wse.h, b_wse + tau - 1)
                        wtp.inc(e.transpose(
                            ptrW[tau % 2][:],
                            qsl[s % 2][:, k * 128:(k + 1) * 128],
                            p.ident_bf[:]))
                        tau += 1

            @b.scalar
            def _(e):
                tau = 0
                for s in range(nsl):
                    dram, cols, dstT, r = slabs[s]
                    for k in range(cols // 128):
                        e.wait_ge(wtp.h, b_wtp + tau + 1)
                        wse.inc(e.activation(
                            dstT[:, k, r * 128:(r + 1) * 128],
                            ptrW[tau % 2][:], AF.Copy))
                        tau += 1
        st_w.close()

        x_ch = p.sbuf("f2_x_ch", [128, CT, D], F32, st_all)
        s1 = p.sbuf("f2_s1", [128, CT], F32, st_all)
        ssq = p.sbuf("f2_ssq", [128, CT], F32, st_all)
        mean = p.sbuf("f2_mean", [128, CT], F32, st_all)
        rstd = p.sbuf("f2_rstd", [128, CT], F32, st_all)
        xm = p.sbuf("f2_xm", [128, D], F32, st_all)
        scr0 = xm
        xq = p.sbuf("f2_xq", [128, 2, D], BF16, st_all)
        xqT = p.sbuf("f2_xqT", [128, 2, DC, 128], BF16, st_all)
        sg = p.sbuf("f2_sg", [128, 2, QW], F32, st_all)
        hbuf = p.sbuf("f2_hb", [128, 6, QW], F32, st_all)
        hq = p.sbuf("f2_hq", [128, I_DIM], BF16, st_all)
        hqT = p.sbuf("f2_hqT", [128, 2, IC, 128], BF16, st_all)

        pb = [p.psum(f"f2_pb{i}", [128, 512], F32, st_all) for i in range(4)]
        pd = [p.psum(f"f2_pd{i}", [128, 512], F32, st_all) for i in range(2)]
        ptr = [p.psum(f"f2_ptr{i}", [128, 128], BF16, st_all) for i in range(2)]

        IW = [(0, 512), (512, 512), (1024, 512), (1536, 512), (2048, 512),
              (2560, 256)]
        GROUPS = [(3, 4), (2, 5), (0, 1)]
        ORD = [3, 4, 2, 5, 0, 1]
        POS = {ii: i + 1 for i, ii in enumerate(ORD)}
        GRP = {3: 1, 4: 1, 2: 2, 5: 2, 0: 3, 1: 3}
        V_II = (3, 4, 2, 5, 0, 1)
        VQIDX = {3: 1, 4: 2, 2: 3, 5: 4, 0: 5, 1: 6}
        VSCHED = [("X", None), ("m", 3), ("m", 4), ("q", 3), ("m", 2),
                  ("m", 5), ("q", 4), ("m", 0), ("m", 1), ("q", 2),
                  ("q", 5), ("q", 0), ("q", 1), ("R", None)]
        CUMTP = {}
        _c = 0
        for _ii in ORD:
            _c += IW[_ii][1] // 128
            CUMTP[_ii] = _c
        UPB = {3: 2, 4: 3, 2: 2, 5: 3, 0: 2, 1: 3}
        GATB = {3: 0, 4: 1, 2: 0, 5: 1, 0: 0, 1: 1}
        HTOFF = {}
        off = 8
        for ii in ORD:
            HTOFF[ii] = off
            off += IW[ii][1] // 128
        TPT = off

        xd = p.sem("f2_xd")
        stq = p.sem("f2_st")
        sA = p.sem("f2_sA")
        vM = p.sem("f2_vM")
        sQ = p.sem("f2_sQ")
        vR = p.sem("f2_vR")
        vx = p.sem("f2_vx")
        tp = p.sem("f2_tp")
        se = p.sem("f2_se")
        mmG = p.sem("f2_mmG")
        sS = p.sem("f2_sS")
        vH = p.sem("f2_vH")
        vq = p.sem("f2_vq")
        mmD = p.sem("f2_mmD")
        vRes = p.sem("f2_vRes")
        b_xd, b_st, b_sA, b_vM, b_sQ, b_vR = (xd.n, stq.n, sA.n, vM.n, sQ.n,
                                              vR.n)
        b_vx, b_tp, b_se, b_mmG, b_sS, b_vH = (vx.n, tp.n, se.n, mmG.n, sS.n,
                                               vH.n)
        b_vq, b_mmD, b_vRes = vq.n, mmD.n, vRes.n

        T = CH * CT
        TAU_x, TAU_h, TP_x, TP_h = {}, {}, {}, {}
        TAU_x[0] = 0
        cnt = 8
        TP_x[0] = 8
        for t_ in range(T):
            if t_ > 0 and t_ % CT == 0:
                TAU_h[t_ - 1] = cnt
                cnt += 22
                TP_h[t_ - 1] = cnt
                TAU_x[t_] = cnt
                cnt += 8
                TP_x[t_] = cnt
            if t_ % CT != 0 and t_ >= 1:
                TAU_h[t_ - 1] = cnt
                cnt += 22
                TP_h[t_ - 1] = cnt
            if t_ + 1 < T and (t_ + 1) % CT != 0:
                TAU_x[t_ + 1] = cnt
                cnt += 8
                TP_x[t_ + 1] = cnt
        TAU_h[T - 1] = cnt
        cnt += 22
        TP_h[T - 1] = cnt

        with nc.Block() as b:
            @b.sync
            def _(e):
                for c in range(CH):
                    if c > 0:
                        e.wait_ge(stq.h, b_st + 16 * c)
                    src = x_in[c * CN:(c + 1) * CN, :].rearrange(
                        "(t p) f -> p t f", p=128)
                    xd.inc(e.dma_start(x_ch[:, 0:CT, :], src), 16)
                    e.wait_ge(vRes.h, b_vRes + 4 * (c + 1))
                    dst = x_out[c * CN:(c + 1) * CN, :].rearrange(
                        "(t p) f -> p t f", p=128)
                    stq.inc(e.dma_start(dst, x_ch[:, 0:CT, :]), 16)
                e.wait_ge(stq.h, b_st + 16 * CH)

            @b.scalar
            def _(e):
                def stats(c):
                    e.wait_ge(xd.h, b_xd + 16 * (c + 1))
                    last = None
                    for tl in range(CT):
                        e.activation(scr0[:, 0:D], x_ch[:, tl, :], AF.Square,
                                     accum_out=ssq[:, tl:tl + 1])
                        last = e.activation(scr0[:, 0:D], x_ch[:, tl, :],
                                            AF.Copy, accum_out=s1[:, tl:tl + 1])
                    sA.inc(last)
                    e.wait_ge(vM.h, b_vM + c + 1)
                    sQ.inc(e.activation(rstd[:, 0:CT], rstd[:, 0:CT], AF.Sqrt,
                                        bias=p.cc[1e-5][:]))

                def evac_x(t2):
                    for k in range(8):
                        tau = TAU_x[t2] + k
                        e.wait_ge(tp.h, b_tp + tau + 1)
                        se.inc(e.activation(xqT[:, t2 % 2, k, :],
                                            ptr[tau % 2][:], AF.Copy))

                def evac_h(td):
                    idx = 0
                    for ii in ORD:
                        for k in range(IW[ii][1] // 128):
                            tau = TAU_h[td] + idx
                            e.wait_ge(tp.h, b_tp + tau + 1)
                            if idx == 0 and td >= 2:
                                e.wait_ge(mmD.h, b_mmD + td - 1)
                            kappa = IW[ii][0] // 128 + k
                            se.inc(e.activation(hqT[:, td % 2, kappa, :],
                                                ptr[tau % 2][:], AF.Copy))
                            idx += 1

                stats(0)
                evac_x(0)
                for t in range(T):
                    if t > 0 and t % CT == 0:
                        evac_h(t - 1)
                        stats(t // CT)
                        evac_x(t)
                    for pp, ii in enumerate(ORD):
                        ppos = pp + 1
                        wlen = IW[ii][1]
                        e.wait_ge(mmG.h, b_mmG + 3 * t + GRP[ii])
                        if 6 * t + ppos - 2 > 0:
                            e.wait_ge(vH.h, b_vH + 6 * t + ppos - 2)
                        sS.inc(e.activation(sg[:, ppos % 2, 0:wlen],
                                            pb[GATB[ii]][:, 0:wlen], AF.Silu))
                        if pp == 1 and t % CT != 0 and t >= 1:
                            evac_h(t - 1)
                    if t + 1 < T and (t + 1) % CT != 0:
                        evac_x(t + 1)
                evac_h(T - 1)

            @b.vector
            def _(e):
                for c in range(CH):
                    e.wait_ge(sA.h, b_sA + c + 1)
                    p.seq(e, e.tensor_scalar(mean[:, 0:CT], s1[:, 0:CT],
                                             1.0 / D, None, ALU.mult))
                    p.seq(e, e.tensor_tensor(rstd[:, 0:CT], mean[:, 0:CT],
                                             mean[:, 0:CT], ALU.mult))
                    vM.inc(e.scalar_tensor_tensor(
                        rstd[:, 0:CT], ssq[:, 0:CT], 1.0 / D, rstd[:, 0:CT],
                        ALU.mult, ALU.subtract))
                    e.wait_ge(sQ.h, b_sQ + c + 1)
                    vR.inc(e.reciprocal(rstd[:, 0:CT], rstd[:, 0:CT]))
                    e.wait_ge(vR.h, b_vR + c + 1)
                    def do_xq(e, tl2):
                        t2 = c * CT + tl2
                        rc = rstd[:, tl2:tl2 + 1]
                        p.seq(e, e.tensor_scalar(xm[:, 0:D], x_ch[:, tl2, :],
                                                 mean[:, tl2:tl2 + 1], None,
                                                 ALU.subtract))
                        if t2 >= 2:
                            e.wait_ge(tp.h, b_tp + TP_x[t2 - 2])
                        emit_quant(p, e, xm[:, 0:512], xq[:, t2 % 2, 0:512],
                                   512, scr, rstd_col=rc)
                        vx.inc(emit_quant(p, e, xm[:, 512:1024],
                                          xq[:, t2 % 2, 512:1024], 512, scr,
                                          rstd_col=rc))
                    do_xq(e, 0)
                    for tl in range(CT):
                        t = c * CT + tl
                        for kind, ii in VSCHED:
                            if kind == "m":
                                ppos = POS[ii]
                                wlen = IW[ii][1]
                                e.wait_ge(mmG.h, b_mmG + 3 * t + GRP[ii])
                                e.wait_ge(sS.h, b_sS + 6 * t + ppos)
                                vH.inc(e.tensor_tensor(
                                    hbuf[:, ii, 0:wlen],
                                    sg[:, ppos % 2, 0:wlen],
                                    pb[UPB[ii]][:, 0:wlen], ALU.mult))
                                e.wait_ge(vH.h, b_vH + 6 * t + ppos)
                            elif kind == "q":
                                wlen = IW[ii][1]
                                if t >= 1:
                                    e.wait_ge(tp.h, b_tp + TAU_h[t - 1]
                                              + CUMTP[ii])
                                vq.inc(emit_quant(
                                    p, e, hbuf[:, ii, 0:wlen],
                                    hq[:, IW[ii][0]:IW[ii][0] + wlen],
                                    wlen, scr))
                            elif kind == "X":
                                if tl < CT - 1:
                                    do_xq(e, tl + 1)
                            else:
                                if tl >= 1:
                                    e.wait_ge(mmD.h, b_mmD + t)
                                    tlp = (t - 1) % CT
                                    p.seq(e, e.tensor_add(x_ch[:, tlp, 0:512],
                                                          x_ch[:, tlp, 0:512],
                                                          pd[0][:]))
                                    vRes.inc(e.tensor_add(
                                        x_ch[:, tlp, 512:1024],
                                        x_ch[:, tlp, 512:1024], pd[1][:]))
                    t = c * CT + CT - 1
                    e.wait_ge(mmD.h, b_mmD + t + 1)
                    p.seq(e, e.tensor_add(x_ch[:, CT - 1, 0:512],
                                          x_ch[:, CT - 1, 0:512], pd[0][:]))
                    vRes.inc(e.tensor_add(
                        x_ch[:, CT - 1, 512:1024], x_ch[:, CT - 1, 512:1024],
                        pd[1][:]))

            @b.tensor
            def _(e):
                def xtp(t2):
                    for k in range(8):
                        if k == 0:
                            e.wait_ge(vx.h, b_vx + t2 + 1)
                        tau = TAU_x[t2] + k
                        if tau >= 2:
                            e.wait_ge(se.h, b_se + tau - 1)
                        tp.inc(e.transpose(
                            ptr[tau % 2][:],
                            xq[:, t2 % 2, k * 128:(k + 1) * 128],
                            p.ident_bf[:]))

                def htp(td):
                    idx = 0
                    for ii in ORD:
                        e.wait_ge(vq.h, b_vq + 6 * td + VQIDX[ii])
                        for k in range(IW[ii][1] // 128):
                            tau = TAU_h[td] + idx
                            if tau >= 2:
                                e.wait_ge(se.h, b_se + tau - 1)
                            tp.inc(e.transpose(
                                ptr[tau % 2][:],
                                hq[:, IW[ii][0] + k * 128:
                                    IW[ii][0] + (k + 1) * 128],
                                p.ident_bf[:]))
                            idx += 1

                def down(td):
                    e.wait_ge(se.h, b_se + TP_h[td])
                    if td >= 1:
                        e.wait_ge(vRes.h, b_vRes + td)
                    last = None
                    for kc in range(IC):
                        st_, sp_ = (kc == 0), (kc == IC - 1)
                        e.matmul(pd[0][:], hqT[:, td % 2, kc, :],
                                 wdT[:, kc, 0:512], start=st_, stop=sp_)
                        last = e.matmul(pd[1][:], hqT[:, td % 2, kc, :],
                                        wdT[:, kc, 512:1024], start=st_,
                                        stop=sp_)
                    mmD.inc(last)

                xtp(0)
                for t in range(T):
                    if t > 0 and t % CT == 0:
                        htp(t - 1)
                        down(t - 1)
                        xtp(t)
                    for gi, (iiA, iiB) in enumerate(GROUPS):
                        j = 3 * t + gi
                        if gi == 0:
                            e.wait_ge(se.h, b_se + TP_x[t])
                        if j > 0:
                            e.wait_ge(sS.h, b_sS + 2 * j)
                            e.wait_ge(vH.h, b_vH + 2 * j)
                        oA, wA = IW[iiA]
                        oB, wB = IW[iiB]
                        last = None
                        for d in range(DC):
                            st_, sp_ = (d == 0), (d == DC - 1)
                            e.matmul(pb[0][:, 0:wA], xqT[:, t % 2, d, :],
                                     wgT[:, d, oA:oA + wA], start=st_,
                                     stop=sp_)
                            e.matmul(pb[1][:, 0:wB], xqT[:, t % 2, d, :],
                                     wgT[:, d, oB:oB + wB], start=st_,
                                     stop=sp_)
                            e.matmul(pb[2][:, 0:wA], xqT[:, t % 2, d, :],
                                     wuT[:, d, oA:oA + wA], start=st_,
                                     stop=sp_)
                            last = e.matmul(pb[3][:, 0:wB],
                                            xqT[:, t % 2, d, :],
                                            wuT[:, d, oB:oB + wB], start=st_,
                                            stop=sp_)
                        mmG.inc(last)
                        if gi == 0 and t % CT != 0 and t >= 1:
                            htp(t - 1)
                    if t % CT != 0 and t >= 1:
                        down(t - 1)
                    if t + 1 < T and (t + 1) % CT != 0:
                        xtp(t + 1)
                htp(T - 1)
                down(T - 1)


def _build_l0f_program():
    stack = ExitStack()
    nc = bass.Bass("TRN2", target_bir_lowering=False)
    x_in = nc.dram_tensor("x_in", [N_TOK, D], F32, kind="ExternalInput").ap()
    W = {
        "w_gate": nc.dram_tensor("w_gate", [2816, 1024], F32, kind="ExternalInput").ap(),
        "w_up": nc.dram_tensor("w_up", [2816, 1024], F32, kind="ExternalInput").ap(),
        "w_down": nc.dram_tensor("w_down", [1024, 2816], F32, kind="ExternalInput").ap(),
    }
    x_out = nc.dram_tensor("x_out", [N_TOK, D], F32, kind="ExternalOutput").ap()
    p = Prog(nc, stack)
    setup_common(p)
    build_l0_ffn(p, x_in, x_out, W)
    stack.close()
    nc.finalize()
    return nc




def build_l1_attn(p: Prog, x_in, x_out, W):
    nc = p.nc
    H, HD, LAT, ROPE = 16, 64, 128, 32
    CH = 8
    CT = NT // CH
    CN = CT * 128

    ld = p.sem("l1a_ld")
    xd = p.sem("l1a_xd")
    st = p.sem("l1a_st")

    with ExitStack() as sa:
        with nc.Block() as b:
            @b.gpsimd
            def _(e):
                p.wkv = load_weight_bf16(p, "l1_wkv", W["Wkv"], D, LAT, ld, e, sa)
                p.wk1 = p.sbuf("l1_wk", [128, 512], BF16, sa)
                ld.inc(e.dma_start(p.wk1[:], W["Wk"]), 16)
                p.wv1 = p.sbuf("l1_wv", [128, 512], BF16, sa)
                ld.inc(e.dma_start(p.wv1[:], W["Wv"]), 16)
                p.woT = p.sbuf("l1_woT", [128, 6, D], BF16, sa)
                p.qg_col = p.sbuf("l1_qg", [128, 1], F32, sa)
                ld.inc(e.dma_start(p.qg_col[0:64, 0:1], W["q_g"][:, None]), 16)
                ld.inc(e.dma_start(p.qg_col[64:128, 0:1], W["q_g"][:, None]), 16)
                p.kg_col = p.sbuf("l1_kg", [128, 1], F32, sa)
                for m3 in range(3):
                    ld.inc(e.dma_start(p.kg_col[32 * m3:32 * m3 + 32, 0:1],
                                       W["k_g"][:, None]), 16)
                e.memset(p.woT[:], 0.0)
                for cc in range(6):
                    for m in range(3):
                        h = 3 * cc + m
                        if h >= H:
                            break
                        ld.inc(e.dma_start(
                            p.woT[32 * m:32 * m + 32, cc, :],
                            W["Wo"][h * 64:h * 64 + 32, :]), 16)
        with nc.Block() as b:
            @b.vector
            def _(e):
                ld.wait(e)
                e.tensor_scalar(p.kg_col[:], p.kg_col[:], 0.125, None, ALU.mult)

        iop = p.sbuf("l1_iop", [128, 1], F32, sa)
        pairsel = p.sbuf("l1_pairsel", [128, 2], BF16, sa)
        trisel = p.sbuf("l1_trisel", [96, 3], BF16, sa)
        triselT = p.sbuf("l1_triselT", [3, 96], BF16, sa)
        tri_i = p.sbuf("l1_tri_i", [3, 96], F32, sa)
        cmask = p.sbuf("l1_cmask", [128, 4, CN], BF16, sa)
        ones33 = p.sbuf("l1_ones33", [33, 32], BF16, sa)
        sc_const = ExitStack()
        cmask_i = p.sbuf("l1_cmask_i", [128, 4, CN], F32, sc_const)
        scrT = p.sbuf("l1_scrT", [96, 3], F32, sc_const)
        with nc.Block() as b:
            @b.gpsimd
            def _(e):
                e.iota(iop[:], [[1, 1]], base=0, channel_multiplier=1,
                       allow_small_or_imprecise_dtypes=True)
                e.iota(tri_i[:], [[1, 96]], base=0, channel_multiplier=-32,
                       allow_small_or_imprecise_dtypes=True)
                e.iota(cmask_i[:], [[-128, 4], [1, CN]], base=0,
                       channel_multiplier=-1,
                       allow_small_or_imprecise_dtypes=True)
                e.memset(ones33[:], 1.0)
        with nc.Block() as b:
            @b.vector
            def _(e):
                e.tensor_single_scalar(pairsel[:, 0:1], iop[:], 64.0, ALU.is_lt)
                e.tensor_single_scalar(pairsel[:, 1:2], iop[:], 63.0, ALU.is_ge)
                for g in range(3):
                    p.seq(e, e.tensor_scalar(scrT[0:96, g:g + 1], iop[0:96, :],
                                             32.0 * g - 1.0, 32.0 * g + 32.0,
                                             ALU.is_gt, ALU.min))
                    p.seq(e, e.tensor_single_scalar(trisel[0:96, g:g + 1],
                                                    iop[0:96, :],
                                                    32.0 * g + 32.0, ALU.is_lt))
                    p.seq(e, e.tensor_tensor(trisel[0:96, g:g + 1],
                                             trisel[0:96, g:g + 1],
                                             scrT[0:96, g:g + 1], ALU.mult))
                p.seq(e, e.tensor_scalar(triselT[:], tri_i[:], 0.0, 64.0,
                                         ALU.is_ge, ALU.min))
                p.seq(e, e.scalar_tensor_tensor(triselT[:], tri_i[:], 32.0,
                                                triselT[:], ALU.is_lt, ALU.mult))
                e.tensor_single_scalar(cmask[:], cmask_i[:], 0.0, ALU.is_ge)
        sc_const.close()

        x_ch = p.sbuf("l1_x_ch", [128, CT, D], F32, sa)
        xn_t = p.sbuf("l1_xn_t", [128, CT, D], BF16, sa)
        xnT = p.sbuf("l1_xnT", [128, DC, CN], BF16, sa)
        s1 = p.sbuf("l1_s1", [128, CT], F32, sa)
        ssq = p.sbuf("l1_ssq", [128, CT], F32, sa)
        mean = p.sbuf("l1_mean", [128, CT], F32, sa)
        rstd = p.sbuf("l1_rstd", [128, CT], F32, sa)
        scr = p.sbuf("l1_scr", [128, CN], F32, sa)
        stats = (s1, ssq, mean, rstd, scr)
        kvT = p.sbuf("l1_kvT", [128, CN], BF16, sa)
        raw = p.sbuf("l1_raw", [128, DC, CN], BF16, sa)
        sqb = p.sbuf("l1_sqb", [128, DC, CN], BF16, sa)
        rrmsK = p.sbuf("l1_rrmsK", [3, 6, CN], BF16, sa)
        rrmsQ = p.sbuf("l1_rrmsQ", [2, DC, CN], BF16, sa)
        ktT = p.sbuf("l1_ktT", [128, 6, N_TOK], BF16, sa)
        vtok = p.sbuf("l1_vtok", [128, NT, H, 33], BF16, sa)
        qtmp = p.sbuf("l1_qtmp", [128, CN], BF16, sa)
        wcol_a = p.sbuf("l1_wcol", [128, 2, DC, 128], BF16, sa)
        wda = p.sem("l1a_wd")
        qrope = p.sbuf("l1_qrope", [128, 6, CN], BF16, sa)
        Ptile = p.sbuf("l1_Pt", [128, 2, CN], BF16, sa)
        oT = p.sbuf("l1_oT", [128, 6, CN], BF16, sa)
        Drb = p.sbuf("l1_Drb", [33, 6, CN], BF16, sa)

        with nc.Block() as b:
            @b.gpsimd
            def _(e):
                e.memset(vtok[:], 0.0)
                e.memset(oT[:], 0.0)

        with nc.Block() as b:
            @b.vector
            def _(e):
                e.memset(vtok[:, :, :, 32:33], 1.0)

        for c in range(CH):
            with ExitStack() as sp:
                pb = [p.psum(f"x{c}pb{i}", [128, 512], F32, sp) for i in range(4)]
                ptr = [p.psum(f"x{c}ptr{i}", [128, 128], BF16, sp) for i in range(2)]
                rms_ps = p.psum(f"x{c}rms", [4, 512], F32, sp)

                rq = scr
                ln_phase(p, x_in, x_ch, xn_t, xnT, stats, CT, xd, ptr, c)

                def mm_kv(e, bank, c=c):
                    if c == 0:
                        ld.wait(e)
                    last = None
                    for d in range(DC):
                        last = e.matmul(bank[:], p.wkv[:, d, :], xnT[:, d, :],
                                        start=(d == 0), stop=(d == DC - 1))
                    return last

                def ev_kv(e, bank):
                    return e.tensor_copy(kvT[:], bank[:])
                p.pipeline([(mm_kv, [Cons("vector", ev_kv)])], pb[0:1])

                G6 = [(g, 3 * g, min(3, H - 3 * g)) for g in range(6)]
                items = []
                for g, h0, nh3 in G6:
                    R = 32 * nh3
                    def mmfn(e, bank, g=g, h0=h0, R=R):
                        return e.matmul(bank[0:R, :],
                                        p.wk1[:, 32 * h0:32 * h0 + R],
                                        kvT[:], start=True, stop=True)

                    def evfn(e, bank, g=g, R=R):
                        e.activation(sqb[0:R, g, :], bank[0:R, :], AF.Square)
                        return e.activation(raw[0:R, g, :], bank[0:R, :],
                                            AF.Copy)
                    items.append((mmfn, [Cons("scalar", evfn)]))
                p.pipeline(items, pb)
                items = []
                for g, h0, nh3 in G6:
                    R = 32 * nh3
                    def mmfn(e, bank, g=g, R=R, nh3=nh3):
                        return e.matmul(bank[0:nh3, :], trisel[0:R, 0:nh3],
                                        sqb[0:R, g, :], start=True, stop=True)

                    def ev1(e, bank, g=g, nh3=nh3):
                        return e.tensor_scalar(rq[0:nh3, 0:CN], bank[0:nh3, :],
                                               1.0 / 32.0, 1e-6, ALU.mult,
                                               ALU.add)

                    def ev2(e, bank, g=g, nh3=nh3):
                        return e.activation(rq[0:nh3, 0:CN], rq[0:nh3, 0:CN],
                                            AF.Sqrt)

                    def ev3(e, bank, g=g, nh3=nh3):
                        p.seq(e, e.reciprocal(rq[0:nh3, 0:CN], rq[0:nh3, 0:CN]))
                        return e.tensor_copy(rrmsK[0:nh3, g, :], rq[0:nh3, 0:CN])
                    items.append((mmfn, [Cons("vector", ev1),
                                         Cons("scalar", ev2, deps=[(0, 0)]),
                                         Cons("vector", ev3, deps=[(1, 0)],
                                              reads_bank=False)]))
                p.pipeline(items, [rms_ps])
                items = []
                for g, h0, nh3 in G6:
                    R = 32 * nh3
                    def mmfn(e, bank, g=g, nh3=nh3, R=R):
                        return e.matmul(bank[0:R, :], triselT[0:nh3, 0:R],
                                        rrmsK[0:nh3, g, :],
                                        start=True, stop=True)

                    def evfn(e, bank, g=g, c=c, R=R):
                        p.seq(e, e.tensor_scalar(scr[0:R, 0:CN],
                                                 raw[0:R, g, :],
                                                 p.kg_col[0:R, 0:1], None,
                                                 ALU.mult))
                        return e.tensor_tensor(
                            ktT[0:R, g, c * CN:(c + 1) * CN],
                            scr[0:R, 0:CN], bank[0:R, :], ALU.mult)
                    items.append((mmfn, [Cons("vector", evfn)]))
                p.pipeline(items, pb)
                items = []
                for g, h0, nh3 in G6:
                    R = 32 * nh3
                    def mmfn(e, bank, g=g, h0=h0, R=R):
                        return e.matmul(bank[0:R, :],
                                        p.wv1[:, 32 * h0:32 * h0 + R],
                                        kvT[:], start=True, stop=True)

                    def evfn(e, bank, g=g, R=R):
                        return e.tensor_copy(raw[0:R, g, :], bank[0:R, :])
                    items.append((mmfn, [Cons("vector", evfn)]))
                p.pipeline(items, pb)
                items = []
                for g, h0, nh3 in G6:
                    R = 32 * nh3
                    for jt in range(CT):
                        def mmfn(e, bank, g=g, jt=jt, R=R):
                            return e.transpose(bank[:, 0:R],
                                               raw[0:R, g, jt * 128:(jt + 1) * 128],
                                               p.ident_bf[0:R, 0:R])

                        def evfn(e, bank, g=g, h0=h0, nh3=nh3, jt=jt, c=c):
                            last = None
                            for m in range(nh3):
                                last = e.tensor_copy(
                                    vtok[:, c * CT + jt, h0 + m, 0:32],
                                    bank[:, 32 * m:32 * m + 32])
                            return last
                        items.append((mmfn, [Cons("vector", evfn)]))
                p.pipeline(items, ptr)

                with nc.Block() as b:
                    @b.gpsimd
                    def _(e):
                        src = W["Wq"][:, 0:128].rearrange("(c p) f -> p c f",
                                                          p=128)
                        wda.inc(e.dma_start(wcol_a[:, 0, :, :], src), 16)
                items = []
                hold_a = {}
                prefsem_a = p.sem("pl_s1")
                base_wda = wda.n
                for dout in range(DC):
                    def mmfn(e, bank, dout=dout, base_wda=base_wda):
                        if dout == 0:
                            e.wait_ge(wda.h, base_wda)
                        else:
                            e.wait_ge(prefsem_a.h,
                                      hold_a["B"] + 16 * dout)
                        last = None
                        for d in range(DC):
                            last = e.matmul(bank[:],
                                            wcol_a[:, dout % 2, d, :],
                                            xnT[:, d, :],
                                            start=(d == 0), stop=(d == DC - 1))
                        return last

                    def evfn(e, bank, dout=dout):
                        e.activation(sqb[:, dout, :], bank[:], AF.Square)
                        return e.activation(raw[:, dout, :], bank[:], AF.Copy)

                    def pref(e, bank, dout=dout):
                        nxt = dout + 1
                        if nxt >= DC:
                            return e.nop()
                        src = W["Wq"][:, nxt * 128:(nxt + 1) * 128].rearrange(
                            "(c p) f -> p c f", p=128)
                        return e.dma_start(wcol_a[:, nxt % 2, :, :], src)
                    items.append((mmfn, [Cons("scalar", evfn),
                                         Cons("gpsimd", pref, reads_bank=False,
                                              dma=True)]))
                hold_a["B"] = prefsem_a.n
                p.pipeline(items, pb)
                items = []
                for dout in range(DC):
                    def mmfn(e, bank, dout=dout):
                        return e.matmul(bank[0:2, :], pairsel[:],
                                        sqb[:, dout, :], start=True, stop=True)

                    def ev1(e, bank, dout=dout):
                        return e.tensor_scalar(rq[0:2, :], bank[0:2, :],
                                               1.0 / 64.0, 1e-6, ALU.mult,
                                               ALU.add)

                    def ev2(e, bank, dout=dout):
                        return e.activation(rq[0:2, :], rq[0:2, :], AF.Sqrt)

                    def ev3(e, bank, dout=dout):
                        p.seq(e, e.reciprocal(rq[0:2, :], rq[0:2, :]))
                        return e.tensor_copy(rrmsQ[0:2, dout, :], rq[0:2, :])
                    items.append((mmfn, [Cons("vector", ev1),
                                         Cons("scalar", ev2, deps=[(0, 0)]),
                                         Cons("vector", ev3, deps=[(1, 0)],
                                              reads_bank=False)]))
                p.pipeline(items, [rms_ps])
                items = []
                for dout in range(DC):
                    def mmfn(e, bank, dout=dout):
                        return e.matmul(bank[:], p.zsel[:],
                                        rrmsQ[0:2, dout, :],
                                        start=True, stop=True)

                    def evfn(e, bank, dout=dout):
                        p.seq(e, e.tensor_scalar(scr[:, 0:CN], raw[:, dout, :],
                                                 p.qg_col[:, 0:1], None,
                                                 ALU.mult))
                        p.seq(e, e.tensor_tensor(qtmp[:, 0:CN], scr[:, 0:CN],
                                                 bank[:], ALU.mult))
                        h0, h1 = 2 * dout, 2 * dout + 1
                        p.seq(e, e.tensor_copy(
                            qrope[32 * (h0 % 3):32 * (h0 % 3) + 32, h0 // 3, :],
                            qtmp[0:32, 0:CN]))
                        return e.tensor_copy(
                            qrope[32 * (h1 % 3):32 * (h1 % 3) + 32, h1 // 3, :],
                            qtmp[64:96, 0:CN])
                    items.append((mmfn, [Cons("vector", evfn)]))
                p.pipeline(items, pb)

            nj = 4 * c + 4
            with ExitStack() as sp:
                sb = [p.psum(f"a{c}sb{i}", [128, 512], F32, sp) for i in range(2)]
                pv = [p.psum(f"a{c}pv{i}", [33, 512], F32, sp) for i in range(6)]
                hgroups = [list(range(0, 6)), list(range(6, 12)),
                           list(range(12, 16))]
                for hg in hgroups:
                    items = []
                    first = {h: True for h in hg}
                    for j in range(nj):
                        for h in hg:
                            def mmfn(e, bank, j=j, h=h):
                                b3, c3 = 32 * (h % 3), h // 3
                                return e.matmul(
                                    bank[:],
                                    ktT[b3:b3 + 32, c3, j * 128:(j + 1) * 128],
                                    qrope[b3:b3 + 32, c3, :],
                                    start=True, stop=True)

                            def ev_exp(e, bank, idx=len(items)):
                                return e.activation(Ptile[:, idx % 2, :],
                                                    bank[:], AF.Exp)

                            diag = (j >= 4 * c)

                            def ev_msk(e, bank, j=j, c=c, idx=len(items),
                                       diag=diag):
                                if not diag:
                                    return e.engine_nop()
                                return e.tensor_tensor(
                                    Ptile[:, idx % 2, :], Ptile[:, idx % 2, :],
                                    cmask[:, j - 4 * c, :], ALU.mult)
                            items.append((mmfn, [
                                Cons("scalar", ev_exp, deps=[("post", 2)]),
                                Cons("vector", ev_msk, deps=[(0, 0)],
                                     reads_bank=False),
                            ]))

                    def mk_post(items_meta):
                        def post(e, i):
                            j, h = items_meta[i]
                            slot = hg.index(h)
                            stt = first[h]
                            first[h] = False
                            return e.matmul(pv[slot][:],
                                            vtok[:, j, h, :],
                                            Ptile[:, i % 2, :],
                                            start=stt, stop=False)
                        return post
                    meta = [(j, h) for j in range(nj) for h in hg]
                    p.pipeline(items, sb, post=mk_post(meta), post_lag=2,
                               post_wait_streams=[0, 1])
                    with nc.Block() as b:
                        @b.vector
                        def _(e, hg=hg):
                            for si, h in enumerate(hg):
                                p.seq(e, e.tensor_copy(scr[32:33, 0:CN],
                                                       pv[si][32:33, :]))
                                p.seq(e, e.reciprocal(scr[32:33, 0:CN],
                                                      scr[32:33, 0:CN]))
                                p.seq(e, e.tensor_copy(Drb[32:33, si, :],
                                                       scr[32:33, 0:CN]))
                    items = []
                    for si, h in enumerate(hg):
                        def mmfn(e, bank, si=si):
                            return e.matmul(bank[0:32, :], ones33[32:33, :],
                                            Drb[32:33, si, :],
                                            start=True, stop=True)

                        def evfn(e, bank, si=si, h=h):
                            p.seq(e, e.tensor_copy(scr[0:32, 0:CN], bank[0:32, :]))
                            p.seq(e, e.tensor_tensor(scr[0:32, 0:CN],
                                                     scr[0:32, 0:CN],
                                                     pv[si][0:32, :], ALU.mult))
                            return e.tensor_copy(
                                oT[32 * (h % 3):32 * (h % 3) + 32, h // 3, :],
                                scr[0:32, 0:CN])
                        items.append((mmfn, [Cons("vector", evfn)]))
                    p.pipeline(items, sb)
                items = []
                for ti in range(CT):
                    for half in range(2):
                        def mmfn(e, bank, ti=ti, half=half):
                            last = None
                            for cc in range(6):
                                last = e.matmul(
                                    bank[:], oT[:, cc, ti * 128:(ti + 1) * 128],
                                    p.woT[:, cc, half * 512:(half + 1) * 512],
                                    start=(cc == 0), stop=(cc == 5))
                            return last

                        def evfn(e, bank, ti=ti, half=half):
                            sl = slice(half * 512, (half + 1) * 512)
                            return e.tensor_add(x_ch[:, ti, sl], x_ch[:, ti, sl],
                                                bank[:])
                        items.append((mmfn, [Cons("vector", evfn)]))
                p.pipeline(items, sb)
            with nc.Block() as b:
                @b.sync
                def _(e, c=c):
                    dst = x_out[c * CN:(c + 1) * CN, :].rearrange(
                        "(t p) f -> p t f", p=128)
                    st.inc(e.dma_start(dst, x_ch[:, 0:CT, :]), 16)
    with nc.Block() as b:
        @b.sync
        def _(e):
            st.wait(e)


def _build_l1a_program():
    stack = ExitStack()
    nc = bass.Bass("TRN2", target_bir_lowering=False)
    x_in = nc.dram_tensor("x_in", [N_TOK, D], F32, kind="ExternalInput").ap()
    W = {
        "Wq": nc.dram_tensor("Wq", [D, D], F32, kind="ExternalInput").ap(),
        "Wkv": nc.dram_tensor("Wkv", [D, 128], F32, kind="ExternalInput").ap(),
        "Wk": nc.dram_tensor("Wk", [128, 512], F32, kind="ExternalInput").ap(),
        "Wv": nc.dram_tensor("Wv", [128, 512], F32, kind="ExternalInput").ap(),
        "Wo": nc.dram_tensor("Wo", [D, D], F32, kind="ExternalInput").ap(),
        "q_g": nc.dram_tensor("q_g", [64], F32, kind="ExternalInput").ap(),
        "k_g": nc.dram_tensor("k_g", [32], F32, kind="ExternalInput").ap(),
    }
    x_out = nc.dram_tensor("x_out", [N_TOK, D], F32, kind="ExternalOutput").ap()
    p = Prog(nc, stack)
    setup_common(p)
    setup_zsel(p)
    build_l1_attn(p, x_in, x_out, W)
    stack.close()
    nc.finalize()
    return nc



CAPACITY = int(1.2 * 32768 / 8)



def prolog_wT_multi(p, tag, sources):
    nc = p.nc
    ns = len(sources)
    with ExitStack() as stk:
        nat = [p.sbuf(f"{tag}_nat{i}", [128, DC, D], BF16, stk)
               for i in range(2)]
        dstT = [p.sbuf(f"{tag}_dst{i}", [128, DC, D], BF16, stk)
                for i in range(2)]
        ptr = [p.psum(f"{tag}_tp{i}", [128, 128], BF16, stk) for i in range(2)]
        ldw = p.sem(tag + "_ld")
        tpw = p.sem(tag + "_tp")
        evw = p.sem(tag + "_ev")
        otw = p.sem(tag + "_ot")
        b_ld, b_tp, b_ev, b_ot = ldw.n, tpw.n, evw.n, otw.n
        with nc.Block() as b:
            @b.gpsimd
            def _(e):
                for si, (src, dst) in enumerate(sources):
                    if si >= 2:
                        e.wait_ge(tpw.h, b_tp + 64 * (si - 1))
                    s2 = src.rearrange("(c p) d -> p c d", p=128)
                    ldw.inc(e.dma_start(nat[si % 2][:], s2), 16)

            @b.tensor
            def _(e):
                for si in range(ns):
                    e.wait_ge(ldw.h, b_ld + 16 * (si + 1))
                    for rc in range(DC):
                        for dcc in range(DC):
                            tau = si * 64 + rc * DC + dcc
                            if tau >= 2:
                                e.wait_ge(evw.h, b_ev + tau - 1)
                            tpw.inc(e.transpose(
                                ptr[tau % 2][:],
                                nat[si % 2][:, rc, dcc * 128:(dcc + 1) * 128],
                                p.ident_bf[:]))

            @b.scalar
            def _(e):
                for si in range(ns):
                    if si >= 2:
                        e.wait_ge(otw.h, b_ot + 16 * (si - 1))
                    for rc in range(DC):
                        for dcc in range(DC):
                            tau = si * 64 + rc * DC + dcc
                            e.wait_ge(tpw.h, b_tp + tau + 1)
                            evw.inc(e.activation(
                                dstT[si % 2][:, dcc, rc * 128:(rc + 1) * 128],
                                ptr[tau % 2][:], AF.Copy))

            @b.sync
            def _(e):
                for si, (src, dst) in enumerate(sources):
                    e.wait_ge(evw.h, b_ev + 64 * (si + 1))
                    d2 = dst.rearrange("(c p) he -> p c he", p=128)
                    otw.inc(e.dma_start(d2, dstT[si % 2][:]), 16)
                e.wait_ge(otw.h, b_ot + 16 * ns)


def build_l1_ffn(p: Prog, x_in, x_out, W, rank_mask, hid_d, cc_in, cc_out):
    nc = p.nc
    H, DH, E = 16, 64, 8
    CH = 8
    CT = NT // CH
    CN = CT * 128

    ld = p.sem("l1f_ld")
    xd = p.sem("l1f_xd")
    st = p.sem("l1f_st")
    cs = p.sem("l1f_cs")

    with ExitStack() as sa:

        def prolog_wT(name, dram, wT_d):
            with ExitStack() as stk:
                nat = p.sbuf(name + "_nat", [128, DC, D], BF16, stk)
                dstT = p.sbuf(name + "_T", [128, DC, D], BF16, stk)
                with nc.Block() as b:
                    @b.gpsimd
                    def _(e):
                        src = dram.rearrange("h e d -> (h e) d").rearrange(
                            "(c p) d -> p c d", p=128)
                        ld.inc(e.dma_start(nat[:], src), 16)
                items = []
                for rc in range(DC):
                    for dcc in range(DC):
                        def mmfn(e, bank, rc=rc, dcc=dcc):
                            if rc == 0 and dcc == 0:
                                ld.wait(e)
                            return e.transpose(bank[:],
                                               nat[:, rc, dcc * 128:(dcc + 1) * 128],
                                               p.ident_bf[:])

                        def evfn(e, bank, rc=rc, dcc=dcc):
                            return e.tensor_copy(dstT[:, dcc, rc * 128:(rc + 1) * 128],
                                                 bank[:])
                        items.append((mmfn, [Cons("vector", evfn)]))
                with ExitStack() as sp:
                    ptr = [p.psum(f"{name}_tp{i}", [128, 128], BF16, sp)
                           for i in range(2)]
                    p.pipeline(items, ptr)
                with nc.Block() as b:
                    @b.sync
                    def _(e):
                        dst = wT_d.rearrange("(c p) he -> p c he", p=128)
                        xd.inc(e.dma_start(dst, dstT[:]), 16)
            with nc.Block() as b:
                @b.sync
                def _(e):
                    xd.wait(e)

        wqT_d = nc.dram_tensor("l1f_wqT", [D, D], BF16, kind="Internal").ap()
        wkT_d = nc.dram_tensor("l1f_wkT", [D, D], BF16, kind="Internal").ap()
        wvT_d = nc.dram_tensor("l1f_wvT", [D, D], BF16, kind="Internal").ap()
        fl = lambda w: w.rearrange("h e d -> (h e) d")
        prolog_wT_multi(p, "f_wt3", [(fl(W["Wq"]), wqT_d),
                                     (fl(W["Wk"]), wkT_d),
                                     (fl(W["Wv"]), wvT_d)])

        with nc.Block() as b:
            @b.gpsimd
            def _(e):
                p.fbq = load_bias_col_f32(p, "f_wq_b",
                                          W["bq"].rearrange("h e -> (h e)"),
                                          D, ld, e, sa)
                p.fbk = load_bias_col_f32(p, "f_wk_b",
                                          W["bk"].rearrange("h e -> (h e)"),
                                          D, ld, e, sa)
                p.fbv = load_bias_col_f32(p, "f_wv_b",
                                          W["bv"].rearrange("h e -> (h e)"),
                                          D, ld, e, sa)
        with nc.Block() as b:
            @b.gpsimd
            def _(e):
                p.wswT = p.sbuf("f_wsw", [128, DC, E], BF16, sa)
                p.bsw_row = load_bias_row_bf16(p, "f_bsw", W["bsw"], E, ld, e, sa)
                p.rkmask = p.sbuf("f_rkm", [128, 1], F32, sa)
                ld.inc(e.dma_start(p.rkmask[:], rank_mask), 16)
        s_rows = ExitStack()
        with nc.Block() as b:
            @b.gpsimd
            def _(e):
                p.wsw_nat = p.sbuf("f_wswn", [E, D], BF16, s_rows)
                ld.inc(e.dma_start(p.wsw_nat[:], W["Wsw"]), 16)
        with ExitStack() as sp:
            ptrw = [p.psum(f"f_wswt{i}", [128, 128], BF16, sp) for i in range(2)]
            items = []
            for ccc in range(DC):
                def mmfn(e, bank, ccc=ccc):
                    if ccc == 0:
                        ld.wait(e)
                    return e.transpose(bank[0:128, 0:E],
                                       p.wsw_nat[0:E, ccc * 128:(ccc + 1) * 128],
                                       p.ident_bf[0:E, 0:E])

                def evfn(e, bank, ccc=ccc):
                    return e.tensor_copy(p.wswT[:, ccc, :], bank[:, 0:E])
                items.append((mmfn, [Cons("vector", evfn)]))
            p.pipeline(items, ptrw)
        with nc.Block() as b:
            @b.vector
            def _(e):
                ld.wait(e)
                e.tensor_scalar(p.fbk[:], p.fbk[:], 1.0 / 32.0, None, ALU.mult)
        s_rows.close()

        x_ch = p.sbuf("f_x_ch", [128, CT, D], F32, sa)
        xn_t = p.sbuf("f_xn_t", [128, CT, D], BF16, sa)
        xnT = p.sbuf("f_xnT", [128, DC, CN], BF16, sa)
        s1 = p.sbuf("f_s1", [128, CT], F32, sa)
        ssq = p.sbuf("f_ssq", [128, CT], F32, sa)
        mean = p.sbuf("f_mean", [128, CT], F32, sa)
        rstd = p.sbuf("f_rstd", [128, CT], F32, sa)
        scr = p.sbuf("f_scr", [128, CN], F32, sa)
        stats = (s1, ssq, mean, rstd, scr)
        umask = p.sbuf("f_umask", [128, 4, CN], BF16, sa)
        with nc.Block() as b:
            @b.vector
            def _(e):
                e.tensor_scalar(umask[:], p.cmask_l1f[:] if False else umask[:],
                                0.0, None, ALU.mult) if False else None
                e.scalar_tensor_tensor(umask[:], p.l1f_cmask[:], -1.0,
                                       p.cc[1e-5][:].broadcast_to([128, 1])
                                       if False else None, ALU.mult, ALU.add) \
                    if False else None

        with ExitStack() as s_att:
            ktT = p.sbuf("f_ktT", [128, DC, N_TOK], BF16, s_att)
            vtok = p.sbuf("f_vtok", [128, NT, H, DH + 1], BF16, s_att)
            qT2 = p.sbuf("f_qT2", [128, DC, CN], BF16, s_att)
            Ptile = p.sbuf("f_Pt", [128, 2, CN], BF16, s_att)
            hT_blk = p.sbuf("f_hT", [128, DC, CN], BF16, s_att)
            vsuf = p.sbuf("f_vsuf", [1, H, DH + 1], BF16, s_att)
            Drb = p.sbuf("f_Drb", [DH + 1, 6, CN], BF16, s_att)
            ones65 = p.sbuf("f_ones65", [DH + 1, DH], BF16, s_att)
            cmask = p.sbuf("f_cmask", [128, 4, CN], BF16, s_att)
            with nc.Block() as b:
                @b.gpsimd
                def _(e):
                    e.memset(vtok[:], 0.0)
                    e.memset(ones65[:], 1.0)
                    e.iota(scr.bitcast(F32)[:, 0:D] if False else scr[:, 0:D],
                           [[1, 1024]], base=0, channel_multiplier=0,
                           allow_small_or_imprecise_dtypes=True) \
                        if False else None
            with nc.Block() as b:
                @b.gpsimd
                def _(e):
                    e.iota(cmask[:], [[-128, 4], [1, CN]], base=0,
                           channel_multiplier=-1,
                           allow_small_or_imprecise_dtypes=True)
            with nc.Block() as b:
                @b.vector
                def _(e):
                    p.seq(e, e.tensor_single_scalar(umask[:], cmask[:], 0.0,
                                                    ALU.is_lt))
                    e.tensor_single_scalar(cmask[:], cmask[:], 0.0, ALU.is_ge)
            with nc.Block() as b:
                @b.vector
                def _(e):
                    e.memset(vtok[:, :, :, DH:DH + 1], 1.0)

            wcol = p.sbuf("f_wcol", [128, 2, DC, 128], BF16, s_att)
            wdsem = p.sem("l1f_wd")

            def stream_proj(wT_d, xnT, evgen, pb, tag):
                with nc.Block() as b:
                    @b.sync
                    def _(e):
                        src = wT_d[:, 0:128].rearrange("(c p) he -> p c he", p=128)
                        wdsem.inc(e.dma_start(wcol[:, 0, :, :], src), 16)
                items = []
                base0 = wdsem.n
                hold = {}
                prefsem = p.sem("pl_s1")
                for dout in range(DC):
                    def mmfn(e, bank, dout=dout):
                        if dout == 0:
                            e.wait_ge(wdsem.h, base0)
                        else:
                            e.wait_ge(prefsem.h, hold["B"] + 16 * dout)
                        last = None
                        for d in range(DC):
                            last = e.matmul(
                                bank[:], wcol[:, dout % 2, d, :],
                                xnT[:, d, :], start=(d == 0), stop=(d == DC - 1))
                        return last

                    def pref(e, bank, dout=dout):
                        nxt = dout + 1
                        if nxt >= DC:
                            return e.nop()
                        src = wT_d[:, nxt * 128:(nxt + 1) * 128].rearrange(
                            "(c p) he -> p c he", p=128)
                        return e.dma_start(wcol[:, nxt % 2, :, :], src)
                    items.append((mmfn, [evgen(dout),
                                         Cons("sync", pref, reads_bank=False,
                                              dma=True)]))
                hold["B"] = prefsem.n
                p.pipeline(items, pb)

            for c in range(CH):
                with ExitStack() as sp:
                    pb = [p.psum(f"fa{c}p{i}", [128, 512], F32, sp) for i in range(4)]
                    ptr = [p.psum(f"fa{c}t{i}", [128, 128], BF16, sp) for i in range(2)]
                    ln_phase(p, x_in, x_ch, xn_t, xnT, stats, CT, xd, ptr, c)
                    def evgen_k(dout, c=c):
                        def evfn(e, bank, dout=dout, c=c):
                            return e.tensor_scalar(
                                ktT[:, dout, c * CN:(c + 1) * CN], bank[:],
                                p.fbk[:, dout:dout + 1], 1.0 / 32.0,
                                ALU.add, ALU.mult)
                        return Cons("vector", evfn)
                    stream_proj(wkT_d, xnT, evgen_k, pb, "k")
                    def evgen_v(dout):
                        def evfn(e, bank, dout=dout):
                            return e.tensor_scalar(hT_blk[:, dout, :], bank[:],
                                                   p.fbv[:, dout:dout + 1],
                                                   None, ALU.add)
                        return Cons("vector", evfn)
                    stream_proj(wvT_d, xnT, evgen_v, pb, "v")
                    items = []
                    for dout in range(DC):
                        for jt in range(CT):
                            def mmfn(e, bank, dout=dout, jt=jt):
                                return e.transpose(
                                    bank[:], hT_blk[:, dout, jt * 128:(jt + 1) * 128],
                                    p.ident_bf[:])

                            def evfn(e, bank, dout=dout, jt=jt, c=c):
                                h0, h1 = 2 * dout, 2 * dout + 1
                                e.tensor_copy(vtok[:, c * CT + jt, h0, 0:DH],
                                              bank[:, 0:64])
                                return e.tensor_copy(vtok[:, c * CT + jt, h1, 0:DH],
                                                     bank[:, 64:128])
                            items.append((mmfn, [Cons("vector", evfn)]))
                    p.pipeline(items, ptr)

            for c in range(CH):
                nj = 4 * c + 4
                with ExitStack() as sp:
                    pb = [p.psum(f"fb{c}p{i}", [128, 512], F32, sp) for i in range(2)]
                    ptr = [p.psum(f"fb{c}t{i}", [128, 128], BF16, sp) for i in range(2)]
                    vs_ps = p.psum(f"fb{c}vs", [1, DH + 1], F32, sp)
                    ln_phase(p, x_in, x_ch, xn_t, xnT, stats, CT, xd, ptr, c)
                    def evgen_q(dout):
                        def evfn(e, bank, dout=dout):
                            return e.tensor_scalar(qT2[:, dout, :], bank[:],
                                                   p.fbq[:, dout:dout + 1],
                                                   None, ALU.add)
                        return Cons("vector", evfn)
                    stream_proj(wqT_d, xnT, evgen_q, pb, "q")
                    sufj = list(range(4 * (c + 1), NT))
                    if sufj:
                        items = []
                        for h in range(H):
                            def mmfn(e, bank, h=h):
                                last = None
                                for ji, j in enumerate(sufj):
                                    last = e.matmul(bank[0:1, :],
                                                    p.ones_col_bf[:],
                                                    vtok[:, j, h, :],
                                                    start=(ji == 0),
                                                    stop=(ji == len(sufj) - 1))
                                return last

                            def evfn(e, bank, h=h):
                                return e.tensor_copy(vsuf[0:1, h, :], bank[0:1, :])
                            items.append((mmfn, [Cons("vector", evfn)]))
                        p.pipeline(items, [vs_ps])
                    else:
                        with nc.Block() as b:
                            @b.vector
                            def _(e):
                                e.memset(vsuf[:], 0.0)
                with ExitStack() as sp:
                    sb = [p.psum(f"fb{c}s{i}", [128, 512], F32, sp) for i in range(2)]
                    pv = [p.psum(f"fb{c}v{i}", [DH + 1, 512], F32, sp)
                          for i in range(6)]
                    hgroups = [list(range(0, 6)), list(range(6, 12)),
                               list(range(12, 16))]
                    for hg in hgroups:
                        items = []
                        first = {h: True for h in hg}
                        for j in range(nj):
                            for h in hg:
                                def mmfn(e, bank, j=j, h=h):
                                    b2, c2 = 64 * (h % 2), h // 2
                                    return e.matmul(
                                        bank[:],
                                        ktT[b2:b2 + 64, c2, j * 128:(j + 1) * 128],
                                        qT2[b2:b2 + 64, c2, :],
                                        start=True, stop=True)

                                def ev_exp(e, bank, idx=len(items)):
                                    return e.activation(Ptile[:, idx % 2, :],
                                                        bank[:], AF.Exp)

                                diag = (j >= 4 * c)

                                def ev_msk(e, bank, j=j, c=c, idx=len(items),
                                           diag=diag):
                                    if not diag:
                                        return e.engine_nop()
                                    return e.tensor_tensor(
                                        Ptile[:, idx % 2, :], Ptile[:, idx % 2, :],
                                        cmask[:, j - 4 * c, :], ALU.mult)
                                items.append((mmfn, [
                                    Cons("scalar", ev_exp, deps=[("post", 2)]),
                                    Cons("vector", ev_msk, deps=[(0, 0)],
                                         reads_bank=False),
                                ]))

                        def mk_post(meta, hg=hg, first=first):
                            def post(e, i):
                                j, h = meta[i]
                                slot = hg.index(h)
                                stt_ = first[h]
                                first[h] = False
                                return e.matmul(pv[slot][:],
                                                vtok[:, j, h, :],
                                                Ptile[:, i % 2, :],
                                                start=stt_, stop=False)
                            return post
                        meta = [(j, h) for j in range(nj) for h in hg]
                        p.pipeline(items, sb, post=mk_post(meta), post_lag=2,
                                   post_wait_streams=[0, 1])
                        items = []
                        for si, h in enumerate(hg):
                            def mmfn(e, bank, si=si, h=h, c=c):
                                for jo in range(4):
                                    e.matmul(bank[:], vtok[:, 4 * c + jo, h, :],
                                             umask[:, jo, :],
                                             start=False, stop=False)
                                return e.matmul(bank[:],
                                                vsuf[0:1, h, :],
                                                p.ones_row_bf[0:1, 0:CN],
                                                start=False, stop=True)

                            def evfn(e, bank, si=si):
                                p.seq(e, e.tensor_copy(scr[64:65, 0:CN],
                                                       bank[64:65, :]))
                                p.seq(e, e.reciprocal(scr[64:65, 0:CN],
                                                      scr[64:65, 0:CN]))
                                return e.tensor_copy(Drb[64:65, si, :],
                                                     scr[64:65, 0:CN])
                            items.append((mmfn, [Cons("vector", evfn)]))
                        p.pipeline(items, [pv[si] for si in range(len(hg))])
                        items = []
                        for si, h in enumerate(hg):
                            def mmfn(e, bank, si=si):
                                return e.matmul(bank[0:64, :], ones65[64:65, :],
                                                Drb[64:65, si, :],
                                                start=True, stop=True)

                            def evfn(e, bank, si=si, h=h):
                                p.seq(e, e.tensor_copy(scr[0:64, 0:CN],
                                                       bank[0:64, :]))
                                p.seq(e, e.tensor_tensor(scr[0:64, 0:CN],
                                                         scr[0:64, 0:CN],
                                                         pv[si][0:64, :],
                                                         ALU.mult))
                                return e.tensor_copy(
                                    hT_blk[64 * (h % 2):64 * (h % 2) + 64,
                                           h // 2, :],
                                    scr[0:64, 0:CN])
                            items.append((mmfn, [Cons("vector", evfn)]))
                        p.pipeline(items, sb)
                with nc.Block() as b:
                    @b.sync
                    def _(e, c=c):
                        dst = hid_d[:, c * CN:(c + 1) * CN].rearrange(
                            "(c2 p) n -> p c2 n", p=128)
                        st.inc(e.dma_start(dst, hT_blk[:]), 16)

        ohf = p.sbuf("f_ohf", [128, NT, E], BF16, sa)
        ohf32 = p.sbuf("f_ohf32", [128, NT, E], F32, sa)
        rp_all = p.sbuf("f_rp", [128, NT], F32, sa)
        gloc = p.sbuf("f_gloc", [128, NT], F32, sa)
        Rsb = p.sbuf("f_Rsb", [1, E], F32, sa)
        offrow = p.sbuf("f_off", [1, E], F32, sa)
        cntp = p.sbuf("f_cntp", [16, 128], F32, sa)
        cnt_all = p.sbuf("f_cnta", [128, 128], F32, sa)
        hidT_c = p.sbuf("f_hidc", [128, DC, CN], BF16, sa)
        m_col = p.sbuf("f_mcol", [128, 1], F32, sa)
        negm = p.sbuf("f_negm", [128, 1], F32, sa)
        se_col = p.sbuf("f_se", [128, 1], F32, sa)
        oh0 = p.sbuf("f_oh0", [128, E], BF16, sa)
        t2b = p.sbuf("f_t2b", [128, E], F32, sa)
        zer8 = p.sbuf("f_zer8", [128, E], F32, sa)
        scr8 = p.sbuf("f_scr8", [128, E], F32, sa)
        scr8b = p.sbuf("f_scr8b", [128, E], F32, sa)
        with nc.Block() as b:
            @b.gpsimd
            def _(e):
                e.memset(zer8[:], 0.0)
                e.memset(Rsb[:], 0.0)
                e.memset(cntp[:], 0.0)
        with ExitStack() as sp:
            lg_ps = p.psum("fc_lg", [128, E], F32, sp)
            cs_ps = p.psum("fc_cs", [128, E], F32, sp)
            R_ps = p.psum("fc_R", [1, E], F32, sp)
            off_ps = p.psum("fc_off", [1, E], F32, sp)
            pe_s = p.sem("fc_pe")
            v_s = p.sem("fc_v")
            sc_s = p.sem("fc_sc")
            for c in range(CH):
                with nc.Block() as b:
                    @b.sync
                    def _(e, c=c):
                        src = hid_d[:, c * CN:(c + 1) * CN].rearrange(
                            "(c2 p) n -> p c2 n", p=128)
                        xd.inc(e.dma_start(hidT_c[:], src), 16)
                base_v_pre = v_s.n
                base_sc_pre = sc_s.n
                with nc.Block() as b:
                    gt0 = c * CT

                    @b.tensor
                    def _(e, c=c, gt0=gt0):
                        xd.wait(e)
                        for t in range(CT):
                            gt = gt0 + t
                            if gt > 0:
                                e.wait_ge(sc_s.h, gt)
                            for d in range(DC):
                                e.matmul(lg_ps[:], hidT_c[:, d, t * 128:(t + 1) * 128],
                                         p.wswT[:, d, :], start=(d == 0),
                                         stop=False)
                            e.matmul(lg_ps[:], p.ones_row_bf[0:1, 0:128],
                                     p.bsw_row[0:1, :], start=False, stop=True) \
                                .then_inc(pe_s.h, 1)
                            pe_s.n += 1
                            e.wait_ge(v_s.h, base_v_pre + 2 * t + 1)
                            e.matmul(cs_ps[:], cmask[:, 0, 0:128],
                                     ohf[:, gt, :], start=True, stop=False)
                            e.matmul(cs_ps[:], p.ones_row_f32[0:1, 0:128],
                                     Rsb[0:1, :], start=False, stop=True) \
                                .then_inc(pe_s.h, 1)
                            pe_s.n += 1
                            e.matmul(R_ps[:], p.ones_col_bf[:], ohf[:, gt, :],
                                     start=(gt == 0), stop=(gt == NT - 1)) \
                                .then_inc(pe_s.h, 1)
                            pe_s.n += 1

                    @b.vector
                    def _(e, c=c, gt0=gt0):
                        base_pe = pe_s.n - 3 * CT
                        base_sc = base_sc_pre
                        for t in range(CT):
                            gt = gt0 + t
                            e.wait_ge(pe_s.h, base_pe + 3 * t + 1)
                            p.seq(e, e.reduce_max(m_col[:], lg_ps[:], AX.X))
                            p.seq(e, e.tensor_scalar(negm[:], m_col[:], -1.0,
                                                     None, ALU.mult))
                            p.seq(e, e.tensor_scalar(oh0[:], lg_ps[:], m_col[:, 0:1],
                                                     None, ALU.is_equal))
                            p.seq(e, e.tensor_tensor_scan(t2b[:], oh0[:], zer8[:],
                                                          0.0, ALU.add, ALU.add))
                            p.seq(e, e.tensor_single_scalar(t2b[:], t2b[:], 1.0,
                                                            ALU.is_equal))
                            v_s.inc(e.tensor_tensor(ohf[:, gt, :], oh0[:], t2b[:],
                                                    ALU.mult))
                            e.wait_ge(v_s.h, v_s.n)
                            p.seq(e, e.tensor_copy(ohf32[:, gt, :], ohf[:, gt, :]))
                            e.wait_ge(sc_s.h, base_sc + t + 1)
                            p.seq(e, e.reciprocal(rp_all[:, gt:gt + 1], se_col[:]))
                            e.wait_ge(pe_s.h, base_pe + 3 * t + 2)
                            p.seq(e, e.tensor_tensor(scr8[:], cs_ps[:],
                                                     ohf32[:, gt, :], ALU.mult))
                            p.seq(e, e.reduce_sum(gloc[:, gt:gt + 1], scr8[:],
                                                  AX.X))
                            e.wait_ge(pe_s.h, base_pe + 3 * t + 3)
                            v_s.inc(e.tensor_copy(Rsb[0:1, :], R_ps[0:1, :]))

                    @b.scalar
                    def _(e, c=c, gt0=gt0):
                        base_v = v_s.n - 2 * CT
                        for t in range(CT):
                            e.wait_ge(v_s.h, base_v + 2 * t + 1)
                            sc_s.inc(e.activation(scr8b[:], lg_ps[:], AF.Exp,
                                                  bias=negm[:, 0:1],
                                                  accum_out=se_col[:]))
                            sc_s.n += 0
            with nc.Block() as b:
                @b.vector
                def _(e):
                    e.tensor_copy(cntp[0:1, 0:E], R_ps[0:1, :])
            with nc.Block() as b:
                @b.sync
                def _(e):
                    xd.inc(e.dma_start(cc_in, cntp[:]), 16)
            with nc.Block() as b:
                @b.gpsimd
                def _(e):
                    xd.wait(e)
                    cs.inc(e.collective_compute(
                        "AllGather", ALU.bypass, [[0, 1, 2, 3, 4, 5, 6, 7]],
                        [cc_in.opt()], [cc_out.opt()]), 1)
            with nc.Block() as b:
                @b.sync
                def _(e):
                    cs.wait(e)
                    xd.inc(e.dma_start(cnt_all[:], cc_out), 16)
            with nc.Block() as b:
                @b.tensor
                def _(e):
                    xd.wait(e)
                    e.matmul(off_ps[:], p.rkmask[:], cnt_all[:, 0:E],
                             start=True, stop=True)
            with nc.Block() as b:
                @b.vector
                def _(e):
                    e.tensor_copy(offrow[0:1, :], off_ps[0:1, :])

        p.dbg_dump("ohf32", ohf32[:], [128, NT, E])
        p.dbg_dump("gloc", gloc[:], [128, NT])
        p.dbg_dump("off", offrow[:], [1, E])
        with ExitStack() as sd:
            weT_d = nc.dram_tensor("l1f_weT", [E * D, D], BF16, kind="Internal").ap()
            prolog_wT_multi(p, "f_wte",
                            [(W["We"][ei], weT_d[ei * D:(ei + 1) * D, :])
                             for ei in range(E)])

            acc = p.sbuf("f_acc", [128, CT, D], F32, sd)
            scrd = p.sbuf("f_scrd", [128, D], F32, sd)
            gam_b = p.sbuf("f_gam_b", [128, D], BF16, sd)
            bet_b = p.sbuf("f_bet_b", [128, D], BF16, sd)
            s_gb = ExitStack()
            with nc.Block() as b:
                @b.gpsimd
                def _(e):
                    p.be_sb = p.sbuf("f_be", [8, D], BF16, sd)
                    ld.inc(e.dma_start(p.be_sb[:], W["be"]), 16)
                    p.gam_row = load_bias_row_bf16(p, "f_gam", W["gamma"], D,
                                                   ld, e, s_gb)
                    p.bet_row = load_bias_row_bf16(p, "f_bet", W["beta"], D,
                                                   ld, e, s_gb)
            with ExitStack() as sp:
                gb_ps = p.psum("f_gbps", [128, 512], F32, sp)
                items = []
                for which in range(4):
                    def mmfn(e, bank, which=which):
                        if which == 0:
                            ld.wait(e)
                        row = p.gam_row if which < 2 else p.bet_row
                        return e.matmul(bank[:], p.ones_row_bf[0:1, 0:128],
                                        row[0:1, (which % 2) * 512:(which % 2 + 1) * 512],
                                        start=True, stop=True)

                    def evfn(e, bank, which=which):
                        dst = gam_b if which < 2 else bet_b
                        return e.tensor_copy(
                            dst[:, (which % 2) * 512:(which % 2 + 1) * 512],
                            bank[:])
                    items.append((mmfn, [Cons("vector", evfn)]))
                p.pipeline(items, [gb_ps])
            s_gb.close()

            hid_tok = p.sbuf("f_hidtok", [128, CT, D], BF16, sd)
            kept = p.sbuf("f_kept", [128, CT], F32, sd)
            kinv = p.sbuf("f_kinv", [128, CT], F32, sd)
            ohfT = p.sbuf("f_ohfT", [8, CT, 128], BF16, sd)
            m2 = p.sbuf("f_m2", [128, CT], F32, sd)
            r2 = p.sbuf("f_r2", [128, CT], F32, sd)
            web = p.sbuf("f_web", [128, 2, DC, D], BF16, sd)
            wes = p.sem("l1f_we")

            for c in range(CH):
                with ExitStack() as sp:
                    pb = [p.psum(f"fd{c}p{i}", [128, 512], F32, sp) for i in range(4)]
                    ptr = [p.psum(f"fd{c}t{i}", [128, 128], BF16, sp) for i in range(2)]
                    ob_ps = p.psum(f"fd{c}ob", [128, E], F32, sp)
                    with nc.Block() as b:
                        @b.sync
                        def _(e, c=c):
                            src = hid_d[:, c * CN:(c + 1) * CN].rearrange(
                                "(c2 p) n -> p c2 n", p=128)
                            xd.inc(e.dma_start(hidT_c[:], src), 16)
                            src0 = weT_d[0:D, :].rearrange("(cp p) f -> p cp f",
                                                           p=128)
                            wes.inc(e.dma_start(web[:, 0, :, :], src0), 16)
                    ln_phase(p, x_in, x_ch, xn_t, xnT, stats, CT, xd, ptr, c)
                    with nc.Block() as b:
                        @b.tensor
                        def _(e, c=c):
                            e.matmul(ob_ps[:], p.ones_row_f32[0:1, 0:128],
                                     offrow[0:1, :], start=True, stop=True)
                    with nc.Block() as b:
                        @b.vector
                        def _(e, c=c):
                            for t in range(CT):
                                gt = c * CT + t
                                p.seq(e, e.tensor_tensor(scr8[:], ob_ps[:],
                                                         ohf32[:, gt, :],
                                                         ALU.mult))
                                p.seq(e, e.reduce_sum(kept[:, t:t + 1], scr8[:],
                                                      AX.X))
                                p.seq(e, e.scalar_tensor_tensor(
                                    kept[:, t:t + 1], gloc[:, gt:gt + 1], -1.0,
                                    kept[:, t:t + 1], ALU.add, ALU.add))
                                p.seq(e, e.tensor_single_scalar(
                                    kept[:, t:t + 1], kept[:, t:t + 1],
                                    CAPACITY - 0.5, ALU.is_lt))
                                p.seq(e, e.tensor_scalar(
                                    kinv[:, t:t + 1], kept[:, t:t + 1], -1.0, 1.0,
                                    ALU.mult, ALU.add))
                    items = []
                    for t in range(CT):
                        for d in range(DC):
                            def mmfn(e, bank, t=t, d=d):
                                return e.transpose(
                                    bank[:], hidT_c[:, d, t * 128:(t + 1) * 128],
                                    p.ident_bf[:])

                            def evfn(e, bank, t=t, d=d):
                                return e.tensor_copy(
                                    hid_tok[:, t, d * 128:(d + 1) * 128], bank[:])
                            items.append((mmfn, [Cons("vector", evfn)]))
                    p.pipeline(items, ptr)
                    items = []
                    for t in range(CT):
                        def mm_oht(e, bank, t=t, c=c):
                            return e.transpose(bank[0:E, 0:128],
                                               ohf[:, c * CT + t, :],
                                               p.ident_bf[:])

                        def ev_oht(e, bank, t=t):
                            return e.tensor_copy(ohfT[0:E, t, :], bank[0:E, 0:128])
                        items.append((mm_oht, [Cons("vector", ev_oht)]))
                    p.pipeline(items, ptr)
                    items = []
                    hold_e = {}
                    prefsem = p.sem("pl_s1")
                    for ei in range(E):
                        for t in range(CT):
                            def mmfn(e, bg, t=t, ei=ei):
                                if ei == 0:
                                    e.wait_ge(wes.h, hold_e["W"])
                                else:
                                    e.wait_ge(prefsem.h,
                                              hold_e["B"] + 16 * ((ei - 1) * CT + 1))
                                last = None
                                for half in range(2):
                                    for d in range(DC):
                                        last = e.matmul(
                                            bg[half][:],
                                            hidT_c[:, d, t * 128:(t + 1) * 128],
                                            web[:, ei % 2, d,
                                                half * 512:(half + 1) * 512],
                                            start=(d == 0), stop=(d == DC - 1))
                                return last

                            def evfn(e, bg, t=t, ei=ei, c=c):
                                gt = c * CT + t
                                last = None
                                for half in range(2):
                                    sl = slice(half * 512, (half + 1) * 512)
                                    if ei == 0:
                                        last = e.tensor_scalar(
                                            acc[:, t, sl], bg[half][:],
                                            ohf32[:, gt, ei:ei + 1], None,
                                            ALU.mult)
                                    else:
                                        last = e.scalar_tensor_tensor(
                                            acc[:, t, sl], bg[half][:],
                                            ohf32[:, gt, ei:ei + 1],
                                            acc[:, t, sl], ALU.mult, ALU.add)
                                    if half == 0:
                                        p.seq(e, last)
                                return last

                            def pref(e, bg, t=t, ei=ei):
                                if t != 0 or ei >= E - 1:
                                    return e.nop()
                                src = weT_d[(ei + 1) * D:(ei + 2) * D, :].rearrange(
                                    "(cp p) f -> p cp f", p=128)
                                return e.dma_start(web[:, (ei + 1) % 2, :, :], src)
                            items.append((mmfn, [Cons("vector", evfn),
                                                 Cons("sync", pref,
                                                      reads_bank=False,
                                                      dma=True)]))
                    hold_e["B"] = prefsem.n
                    hold_e["W"] = wes.n
                    p.pipeline(items, [(pb[0], pb[1]), (pb[2], pb[3])])
                    items = []
                    for t in range(CT):
                        def mm_be(e, bg, t=t):
                            e.matmul(bg[0][:], ohfT[0:E, t, :], p.be_sb[:, 0:512],
                                     start=True, stop=True)
                            return e.matmul(bg[1][:], ohfT[0:E, t, :],
                                            p.be_sb[:, 512:1024],
                                            start=True, stop=True)

                        def ev_be(e, bg, t=t, c=c):
                            gt = c * CT + t
                            for half in range(2):
                                sl = slice(half * 512, (half + 1) * 512)
                                p.seq(e, e.tensor_add(acc[:, t, sl], acc[:, t, sl],
                                                      bg[half][:]))
                            p.seq(e, e.tensor_scalar(acc[:, t, :], acc[:, t, :],
                                                     kept[:, t:t + 1], None,
                                                     ALU.mult))
                            p.seq(e, e.scalar_tensor_tensor(
                                scrd[:, 0:D], hid_tok[:, t, :],
                                kinv[:, t:t + 1], acc[:, t, :],
                                ALU.mult, ALU.add))
                            p.seq(e, e.tensor_scalar(scrd[:, 0:D], scrd[:, 0:D],
                                                     rp_all[:, gt:gt + 1],
                                                     None, ALU.mult))
                            p.seq(e, e.tensor_scalar(acc[:, t, :], x_ch[:, t, :],
                                                     mean[:, t:t + 1],
                                                     rstd[:, t:t + 1],
                                                     ALU.subtract, ALU.mult))
                            return e.tensor_add(acc[:, t, :], acc[:, t, :],
                                                scrd[:, 0:D])
                        items.append((mm_be, [Cons("vector", ev_be)]))
                    p.pipeline(items, [(pb[0], pb[1]), (pb[2], pb[3])])
                    with nc.Block() as b:
                        @b.scalar
                        def _(e):
                            for t in range(CT):
                                e.activation(scrd[:, 0:D], acc[:, t, :], AF.Square,
                                             accum_out=ssq[:, t:t + 1])
                                e.activation(scrd[:, 0:D], acc[:, t, :], AF.Copy,
                                             accum_out=s1[:, t:t + 1])
                    with nc.Block() as b:
                        @b.vector
                        def _(e):
                            p.seq(e, e.tensor_scalar(m2[:, 0:CT], s1[:, 0:CT],
                                                     1.0 / D, None, ALU.mult))
                            p.seq(e, e.tensor_tensor(r2[:, 0:CT], m2[:, 0:CT],
                                                     m2[:, 0:CT], ALU.mult))
                            e.scalar_tensor_tensor(r2[:, 0:CT], ssq[:, 0:CT],
                                                   1.0 / D, r2[:, 0:CT],
                                                   ALU.mult, ALU.subtract)
                    with nc.Block() as b:
                        @b.scalar
                        def _(e):
                            e.activation(r2[:, 0:CT], r2[:, 0:CT], AF.Sqrt,
                                         bias=p.cc[1e-5][:])
                    with nc.Block() as b:
                        @b.vector
                        def _(e):
                            e.reciprocal(r2[:, 0:CT], r2[:, 0:CT])
                    with nc.Block() as b:
                        @b.vector
                        def _(e, c=c):
                            for t in range(CT):
                                p.seq(e, e.tensor_scalar(scrd[:, 0:D], acc[:, t, :],
                                                         m2[:, t:t + 1],
                                                         r2[:, t:t + 1],
                                                         ALU.subtract, ALU.mult))
                                p.seq(e, e.tensor_tensor(scrd[:, 0:D], scrd[:, 0:D],
                                                         gam_b[:], ALU.mult))
                                p.seq(e, e.tensor_add(scrd[:, 0:D], scrd[:, 0:D],
                                                      bet_b[:]))
                                p.seq(e, e.tensor_add(x_ch[:, t, :], x_ch[:, t, :],
                                                      scrd[:, 0:D]))
                    with nc.Block() as b:
                        @b.sync
                        def _(e, c=c):
                            dst = x_out[c * CN:(c + 1) * CN, :].rearrange(
                                "(t p) f -> p t f", p=128)
                            st.inc(e.dma_start(dst, x_ch[:, 0:CT, :]), 16)
    with nc.Block() as b:
        @b.sync
        def _(e):
            st.wait(e)


def _build_l1f_program():
    stack = ExitStack()
    nc = bass.Bass("TRN2", target_bir_lowering=False)
    x_in = nc.dram_tensor("x_in", [N_TOK, D], F32, kind="ExternalInput").ap()
    W = {
        "Wq": nc.dram_tensor("Wq", [16, 64, D], F32, kind="ExternalInput").ap(),
        "bq": nc.dram_tensor("bq", [16, 64], F32, kind="ExternalInput").ap(),
        "Wk": nc.dram_tensor("Wk", [16, 64, D], F32, kind="ExternalInput").ap(),
        "bk": nc.dram_tensor("bk", [16, 64], F32, kind="ExternalInput").ap(),
        "Wv": nc.dram_tensor("Wv", [16, 64, D], F32, kind="ExternalInput").ap(),
        "bv": nc.dram_tensor("bv", [16, 64], F32, kind="ExternalInput").ap(),
        "We": nc.dram_tensor("We", [8, D, D], F32, kind="ExternalInput").ap(),
        "be": nc.dram_tensor("be", [8, D], F32, kind="ExternalInput").ap(),
        "Wsw": nc.dram_tensor("Wsw", [8, D], F32, kind="ExternalInput").ap(),
        "bsw": nc.dram_tensor("bsw", [8], F32, kind="ExternalInput").ap(),
        "gamma": nc.dram_tensor("gamma", [D], F32, kind="ExternalInput").ap(),
        "beta": nc.dram_tensor("beta", [D], F32, kind="ExternalInput").ap(),
    }
    rank_mask = nc.dram_tensor("rank_mask", [128, 1], F32, kind="ExternalInput").ap()
    x_out = nc.dram_tensor("x_out", [N_TOK, D], F32, kind="ExternalOutput").ap()
    hid_d = nc.dram_tensor("hid_d", [D, N_TOK], BF16, kind="Internal").ap()
    cc_in = nc.dram_tensor("cc_in", [16, 128], F32, kind="Internal").ap()
    cc_out = nc.dram_tensor("cc_out", [128, 128], F32, kind="Internal").ap()
    p = Prog(nc, stack)
    setup_common(p)
    setup_ones_f32(p)
    build_l1_ffn(p, x_in, x_out, W, rank_mask, hid_d, cc_in, cc_out)
    stack.close()
    nc.finalize()
    return nc




def _build_fused_program():
    stack = ExitStack()
    nc = bass.Bass("TRN2", target_bir_lowering=False)
    xin = nc.dram_tensor("L0_attn_x", [N_TOK, D], F32, kind="ExternalInput").ap()
    EI = lambda n, sh: nc.dram_tensor(n, sh, F32, kind="ExternalInput").ap()
    W0a = {"wg_w": EI("wg_w", [D, D]), "wg_b": EI("wg_b", [D]),
           "wq_w": EI("wq_w", [D, D]), "wq_b": EI("wq_b", [D]),
           "wk_w": EI("wk_w", [D, D]), "wk_b": EI("wk_b", [D]),
           "wv_w": EI("wv_w", [D, D]), "wv_b": EI("wv_b", [D]),
           "out_w": EI("out_w", [D, D]), "out_b": EI("out_b", [D])}
    W0f = {"w_gate": EI("w_gate", [2816, D]), "w_up": EI("w_up", [2816, D]),
           "w_down": EI("w_down", [D, 2816])}
    W1a = {"Wq": EI("a_Wq", [D, D]), "Wkv": EI("a_Wkv", [D, 128]),
           "Wk": EI("a_Wk", [128, 512]), "Wv": EI("a_Wv", [128, 512]),
           "Wo": EI("a_Wo", [D, D]), "q_g": EI("a_qg", [64]),
           "k_g": EI("a_kg", [32])}
    W1f = {"Wq": EI("f_Wq", [16, 64, D]), "bq": EI("f_bq", [16, 64]),
           "Wk": EI("f_Wk", [16, 64, D]), "bk": EI("f_bk", [16, 64]),
           "Wv": EI("f_Wv", [16, 64, D]), "bv": EI("f_bv", [16, 64]),
           "We": EI("f_We", [8, D, D]), "be": EI("f_be_in", [8, D]),
           "Wsw": EI("f_Wsw", [8, D]), "bsw": EI("f_bsw_in", [8]),
           "gamma": EI("f_gamma", [D]), "beta": EI("f_beta", [D])}
    rank_mask = EI("rank_mask", [128, 1])
    x_out = nc.dram_tensor("x_out", [N_TOK, D], F32, kind="ExternalOutput").ap()
    x1 = nc.dram_tensor("x1_i", [N_TOK, D], F32, kind="Internal").ap()
    x2 = nc.dram_tensor("x2_i", [N_TOK, D], F32, kind="Internal").ap()
    x3 = nc.dram_tensor("x3_i", [N_TOK, D], F32, kind="Internal").ap()
    hid_d = nc.dram_tensor("hid_d", [D, N_TOK], BF16, kind="Internal").ap()
    cc_in = nc.dram_tensor("cc_in", [16, 128], F32, kind="Internal").ap()
    cc_out = nc.dram_tensor("cc_out", [128, 128], F32, kind="Internal").ap()

    p = Prog(nc, stack)
    setup_common(p)
    setup_zsel(p)
    setup_ones_f32(p)
    build_l0_attn(p, xin, x1, W0a)
    build_l0_ffn(p, x1, x2, W0f)
    build_l1_attn(p, x2, x3, W1a)
    build_l1_ffn(p, x3, x_out, W1f, rank_mask, hid_d, cc_in, cc_out)
    stack.close()
    nc.finalize()
    return nc


def kernel(**inputs):
    cores = list(range(N_CORES))
    x = np.ascontiguousarray(inputs["L0_attn_x"], dtype=np.float32)
    nc = _get_prog("fused", _build_fused_program)
    ren0a = {"wg_w": "L0_attn_wg_w", "wg_b": "L0_attn_wg_b",
             "wq_w": "L0_attn_wq_w", "wq_b": "L0_attn_wq_b",
             "wk_w": "L0_attn_wk_w", "wk_b": "L0_attn_wk_b",
             "wv_w": "L0_attn_wv_w", "wv_b": "L0_attn_wv_b",
             "out_w": "L0_attn_out_w", "out_b": "L0_attn_out_b",
             "w_gate": "L0_ffn_w_gate", "w_up": "L0_ffn_w_up",
             "w_down": "L0_ffn_w_down",
             "a_Wq": "L1_attn_Wq", "a_Wkv": "L1_attn_Wkv",
             "a_Wk": "L1_attn_Wk", "a_Wv": "L1_attn_Wv", "a_Wo": "L1_attn_Wo",
             "a_qg": "L1_attn_q_g", "a_kg": "L1_attn_k_g",
             "f_Wq": "L1_ffn_Wq", "f_bq": "L1_ffn_bq", "f_Wk": "L1_ffn_Wk",
             "f_bk": "L1_ffn_bk", "f_Wv": "L1_ffn_Wv", "f_bv": "L1_ffn_bv",
             "f_We": "L1_ffn_We", "f_be_in": "L1_ffn_be",
             "f_Wsw": "L1_ffn_Wsw", "f_bsw_in": "L1_ffn_bsw",
             "f_gamma": "L1_ffn_gamma", "f_beta": "L1_ffn_beta"}
    in_maps = []
    for c in cores:
        m = {"L0_attn_x": x[c]}
        for k, src in ren0a.items():
            m[k] = np.ascontiguousarray(inputs[src], np.float32)
        rk = np.zeros((128, 1), np.float32)
        for r in range(c):
            rk[16 * r, 0] = 1.0
        m["rank_mask"] = rk
        in_maps.append(m)
    res = run_bass_kernel_spmd(nc, in_maps, core_ids=cores)
    return np.stack([np.asarray(res.results[c]["x_out"], dtype=np.float32)
                     for c in cores])


# version 07
# speedup vs baseline: 1.0961x; 1.0027x over previous
import math
from contextlib import ExitStack

import numpy as np
import concourse.bass as bass
import concourse.mybir as mybir
from concourse.bass_utils import run_bass_kernel_spmd

F32 = mybir.dt.float32
BF16 = mybir.dt.bfloat16
I32 = mybir.dt.int32
AF = mybir.ActivationFunctionType
ALU = mybir.AluOpType
AX = mybir.AxisListType

N_CORES = 8
N_TOK = 4096
D = 1024
NT = N_TOK // 128
DC = D // 128



class Sem:
    def __init__(self, nc, stack, name):
        self.h = stack.enter_context(nc.semaphore(name))
        self.n = 0

    def inc(self, instr, k=1):
        instr.then_inc(self.h, k)
        self.n += k
        return self.n

    def wait(self, eng, n=None):
        eng.wait_ge(self.h, self.n if n is None else n)


class Cons:

    def __init__(self, engine, fn, deps=(), reads_bank=True, dma=False):
        self.engine = engine
        self.fn = fn
        self.deps = deps
        self.reads_bank = reads_bank
        self.inc = 16 if dma else 1


class Prog:
    def __init__(self, nc, stack):
        self.nc = nc
        self.stack = stack
        self.sems = {}
        self._uid = 0

    def uid(self):
        self._uid += 1
        return self._uid

    def sem(self, name):
        if name not in self.sems:
            self.sems[name] = Sem(self.nc, self.stack, name)
        return self.sems[name]

    def sbuf(self, name, shape, dtype, stack=None):
        return (stack or self.stack).enter_context(
            self.nc.sbuf_tensor(name, shape, dtype))

    def psum(self, name, shape, dtype, stack=None):
        return (stack or self.stack).enter_context(
            self.nc.psum_tensor(name, shape, dtype))

    def seq(self, e, instr):
        sem = self.sem("chain")
        sem.inc(instr)
        sem.wait(e)
        return instr

    def dbg_dump(self, name, sb_ap, shape):
        if not getattr(self, "dbg", False):
            return
        nc = self.nc
        t = nc.dram_tensor(f"dbg_{name}", list(shape), sb_ap.dtype,
                           kind="ExternalOutput").ap()
        sd = self.sem("dbg_sd")
        with nc.Block() as b:
            @b.sync
            def _(e):
                sd.inc(e.dma_start(t, sb_ap), 16)
        with nc.Block() as b:
            @b.sync
            def _(e):
                sd.wait(e)

    def pipeline(self, items, banks, post=None, post_lag=2,
                 post_wait_streams=None):
        nc = self.nc
        mm = self.sem("pl_mm")
        pp = self.sem("pl_pp")
        nb = len(banks)
        n_items = len(items)
        sig = items[0][1]
        ns = len(sig)
        for _mmfn, cons in items:
            assert len(cons) == ns and all(
                c.engine == s.engine for c, s in zip(cons, sig)), \
                "pipeline items must share one stream signature"
        ssem = [self.sem(f"pl_s{k}") for k in range(ns)]
        mm_base = mm.n
        pp_base = pp.n
        s_base = [s.n for s in ssem]
        bank_readers = [k for k in range(ns) if sig[k].reads_bank]
        if post_wait_streams is None:
            post_wait_streams = list(range(ns))

        with nc.Block() as b:
            @b.tensor
            def _(e):
                def do_post(pi):
                    for k in post_wait_streams:
                        e.wait_ge(ssem[k].h, s_base[k] + (pi + 1) * sig[k].inc)
                    pp.inc(post(e, pi))
                for i, (mmfn, _cons) in enumerate(items):
                    if i >= nb:
                        j = i - nb
                        for k in bank_readers:
                            e.wait_ge(ssem[k].h,
                                      s_base[k] + (j + 1) * sig[k].inc)
                    mm.inc(mmfn(e, banks[i % nb]))
                    if post is not None and i >= post_lag:
                        do_post(i - post_lag)
                if post is not None:
                    for pi in range(max(0, n_items - post_lag), n_items):
                        do_post(pi)

            by_engine = {}
            for k, c in enumerate(sig):
                by_engine.setdefault(c.engine, []).append(k)

            def mk_body(ks):
                def body(e):
                    for i, (_mmfn, cons) in enumerate(items):
                        for k in ks:
                            c = cons[k]
                            e.wait_ge(mm.h, mm_base + i + 1)
                            for (dk, lag) in c.deps:
                                need = i + 1 - lag
                                if need > 0:
                                    if dk == "post":
                                        e.wait_ge(pp.h, pp_base + need)
                                    else:
                                        e.wait_ge(ssem[dk].h,
                                                  s_base[dk] + need * sig[dk].inc)
                            last = c.fn(e, banks[i % nb])
                            ssem[k].inc(last, sig[k].inc)
                return body

            for ename, ks in by_engine.items():
                getattr(b, ename)(mk_body(ks))




CONST_COLS = (1e-5, 1e-6)


def setup_common(p: Prog):
    nc = p.nc
    p.ident_bf = p.sbuf("ident_bf", [128, 128], BF16)
    p.iota_jp = p.sbuf("iota_jp", [128, 128], F32)
    p.ones_row_bf = p.sbuf("ones_row_bf", [1, 512], BF16)
    p.ones_col_bf = p.sbuf("ones_col_bf", [128, 1], BF16)
    p.cc = {v: p.sbuf(f"cc_{i}", [128, 1], F32) for i, v in enumerate(CONST_COLS)}
    with nc.Block() as b:
        @b.gpsimd
        def _(e):
            e.iota(p.iota_jp[:], [[1, 128]], base=0, channel_multiplier=-1,
                   allow_small_or_imprecise_dtypes=True)
            e.memset(p.ones_row_bf[:], 1.0)
            e.memset(p.ones_col_bf[:], 1.0)
            for v, t in p.cc.items():
                e.memset(t[:], v)
    with nc.Block() as b:
        @b.vector
        def _(e):
            e.tensor_single_scalar(p.ident_bf[:], p.iota_jp[:], 0.0, ALU.is_equal)


def setup_zsel(p: Prog):
    nc = p.nc
    p.zsel = p.sbuf("zsel", [2, 128], BF16)
    p.iota2 = p.sbuf("iota2", [2, 128], F32)
    with nc.Block() as b:
        @b.gpsimd
        def _(e):
            e.iota(p.iota2[:], [[1, 128]], base=0, channel_multiplier=-64,
                   allow_small_or_imprecise_dtypes=True)
    with nc.Block() as b:
        @b.vector
        def _(e):
            p.seq(e, e.tensor_scalar(p.zsel[:], p.iota2[:], 0.0, 64.0,
                                     ALU.is_ge, ALU.min))
            e.scalar_tensor_tensor(p.zsel[:], p.iota2[:], 64.0, p.zsel[:],
                                   ALU.is_lt, ALU.mult)


def setup_ones_f32(p: Prog):
    nc = p.nc
    p.ones_row_f32 = p.sbuf("ones_row_f32", [1, 512], F32)
    with nc.Block() as b:
        @b.gpsimd
        def _(e):
            e.memset(p.ones_row_f32[:], 1.0)


def load_weight_bf16(p, name, dram_ap, rows, cols, sem, eng, stack=None):
    t = p.sbuf(name, [128, rows // 128, cols], BF16, stack)
    src = dram_ap.rearrange("(c p) f -> p c f", p=128)
    sem.inc(eng.dma_start(t[:], src), 16)
    return t


def load_bias_row_bf16(p, name, dram_ap, n, sem, eng, stack=None):
    t = p.sbuf(name, [1, n], BF16, stack)
    sem.inc(eng.dma_start(t[:], dram_ap[None, :]), 16)
    return t


def load_bias_col_f32(p, name, dram_ap, n, sem, eng, stack=None):
    t = p.sbuf(name, [128, n // 128], F32, stack)
    src = dram_ap.rearrange("(c p) -> p c", p=128)
    sem.inc(eng.dma_start(t[:], src, allow_slow_non_contiguous=True), 16)
    return t


def ln_phase(p: Prog, x_src_dram, x_ch, xn_t, xnT, stats, CT, xd: Sem,
             tp_banks, c):
    nc = p.nc
    s1, ssq, mean, rstd, scr = stats
    CN = CT * 128
    lns1 = p.sem("ln_s1")
    lnv1 = p.sem("ln_v1")
    lns2 = p.sem("ln_s2")
    lnxn = p.sem("ln_xn")
    lntp = p.sem("ln_tp")
    lnev = p.sem("ln_ev")
    b_s1, b_v1, b_s2 = lns1.n, lnv1.n, lns2.n
    b_xn, b_tp, b_ev = lnxn.n, lntp.n, lnev.n
    b_xd = xd.n
    with nc.Block() as b:
        @b.sync
        def _(e):
            src = x_src_dram[c * CN:(c + 1) * CN, :].rearrange(
                "(t p) f -> p t f", p=128)
            xd.inc(e.dma_start(x_ch[:, 0:CT, :], src), 16)

        @b.scalar
        def _(e):
            e.wait_ge(xd.h, b_xd + 16)
            last = None
            for t in range(CT):
                e.activation(xn_t[:, t, :], x_ch[:, t, :], AF.Square,
                             accum_out=ssq[:, t:t + 1])
                last = e.activation(xn_t[:, t, :], x_ch[:, t, :], AF.Copy,
                                    accum_out=s1[:, t:t + 1])
            lns1.inc(last)
            e.wait_ge(lnv1.h, b_v1 + 1)
            lns2.inc(e.activation(rstd[:, 0:CT], rstd[:, 0:CT], AF.Sqrt,
                                  bias=p.cc[1e-5][:]))
            tau = 0
            for t in range(CT):
                for d in range(DC):
                    e.wait_ge(lntp.h, b_tp + tau + 1)
                    lnev.inc(e.activation(xnT[:, d, t * 128:(t + 1) * 128],
                                          tp_banks[tau % 2][:], AF.Copy))
                    tau += 1

        @b.vector
        def _(e):
            e.wait_ge(lns1.h, b_s1 + 1)
            p.seq(e, e.tensor_scalar(mean[:, 0:CT], s1[:, 0:CT], 1.0 / D,
                                     None, ALU.mult))
            p.seq(e, e.tensor_tensor(rstd[:, 0:CT], mean[:, 0:CT],
                                     mean[:, 0:CT], ALU.mult))
            lnv1.inc(e.scalar_tensor_tensor(rstd[:, 0:CT], ssq[:, 0:CT],
                                            1.0 / D, rstd[:, 0:CT],
                                            ALU.mult, ALU.subtract))
            e.wait_ge(lns2.h, b_s2 + 1)
            p.seq(e, e.reciprocal(rstd[:, 0:CT], rstd[:, 0:CT]))
            for t in range(CT):
                lnxn.inc(e.tensor_scalar(
                    xn_t[:, t, :], x_ch[:, t, :], mean[:, t:t + 1],
                    rstd[:, t:t + 1], ALU.subtract, ALU.mult))

        @b.tensor
        def _(e):
            tau = 0
            for t in range(CT):
                e.wait_ge(lnxn.h, b_xn + t + 1)
                for d in range(DC):
                    if tau >= 2:
                        e.wait_ge(lnev.h, b_ev + tau - 1)
                    lntp.inc(e.transpose(
                        tp_banks[tau % 2][:],
                        xn_t[:, t, d * 128:(d + 1) * 128], p.ident_bf[:]))
                    tau += 1




def build_l0_attn(p: Prog, x_in, x_out, W):
    nc = p.nc
    CH = 4
    CT = NT // CH
    CN = CT * 128

    ld = p.sem("l0a_ld")
    xd = p.sem("l0a_xd")
    st = p.sem("l0a_st")

    with ExitStack() as st_all:
        x_ch = p.sbuf("l0_x_ch", [128, CT, D], F32, st_all)
        xn_t = p.sbuf("l0_xn_t", [128, CT, D], BF16, st_all)
        xnT = p.sbuf("l0_xnT", [128, DC, CN], BF16, st_all)
        s1 = p.sbuf("l0_s1", [128, CT], F32, st_all)
        ssq = p.sbuf("l0_ssq", [128, CT], F32, st_all)
        mean = p.sbuf("l0_mean", [128, CT], F32, st_all)
        rstd = p.sbuf("l0_rstd", [128, CT], F32, st_all)
        scr = p.sbuf("l0_scr", [128, D], F32, st_all)
        kv_sb = p.sbuf("l0_kv", [128, 8, 65], F32, st_all)
        kv_bd = p.sbuf("l0_kv_bd", [128, 8, 128], BF16, st_all)
        klhs = p.sbuf("l0_klhs", [128, 8, 2], BF16, st_all)
        stats = (s1, ssq, mean, rstd, scr)

        pb = [p.psum(f"l0_pb{i}", [128, 512], F32, st_all) for i in range(4)]
        ptr = [p.psum(f"l0_ptr{i}", [128, 128], BF16, st_all) for i in range(2)]

        with ExitStack() as st_p1:
            with nc.Block() as b:
                @b.gpsimd
                def _(e):
                    p.wk = load_weight_bf16(p, "l0_wk", W["wk_w"], D, D, ld, e, st_p1)
                    p.wv = load_weight_bf16(p, "l0_wv", W["wv_w"], D, D, ld, e, st_p1)
                    p.wk_br = load_bias_row_bf16(p, "l0_wk_b", W["wk_b"], D, ld, e, st_p1)
                    p.wv_br = load_bias_row_bf16(p, "l0_wv_b", W["wv_b"], D, ld, e, st_p1)
            kt_ch = p.sbuf("l0_kt_ch", [128, CT, D], BF16, st_p1)
            km_ch = p.sbuf("l0_km_ch", [128, CT, D], BF16, st_p1)
            vt_ch = p.sbuf("l0_vt_ch", [128, CT, D], BF16, st_p1)

            for c in range(CH):
                ln_phase(p, x_in, x_ch, xn_t, xnT, stats, CT, xd, ptr, c)
                items = []
                for t in range(CT):
                    for half in range(2):
                        for wi in range(2):
                            def mmfn(e, bank, t=t, half=half, wi=wi, c=c):
                                if c == 0 and t == 0 and half == 0 and wi == 0:
                                    ld.wait(e)
                                w_sb = p.wk if wi == 0 else p.wv
                                brow = p.wk_br if wi == 0 else p.wv_br
                                for d in range(DC):
                                    e.matmul(bank[:],
                                             xnT[:, d, t * 128:(t + 1) * 128],
                                             w_sb[:, d, half * 512:(half + 1) * 512],
                                             start=(d == 0), stop=False)
                                return e.matmul(
                                    bank[:], p.ones_row_bf[0:1, 0:128],
                                    brow[0:1, half * 512:(half + 1) * 512],
                                    start=False, stop=True)

                            if wi == 0:
                                def evfn(e, bank, t=t, half=half):
                                    sl = slice(half * 512, (half + 1) * 512)
                                    e.tensor_scalar(km_ch[:, t, sl], bank[:],
                                                    0.0, None, ALU.min)
                                    return e.tensor_scalar(kt_ch[:, t, sl],
                                                           bank[:], 0.0, None,
                                                           ALU.max)

                                def exfn(e, bank, t=t, half=half):
                                    sl = slice(half * 512, (half + 1) * 512)
                                    return e.activation(km_ch[:, t, sl],
                                                        km_ch[:, t, sl], AF.Exp)

                                def adfn(e, bank, t=t, half=half):
                                    sl = slice(half * 512, (half + 1) * 512)
                                    return e.tensor_add(kt_ch[:, t, sl],
                                                        kt_ch[:, t, sl],
                                                        km_ch[:, t, sl])
                                items.append((mmfn, [
                                    Cons("vector", evfn),
                                    Cons("scalar", exfn, deps=[(0, 0)],
                                         reads_bank=False),
                                    Cons("vector", adfn, deps=[(1, 0)],
                                         reads_bank=False)]))
                            else:
                                def evfn(e, bank, t=t, half=half):
                                    sl = slice(half * 512, (half + 1) * 512)
                                    return e.tensor_copy(vt_ch[:, t, sl], bank[:])
                                items.append((mmfn, [
                                    Cons("vector", evfn),
                                    Cons("scalar",
                                         lambda e, bank: e.nop(),
                                         reads_bank=False),
                                    Cons("vector",
                                         lambda e, bank: e.engine_nop(),
                                         reads_bank=False)]))
                p.pipeline(items, pb)
                items = []
                for t in range(CT):
                    for hp in range(8):
                        def mmfn(e, bank, t=t, hp=hp):
                            e.matmul(bank[:, 0:128],
                                     kt_ch[:, t, hp * 128:(hp + 1) * 128],
                                     vt_ch[:, t, hp * 128:(hp + 1) * 128],
                                     start=True, stop=False)
                            return e.matmul(bank[:, 128:129],
                                            kt_ch[:, t, hp * 128:(hp + 1) * 128],
                                            p.ones_col_bf[:],
                                            start=True, stop=True)

                        def evfn(e, bank, t=t, hp=hp, c=c):
                            first = (c == 0 and t == 0)
                            if first:
                                e.tensor_copy(kv_sb[0:64, hp, 0:64],
                                              bank[0:64, 0:64])
                                e.tensor_copy(kv_sb[64:128, hp, 0:64],
                                              bank[64:128, 64:128])
                                return e.tensor_copy(kv_sb[:, hp, 64:65],
                                                     bank[:, 128:129])
                            e.tensor_add(kv_sb[0:64, hp, 0:64],
                                         kv_sb[0:64, hp, 0:64], bank[0:64, 0:64])
                            e.tensor_add(kv_sb[64:128, hp, 0:64],
                                         kv_sb[64:128, hp, 0:64],
                                         bank[64:128, 64:128])
                            return e.tensor_add(kv_sb[:, hp, 64:65],
                                                kv_sb[:, hp, 64:65],
                                                bank[:, 128:129])
                        items.append((mmfn, [Cons("vector", evfn)]))
                p.pipeline(items, pb)

        p.dbg_dump("xnT", xnT[:], [128, DC, CN])
        p.dbg_dump("kv_sb", kv_sb[:], [128, 8, 65])
        with nc.Block() as b:
            @b.vector
            def _(e):
                e.memset(kv_bd[:], 0.0)
                e.memset(klhs[:], 0.0)
                for h in range(16):
                    hp, lo = h // 2, (h % 2) * 64
                    e.tensor_copy(kv_bd[lo:lo + 64, hp, lo:lo + 64],
                                  kv_sb[lo:lo + 64, hp, 0:64])
                for cc in range(8):
                    e.tensor_copy(klhs[0:64, cc, 0:1], kv_sb[0:64, cc, 64:65])
                    e.tensor_copy(klhs[64:128, cc, 1:2], kv_sb[64:128, cc, 64:65])

        with ExitStack() as st_p2:
            with nc.Block() as b:
                @b.gpsimd
                def _(e):
                    p.wq = load_weight_bf16(p, "l0_wq", W["wq_w"], D, D, ld, e, st_p2)
                    p.wg = load_weight_bf16(p, "l0_wg", W["wg_w"], D, D, ld, e, st_p2)
                    p.ow = load_weight_bf16(p, "l0_ow", W["out_w"], D, D, ld, e, st_p2)
                    p.wq_bc = load_bias_col_f32(p, "l0_wq_bc", W["wq_b"], D, ld, e, st_p2)
                    p.wg_bc = load_bias_col_f32(p, "l0_wg_bc", W["wg_b"], D, ld, e, st_p2)
                    p.ow_br = load_bias_row_bf16(p, "l0_ow_b", W["out_b"], D, ld, e, st_p2)
            qT = p.sbuf("l0_qT", [128, DC, CN], BF16, st_p2)
            qmin = p.sbuf("l0_qmin", [128, DC, CN], BF16, st_p2)
            gT_d = p.sbuf("l0_gT_d", [128, 2, 512], BF16, st_p2)
            zb_d = p.sbuf("l0_zb_d", [128, 2, 512], F32, st_p2)
            gsT = p.sbuf("l0_gsT", [128, DC, CN], BF16, st_p2)
            zT = p.sbuf("l0_zT", [2, DC, CN], BF16, st_p2)
            scrZ = p.sbuf("l0_scrZ", [2, 512], F32, st_p2)
            sc_f32 = p.sbuf("l0_sc", [128, 2, 512], F32, st_p2)
            zps = p.psum("l0_zps", [2, 512], F32, st_p2)
            zps2 = p.psum("l0_zps2", [2, 512], F32, st_p2)

            for c in range(CH):
                ln_phase(p, x_in, x_ch, xn_t, xnT, stats, CT, xd, ptr, c)
                items = []
                for dout in range(DC):
                    for nh in range(2):
                        def mmfn(e, bank, dout=dout, nh=nh, c=c):
                            if c == 0 and dout == 0 and nh == 0:
                                ld.wait(e)
                            last = None
                            for d in range(DC):
                                last = e.matmul(
                                    bank[:],
                                    p.wq[:, d, dout * 128:(dout + 1) * 128],
                                    xnT[:, d, nh * 512:(nh + 1) * 512],
                                    start=(d == 0), stop=(d == DC - 1))
                            return last

                        def evfn(e, bank, dout=dout, nh=nh):
                            sl = slice(nh * 512, (nh + 1) * 512)
                            e.tensor_scalar(qmin[:, dout, sl], bank[:],
                                            p.wq_bc[:, dout:dout + 1], 0.0,
                                            ALU.add, ALU.min)
                            return e.tensor_scalar(qT[:, dout, sl], bank[:],
                                                   p.wq_bc[:, dout:dout + 1],
                                                   0.0, ALU.add, ALU.max)

                        def exfn(e, bank, dout=dout, nh=nh):
                            sl = slice(nh * 512, (nh + 1) * 512)
                            return e.activation(qmin[:, dout, sl],
                                                qmin[:, dout, sl], AF.Exp)

                        def adfn(e, bank, dout=dout, nh=nh):
                            sl = slice(nh * 512, (nh + 1) * 512)
                            return e.tensor_add(qT[:, dout, sl],
                                                qT[:, dout, sl],
                                                qmin[:, dout, sl])
                        items.append((mmfn, [
                            Cons("vector", evfn),
                            Cons("scalar", exfn, deps=[(0, 0)],
                                 reads_bank=False),
                            Cons("vector", adfn, deps=[(1, 0)],
                                 reads_bank=False)]))
                p.pipeline(items, pb)
                items = []
                for nh in range(2):
                    for hp in range(DC):
                        def mmfn(e, bank, nh=nh, hp=hp):
                            return e.matmul(bank[0:2, 0:512], klhs[:, hp, :],
                                            qT[:, hp, nh * 512:(nh + 1) * 512],
                                            start=True, stop=True)

                        def evfn(e, bank, nh=nh, hp=hp):
                            p.seq(e, e.tensor_scalar(scrZ[:], bank[0:2, 0:512],
                                                     1e-6, None, ALU.add))
                            p.seq(e, e.reciprocal(scrZ[:], scrZ[:]))
                            return e.tensor_copy(
                                zT[0:2, hp, nh * 512:(nh + 1) * 512], scrZ[:])
                        items.append((mmfn, [Cons("vector", evfn)]))
                p.pipeline(items, [zps, zps2])
                if c == 0:
                    p.dbg_dump("qT0", qT[:], [128, DC, CN])
                    p.dbg_dump("zT0", zT[:], [2, DC, CN])
                items = []
                for dout in range(DC):
                    for nh in range(2):
                        def mmfn(e, bg, dout=dout, nh=nh):
                            bank_g, bank_z, bank_s = bg
                            for d in range(DC):
                                e.matmul(bank_g[:],
                                         p.wg[:, d, dout * 128:(dout + 1) * 128],
                                         xnT[:, d, nh * 512:(nh + 1) * 512],
                                         start=(d == 0), stop=(d == DC - 1))
                            e.matmul(bank_z[:], p.zsel[:],
                                     zT[0:2, dout, nh * 512:(nh + 1) * 512],
                                     start=True, stop=True)
                            return e.matmul(bank_s[:], kv_bd[:, dout, :],
                                            qT[:, dout, nh * 512:(nh + 1) * 512],
                                            start=True, stop=True)

                        def ev_g(e, bg, dout=dout, nh=nh):
                            return e.activation(gT_d[:, nh, 0:512], bg[0][:],
                                                AF.Sigmoid,
                                                bias=p.wg_bc[:, dout:dout + 1])

                        def ev_z(e, bg, nh=nh):
                            return e.tensor_copy(zb_d[:, nh, 0:512], bg[1][:])

                        def ev_s(e, bg, dout=dout, nh=nh):
                            p.seq(e, e.tensor_tensor(sc_f32[:, nh, :], bg[2][:],
                                                     zb_d[:, nh, 0:512],
                                                     ALU.mult))
                            return e.tensor_tensor(
                                gsT[:, dout, nh * 512:(nh + 1) * 512],
                                sc_f32[:, nh, :], gT_d[:, nh, 0:512], ALU.mult)
                        items.append((mmfn, [
                            Cons("scalar", ev_g),
                            Cons("vector", ev_z),
                            Cons("vector", ev_s, deps=[(0, 0)]),
                        ]))
                p.pipeline(items, [(pb[0], pb[1], pb[2])])

                if c == 0:
                    p.dbg_dump("gsT0", gsT[:], [128, DC, CN])
                items = []
                for t in range(CT):
                    for half in range(2):
                        def mmfn(e, bank, t=t, half=half):
                            for d in range(DC):
                                e.matmul(bank[:],
                                         gsT[:, d, t * 128:(t + 1) * 128],
                                         p.ow[:, d, half * 512:(half + 1) * 512],
                                         start=(d == 0), stop=False)
                            return e.matmul(
                                bank[:], p.ones_row_bf[0:1, 0:128],
                                p.ow_br[0:1, half * 512:(half + 1) * 512],
                                start=False, stop=True)

                        def evfn(e, bank, t=t, half=half):
                            sl = slice(half * 512, (half + 1) * 512)
                            return e.tensor_add(x_ch[:, t, sl], x_ch[:, t, sl],
                                                bank[:])
                        items.append((mmfn, [Cons("vector", evfn)]))
                p.pipeline(items, pb)
                with nc.Block() as b:
                    @b.sync
                    def _(e):
                        dst = x_out[c * CN:(c + 1) * CN, :].rearrange(
                            "(t p) f -> p t f", p=128)
                        st.inc(e.dma_start(dst, x_ch[:, 0:CT, :]), 16)
    with nc.Block() as b:
        @b.sync
        def _(e):
            st.wait(e)




def _build_l0a_program():
    stack = ExitStack()
    nc = bass.Bass("TRN2", target_bir_lowering=False)
    x_in = nc.dram_tensor("x_in", [N_TOK, D], F32, kind="ExternalInput").ap()
    names = ["wg_w", "wg_b", "wq_w", "wq_b", "wk_w", "wk_b", "wv_w", "wv_b",
             "out_w", "out_b"]
    W = {}
    for n in names:
        shape = [D, D] if n.endswith("_w") else [D]
        W[n] = nc.dram_tensor(n, shape, F32, kind="ExternalInput").ap()
    x_out = nc.dram_tensor("x_out", [N_TOK, D], F32, kind="ExternalOutput").ap()
    p = Prog(nc, stack)
    setup_common(p)
    setup_zsel(p)
    build_l0_attn(p, x_in, x_out, W)
    stack.close()
    nc.finalize()
    return nc


_PROGRAMS = {}


def _get_prog(name, builder):
    if name not in _PROGRAMS:
        _PROGRAMS[name] = builder()
    return _PROGRAMS[name]


def kernel_multi(**inputs):
    x = np.ascontiguousarray(inputs["L0_attn_x"], dtype=np.float32)
    cores = list(range(N_CORES))

    nc = _get_prog("l0a", _build_l0a_program)
    wn = ["wg_w", "wg_b", "wq_w", "wq_b", "wk_w", "wk_b", "wv_w", "wv_b",
          "out_w", "out_b"]
    in_maps = []
    for c in cores:
        m = {"x_in": x[c]}
        for n in wn:
            m[n] = np.ascontiguousarray(inputs["L0_attn_" + n], np.float32)
        in_maps.append(m)
    res = run_bass_kernel_spmd(nc, in_maps, core_ids=cores)
    x1 = [res.results[c]["x_out"] for c in cores]

    nc = _get_prog("l0f", _build_l0f_program)
    in_maps = []
    for c in cores:
        m = {"x_in": x1[c],
             "w_gate": np.ascontiguousarray(inputs["L0_ffn_w_gate"], np.float32),
             "w_up": np.ascontiguousarray(inputs["L0_ffn_w_up"], np.float32),
             "w_down": np.ascontiguousarray(inputs["L0_ffn_w_down"], np.float32)}
        in_maps.append(m)
    res = run_bass_kernel_spmd(nc, in_maps, core_ids=cores)
    x2 = [res.results[c]["x_out"] for c in cores]

    nc = _get_prog("l1a", _build_l1a_program)
    wn = ["Wq", "Wkv", "Wk", "Wv", "Wo", "q_g", "k_g"]
    in_maps = []
    for c in cores:
        m = {"x_in": x2[c]}
        for n in wn:
            m[n] = np.ascontiguousarray(inputs["L1_attn_" + n], np.float32)
        in_maps.append(m)
    res = run_bass_kernel_spmd(nc, in_maps, core_ids=cores)
    x3 = [res.results[c]["x_out"] for c in cores]

    nc = _get_prog("l1f", _build_l1f_program)
    wn = ["Wq", "bq", "Wk", "bk", "Wv", "bv", "We", "be", "Wsw", "bsw",
          "gamma", "beta"]
    in_maps = []
    for c in cores:
        m = {"x_in": x3[c]}
        for n in wn:
            m[n] = np.ascontiguousarray(inputs["L1_ffn_" + n], np.float32)
        rk = np.zeros((128, 1), np.float32)
        for r in range(c):
            rk[16 * r, 0] = 1.0
        m["rank_mask"] = rk
        in_maps.append(m)
    res = run_bass_kernel_spmd(nc, in_maps, core_ids=cores)
    out = np.stack([np.asarray(res.results[c]["x_out"], dtype=np.float32)
                    for c in cores])
    return out


def emit_quant(p, e, src, dst, W, scr, rstd_col=None):
    nb = W // 16
    B2, B3, B4 = scr["B2"], scr["B3"], scr["B4"]
    A, S, R = scr["A"], scr["S"], scr["R"]
    sq = lambda i: p.seq(e, i)
    I = lambda ap: ap.bitcast(I32)
    v3 = lambda t: t[:, 0:W].rearrange("p (b k) -> p b k", k=16)
    npart = src.shape[0]
    sq(e.tensor_single_scalar(I(B2[:, 0:W]), I(src), 0x7FFFFFFF,
                              ALU.bitwise_and))
    sq(e.tensor_reduce(A[:, 0:nb], v3(B2), AX.X, ALU.max))
    if rstd_col is not None:
        sq(e.tensor_scalar(S[:, 0:nb], A[:, 0:nb], rstd_col, 1.0 / 6.0,
                           ALU.mult, ALU.mult))
    else:
        sq(e.tensor_scalar(S[:, 0:nb], A[:, 0:nb], 1.0 / 6.0, None, ALU.mult))
    sq(e.tensor_single_scalar(S[:, 0:nb], S[:, 0:nb], 1e-12, ALU.max))
    sq(e.reciprocal(R[:, 0:nb], S[:, 0:nb]))
    if rstd_col is not None:
        sq(e.tensor_scalar(R[:, 0:nb], R[:, 0:nb], rstd_col, None, ALU.mult))
    Rb = R[0:npart, 0:nb].unsqueeze(2).broadcast_to([npart, nb, 16])
    Sb = S[0:npart, 0:nb].unsqueeze(2).broadcast_to([npart, nb, 16])
    sq(e.tensor_tensor(v3(B3), v3(B2), Rb, ALU.mult))
    sq(e.tensor_single_scalar(B4[:, 0:W], B3[:, 0:W], 0.25, ALU.is_gt))
    for t in (0.75, 1.25, 1.75):
        sq(e.scalar_tensor_tensor(B4[:, 0:W], B3[:, 0:W], t, B4[:, 0:W],
                                  ALU.is_gt, ALU.add))
    sq(e.tensor_single_scalar(B2[:, 0:W], B3[:, 0:W], 2.5, ALU.is_gt))
    sq(e.scalar_tensor_tensor(B2[:, 0:W], B3[:, 0:W], 3.5, B2[:, 0:W],
                              ALU.is_gt, ALU.add))
    sq(e.scalar_tensor_tensor(B4[:, 0:W], B4[:, 0:W], 0.5, B2[:, 0:W],
                              ALU.mult, ALU.add))
    sq(e.tensor_scalar(B2[:, 0:W], B3[:, 0:W], 5.0, 2.0, ALU.is_gt, ALU.mult))
    sq(e.tensor_add(B4[:, 0:W], B4[:, 0:W], B2[:, 0:W]))
    sq(e.tensor_tensor(v3(B4), v3(B4), Sb, ALU.mult))
    sq(e.tensor_single_scalar(I(B2[:, 0:W]), I(src), -2147483648,
                              ALU.bitwise_and))
    sq(e.tensor_tensor(I(B4[:, 0:W]), I(B4[:, 0:W]), I(B2[:, 0:W]),
                       ALU.bitwise_or))
    return e.tensor_copy(dst, B4[:, 0:W])


def emit_quant_srv(p, e, src, W, Asl, Ssl, Rsl, rstd_col=None):
    nb = W // 16
    B2 = p.f2_B2s
    sq = lambda i: p.seq(e, i)
    I = lambda ap: ap.bitcast(I32)
    sq(e.tensor_single_scalar(I(B2[:, 0:W]), I(src), 0x7FFFFFFF,
                              ALU.bitwise_and))
    v3 = B2[:, 0:W].rearrange("p (b k) -> p b k", k=16)
    sq(e.tensor_reduce(Asl, v3, AX.X, ALU.max))
    if rstd_col is not None:
        sq(e.tensor_scalar(Ssl, Asl, rstd_col, 1.0 / 6.0, ALU.mult, ALU.mult))
    else:
        sq(e.tensor_scalar(Ssl, Asl, 1.0 / 6.0, None, ALU.mult))
    sq(e.tensor_single_scalar(Ssl, Ssl, 1e-12, ALU.max))
    return e.reciprocal(Rsl, Ssl)


def emit_quant_pool(p, e, src, dst, W, Asl, Ssl, Rsl):
    nb = W // 16
    B2, B3, B4, S1 = p.f2_B2p, p.f2_B3p, p.f2_B4p, p.f2_Ps1
    sqp = p.sem("chainP")

    def sq(i):
        sqp.inc(i)
        sqp.wait(e)
        return i
    v3 = lambda t: t[:, 0:W].rearrange("p (b k) -> p b k", k=16)
    sq(e.tensor_scalar(B3[:, 0:W], src, 0.0, 2.0, ALU.is_ge, ALU.mult))
    sq(e.tensor_scalar(S1[:, 0:W], B3[:, 0:W], 1.0, None, ALU.subtract))
    sq(e.tensor_tensor(B2[:, 0:W], src, S1[:, 0:W], ALU.mult))
    Rb = Rsl.unsqueeze(2).broadcast_to([128, nb, 16])
    Sb = Ssl.unsqueeze(2).broadcast_to([128, nb, 16])
    sq(e.tensor_tensor(v3(B3), v3(B2), Rb, ALU.mult))
    sq(e.tensor_single_scalar(B4[:, 0:W], B3[:, 0:W], 0.25, ALU.is_gt))
    for t in (0.75, 1.25, 1.75):
        sq(e.tensor_single_scalar(B2[:, 0:W], B3[:, 0:W], t, ALU.is_gt))
        sq(e.tensor_add(B4[:, 0:W], B4[:, 0:W], B2[:, 0:W]))
    sq(e.tensor_scalar(B4[:, 0:W], B4[:, 0:W], 0.5, None, ALU.mult))
    for t in (2.5, 3.5):
        sq(e.tensor_single_scalar(B2[:, 0:W], B3[:, 0:W], t, ALU.is_gt))
        sq(e.tensor_add(B4[:, 0:W], B4[:, 0:W], B2[:, 0:W]))
    sq(e.tensor_scalar(B2[:, 0:W], B3[:, 0:W], 5.0, 2.0, ALU.is_gt, ALU.mult))
    sq(e.tensor_add(B4[:, 0:W], B4[:, 0:W], B2[:, 0:W]))
    sq(e.tensor_tensor(v3(B4), v3(B4), Sb, ALU.mult))
    return e.tensor_tensor(dst, B4[:, 0:W], S1[:, 0:W], ALU.mult)


def build_l0_ffn(p: Prog, x_in, x_out, W):
    nc = p.nc
    I_DIM = 2816
    IC = I_DIM // 128
    CH = 8
    CT = NT // CH
    CN = CT * 128

    with ExitStack() as st_all:
        QW = 512
        scr = {
            "B2": p.sbuf("f2_B2", [128, QW], F32, st_all),
            "B3": p.sbuf("f2_B3", [128, QW], F32, st_all),
            "B4": p.sbuf("f2_B4", [128, QW], F32, st_all),
            "A": p.sbuf("f2_A", [128, QW // 16], F32, st_all),
            "S": p.sbuf("f2_S", [128, QW // 16], F32, st_all),
            "R": p.sbuf("f2_R", [128, QW // 16], F32, st_all),
        }

        wgT = p.sbuf("f2_wgT", [128, DC, I_DIM], BF16, st_all)
        wuT = p.sbuf("f2_wuT", [128, DC, I_DIM], BF16, st_all)
        wdT = p.sbuf("f2_wdT", [128, IC, D], BF16, st_all)

        st_w = ExitStack()
        slab = [p.sbuf(f"f2_slab{i}", [128, I_DIM], F32, st_w) for i in range(3)]
        qsl = [p.sbuf(f"f2_qsl{i}", [128, I_DIM], BF16, st_w) for i in range(2)]
        ptrW = [p.psum(f"f2_ptrw{i}", [128, 128], BF16, st_w) for i in range(2)]

        slabs = []
        for r in range(I_DIM // 128):
            slabs.append((W["w_gate"][r * 128:(r + 1) * 128, :], D, wgT, r))
        for r in range(I_DIM // 128):
            slabs.append((W["w_up"][r * 128:(r + 1) * 128, :], D, wuT, r))
        for r in range(D // 128):
            slabs.append((W["w_down"][r * 128:(r + 1) * 128, :], I_DIM, wdT, r))
        gk = [0]
        sched = []
        for (dram, cols, dstT, r) in slabs:
            ch = []
            for off in range(0, cols, QW):
                wlen = min(QW, cols - off)
                ch.append((off, wlen, "V"))
                gk[0] += 1
            sched.append(ch)
        nsl = len(slabs)
        cumP = [0] * (nsl + 1)
        cumTP = [0] * (nsl + 1)
        for s in range(nsl):
            cumP[s + 1] = cumP[s] + sum(1 for c in sched[s] if c[2] == "P")
            cumTP[s + 1] = cumTP[s] + slabs[s][1] // 128

        wd = p.sem("f2_wd")
        wvq = p.sem("f2_wvq")
        wtp = p.sem("f2_wtp")
        wse = p.sem("f2_wse")
        b_wd, b_wvq = wd.n, wvq.n
        b_wtp, b_wse = wtp.n, wse.n

        with nc.Block() as b:
            @b.sync
            def _(e):
                for s in range(nsl):
                    if s >= 3:
                        e.wait_ge(wvq.h, b_wvq + s - 2)
                    dram, cols, dstT, r = slabs[s]
                    wd.inc(e.dma_start(slab[s % 3][:, 0:cols], dram), 16)

            @b.vector
            def _(e):
                for s in range(nsl):
                    dram, cols, dstT, r = slabs[s]
                    e.wait_ge(wd.h, b_wd + 16 * (s + 1))
                    if s >= 2:
                        e.wait_ge(wtp.h, b_wtp + cumTP[s - 1])
                    last = None
                    for (off, wlen, who) in sched[s]:
                        last = emit_quant(p, e, slab[s % 3][:, off:off + wlen],
                                          qsl[s % 2][:, off:off + wlen],
                                          wlen, scr)
                    wvq.inc(last)

            @b.tensor
            def _(e):
                tau = 0
                for s in range(nsl):
                    dram, cols, dstT, r = slabs[s]
                    e.wait_ge(wvq.h, b_wvq + s + 1)
                    for k in range(cols // 128):
                        if tau >= 2:
                            e.wait_ge(wse.h, b_wse + tau - 1)
                        wtp.inc(e.transpose(
                            ptrW[tau % 2][:],
                            qsl[s % 2][:, k * 128:(k + 1) * 128],
                            p.ident_bf[:]))
                        tau += 1

            @b.scalar
            def _(e):
                tau = 0
                for s in range(nsl):
                    dram, cols, dstT, r = slabs[s]
                    for k in range(cols // 128):
                        e.wait_ge(wtp.h, b_wtp + tau + 1)
                        wse.inc(e.activation(
                            dstT[:, k, r * 128:(r + 1) * 128],
                            ptrW[tau % 2][:], AF.Copy))
                        tau += 1
        st_w.close()

        x_ch = p.sbuf("f2_x_ch", [128, CT, D], F32, st_all)
        s1 = p.sbuf("f2_s1", [128, CT], F32, st_all)
        ssq = p.sbuf("f2_ssq", [128, CT], F32, st_all)
        mean = p.sbuf("f2_mean", [128, CT], F32, st_all)
        rstd = p.sbuf("f2_rstd", [128, CT], F32, st_all)
        xm = p.sbuf("f2_xm", [128, D], F32, st_all)
        scr0 = xm
        xq = p.sbuf("f2_xq", [128, 2, D], BF16, st_all)
        xqT = p.sbuf("f2_xqT", [128, 2, DC, 128], BF16, st_all)
        sg = p.sbuf("f2_sg", [128, 2, QW], F32, st_all)
        hbuf = p.sbuf("f2_hb", [128, 6, QW], F32, st_all)
        hq = p.sbuf("f2_hq", [128, I_DIM], BF16, st_all)
        hqT = p.sbuf("f2_hqT", [128, 2, IC, 128], BF16, st_all)

        pb = [p.psum(f"f2_pb{i}", [128, 512], F32, st_all) for i in range(4)]
        pd = [p.psum(f"f2_pd{i}", [128, 512], F32, st_all) for i in range(2)]
        ptr = [p.psum(f"f2_ptr{i}", [128, 128], BF16, st_all) for i in range(2)]

        IW = [(0, 512), (512, 512), (1024, 512), (1536, 512), (2048, 512),
              (2560, 256)]
        GROUPS = [(3, 4), (2, 5), (0, 1)]
        ORD = [3, 4, 2, 5, 0, 1]
        POS = {ii: i + 1 for i, ii in enumerate(ORD)}
        GRP = {3: 1, 4: 1, 2: 2, 5: 2, 0: 3, 1: 3}
        V_II = (3, 4, 2, 5, 0, 1)
        VQIDX = {3: 1, 4: 2, 2: 3, 5: 4, 0: 5, 1: 6}
        VSCHED = [("X", None), ("m", 3), ("m", 4), ("q", 3), ("m", 2),
                  ("m", 5), ("q", 4), ("m", 0), ("m", 1), ("q", 2),
                  ("q", 5), ("q", 0), ("q", 1), ("R", None)]
        CUMTP = {}
        _c = 0
        for _ii in ORD:
            _c += IW[_ii][1] // 128
            CUMTP[_ii] = _c
        UPB = {3: 2, 4: 3, 2: 2, 5: 3, 0: 2, 1: 3}
        GATB = {3: 0, 4: 1, 2: 0, 5: 1, 0: 0, 1: 1}
        HTOFF = {}
        off = 8
        for ii in ORD:
            HTOFF[ii] = off
            off += IW[ii][1] // 128
        TPT = off

        xd = p.sem("f2_xd")
        stq = p.sem("f2_st")
        sA = p.sem("f2_sA")
        vM = p.sem("f2_vM")
        sQ = p.sem("f2_sQ")
        vR = p.sem("f2_vR")
        vx = p.sem("f2_vx")
        tp = p.sem("f2_tp")
        se = p.sem("f2_se")
        mmG = p.sem("f2_mmG")
        sS = p.sem("f2_sS")
        vH = p.sem("f2_vH")
        vq = p.sem("f2_vq")
        mmD = p.sem("f2_mmD")
        vRes = p.sem("f2_vRes")
        b_xd, b_st, b_sA, b_vM, b_sQ, b_vR = (xd.n, stq.n, sA.n, vM.n, sQ.n,
                                              vR.n)
        b_vx, b_tp, b_se, b_mmG, b_sS, b_vH = (vx.n, tp.n, se.n, mmG.n, sS.n,
                                               vH.n)
        b_vq, b_mmD, b_vRes = vq.n, mmD.n, vRes.n

        T = CH * CT
        TAU_x, TAU_h, TP_x, TP_h = {}, {}, {}, {}
        TAU_x[0] = 0
        cnt = 8
        TP_x[0] = 8
        for t_ in range(T):
            if t_ > 0 and t_ % CT == 0:
                TAU_h[t_ - 1] = cnt
                cnt += 22
                TP_h[t_ - 1] = cnt
                TAU_x[t_] = cnt
                cnt += 8
                TP_x[t_] = cnt
            if t_ % CT != 0 and t_ >= 1:
                TAU_h[t_ - 1] = cnt
                cnt += 22
                TP_h[t_ - 1] = cnt
            if t_ + 1 < T and (t_ + 1) % CT != 0:
                TAU_x[t_ + 1] = cnt
                cnt += 8
                TP_x[t_ + 1] = cnt
        TAU_h[T - 1] = cnt
        cnt += 22
        TP_h[T - 1] = cnt

        with nc.Block() as b:
            @b.sync
            def _(e):
                for c in range(CH):
                    if c > 0:
                        e.wait_ge(stq.h, b_st + 16 * c)
                    src = x_in[c * CN:(c + 1) * CN, :].rearrange(
                        "(t p) f -> p t f", p=128)
                    xd.inc(e.dma_start(x_ch[:, 0:CT, :], src), 16)
                    e.wait_ge(vRes.h, b_vRes + 4 * (c + 1))
                    dst = x_out[c * CN:(c + 1) * CN, :].rearrange(
                        "(t p) f -> p t f", p=128)
                    stq.inc(e.dma_start(dst, x_ch[:, 0:CT, :]), 16)
                e.wait_ge(stq.h, b_st + 16 * CH)

            @b.scalar
            def _(e):
                def stats(c):
                    e.wait_ge(xd.h, b_xd + 16 * (c + 1))
                    last = None
                    for tl in range(CT):
                        e.activation(scr0[:, 0:D], x_ch[:, tl, :], AF.Square,
                                     accum_out=ssq[:, tl:tl + 1])
                        last = e.activation(scr0[:, 0:D], x_ch[:, tl, :],
                                            AF.Copy, accum_out=s1[:, tl:tl + 1])
                    sA.inc(last)
                    e.wait_ge(vM.h, b_vM + c + 1)
                    sQ.inc(e.activation(rstd[:, 0:CT], rstd[:, 0:CT], AF.Sqrt,
                                        bias=p.cc[1e-5][:]))

                def evac_x(t2):
                    for k in range(8):
                        tau = TAU_x[t2] + k
                        e.wait_ge(tp.h, b_tp + tau + 1)
                        se.inc(e.activation(xqT[:, t2 % 2, k, :],
                                            ptr[tau % 2][:], AF.Copy))

                def evac_h(td):
                    idx = 0
                    for ii in ORD:
                        for k in range(IW[ii][1] // 128):
                            tau = TAU_h[td] + idx
                            e.wait_ge(tp.h, b_tp + tau + 1)
                            if idx == 0 and td >= 2:
                                e.wait_ge(mmD.h, b_mmD + td - 1)
                            kappa = IW[ii][0] // 128 + k
                            se.inc(e.activation(hqT[:, td % 2, kappa, :],
                                                ptr[tau % 2][:], AF.Copy))
                            idx += 1

                stats(0)
                evac_x(0)
                for t in range(T):
                    if t > 0 and t % CT == 0:
                        evac_h(t - 1)
                        stats(t // CT)
                        evac_x(t)
                    for pp, ii in enumerate(ORD):
                        ppos = pp + 1
                        wlen = IW[ii][1]
                        e.wait_ge(mmG.h, b_mmG + 3 * t + GRP[ii])
                        if 6 * t + ppos - 2 > 0:
                            e.wait_ge(vH.h, b_vH + 6 * t + ppos - 2)
                        sS.inc(e.activation(sg[:, ppos % 2, 0:wlen],
                                            pb[GATB[ii]][:, 0:wlen], AF.Silu))
                        if pp == 1 and t % CT != 0 and t >= 1:
                            evac_h(t - 1)
                    if t + 1 < T and (t + 1) % CT != 0:
                        evac_x(t + 1)
                evac_h(T - 1)

            @b.vector
            def _(e):
                for c in range(CH):
                    e.wait_ge(sA.h, b_sA + c + 1)
                    p.seq(e, e.tensor_scalar(mean[:, 0:CT], s1[:, 0:CT],
                                             1.0 / D, None, ALU.mult))
                    p.seq(e, e.tensor_tensor(rstd[:, 0:CT], mean[:, 0:CT],
                                             mean[:, 0:CT], ALU.mult))
                    vM.inc(e.scalar_tensor_tensor(
                        rstd[:, 0:CT], ssq[:, 0:CT], 1.0 / D, rstd[:, 0:CT],
                        ALU.mult, ALU.subtract))
                    e.wait_ge(sQ.h, b_sQ + c + 1)
                    vR.inc(e.reciprocal(rstd[:, 0:CT], rstd[:, 0:CT]))
                    e.wait_ge(vR.h, b_vR + c + 1)
                    def do_xq(e, tl2):
                        t2 = c * CT + tl2
                        rc = rstd[:, tl2:tl2 + 1]
                        p.seq(e, e.tensor_scalar(xm[:, 0:D], x_ch[:, tl2, :],
                                                 mean[:, tl2:tl2 + 1], None,
                                                 ALU.subtract))
                        if t2 >= 2:
                            e.wait_ge(tp.h, b_tp + TP_x[t2 - 2])
                        emit_quant(p, e, xm[:, 0:512], xq[:, t2 % 2, 0:512],
                                   512, scr, rstd_col=rc)
                        vx.inc(emit_quant(p, e, xm[:, 512:1024],
                                          xq[:, t2 % 2, 512:1024], 512, scr,
                                          rstd_col=rc))
                    do_xq(e, 0)
                    for tl in range(CT):
                        t = c * CT + tl
                        for kind, ii in VSCHED:
                            if kind == "m":
                                ppos = POS[ii]
                                wlen = IW[ii][1]
                                e.wait_ge(mmG.h, b_mmG + 3 * t + GRP[ii])
                                e.wait_ge(sS.h, b_sS + 6 * t + ppos)
                                vH.inc(e.tensor_tensor(
                                    hbuf[:, ii, 0:wlen],
                                    sg[:, ppos % 2, 0:wlen],
                                    pb[UPB[ii]][:, 0:wlen], ALU.mult))
                                e.wait_ge(vH.h, b_vH + 6 * t + ppos)
                            elif kind == "q":
                                wlen = IW[ii][1]
                                if t >= 1:
                                    e.wait_ge(tp.h, b_tp + TAU_h[t - 1]
                                              + CUMTP[ii])
                                vq.inc(emit_quant(
                                    p, e, hbuf[:, ii, 0:wlen],
                                    hq[:, IW[ii][0]:IW[ii][0] + wlen],
                                    wlen, scr))
                            elif kind == "X":
                                if tl < CT - 1:
                                    do_xq(e, tl + 1)
                            else:
                                if tl >= 1:
                                    e.wait_ge(mmD.h, b_mmD + t)
                                    tlp = (t - 1) % CT
                                    p.seq(e, e.tensor_add(x_ch[:, tlp, 0:512],
                                                          x_ch[:, tlp, 0:512],
                                                          pd[0][:]))
                                    vRes.inc(e.tensor_add(
                                        x_ch[:, tlp, 512:1024],
                                        x_ch[:, tlp, 512:1024], pd[1][:]))
                    t = c * CT + CT - 1
                    e.wait_ge(mmD.h, b_mmD + t + 1)
                    p.seq(e, e.tensor_add(x_ch[:, CT - 1, 0:512],
                                          x_ch[:, CT - 1, 0:512], pd[0][:]))
                    vRes.inc(e.tensor_add(
                        x_ch[:, CT - 1, 512:1024], x_ch[:, CT - 1, 512:1024],
                        pd[1][:]))

            @b.tensor
            def _(e):
                def xtp(t2):
                    for k in range(8):
                        if k == 0:
                            e.wait_ge(vx.h, b_vx + t2 + 1)
                        tau = TAU_x[t2] + k
                        if tau >= 2:
                            e.wait_ge(se.h, b_se + tau - 1)
                        tp.inc(e.transpose(
                            ptr[tau % 2][:],
                            xq[:, t2 % 2, k * 128:(k + 1) * 128],
                            p.ident_bf[:]))

                def htp(td):
                    idx = 0
                    for ii in ORD:
                        e.wait_ge(vq.h, b_vq + 6 * td + VQIDX[ii])
                        for k in range(IW[ii][1] // 128):
                            tau = TAU_h[td] + idx
                            if tau >= 2:
                                e.wait_ge(se.h, b_se + tau - 1)
                            tp.inc(e.transpose(
                                ptr[tau % 2][:],
                                hq[:, IW[ii][0] + k * 128:
                                    IW[ii][0] + (k + 1) * 128],
                                p.ident_bf[:]))
                            idx += 1

                def down(td):
                    e.wait_ge(se.h, b_se + TP_h[td])
                    if td >= 1:
                        e.wait_ge(vRes.h, b_vRes + td)
                    last = None
                    for kc in range(IC):
                        st_, sp_ = (kc == 0), (kc == IC - 1)
                        e.matmul(pd[0][:], hqT[:, td % 2, kc, :],
                                 wdT[:, kc, 0:512], start=st_, stop=sp_)
                        last = e.matmul(pd[1][:], hqT[:, td % 2, kc, :],
                                        wdT[:, kc, 512:1024], start=st_,
                                        stop=sp_)
                    mmD.inc(last)

                xtp(0)
                for t in range(T):
                    if t > 0 and t % CT == 0:
                        htp(t - 1)
                        down(t - 1)
                        xtp(t)
                    for gi, (iiA, iiB) in enumerate(GROUPS):
                        j = 3 * t + gi
                        if gi == 0:
                            e.wait_ge(se.h, b_se + TP_x[t])
                        if j > 0:
                            e.wait_ge(sS.h, b_sS + 2 * j)
                            e.wait_ge(vH.h, b_vH + 2 * j)
                        oA, wA = IW[iiA]
                        oB, wB = IW[iiB]
                        last = None
                        for d in range(DC):
                            st_, sp_ = (d == 0), (d == DC - 1)
                            e.matmul(pb[0][:, 0:wA], xqT[:, t % 2, d, :],
                                     wgT[:, d, oA:oA + wA], start=st_,
                                     stop=sp_)
                            e.matmul(pb[1][:, 0:wB], xqT[:, t % 2, d, :],
                                     wgT[:, d, oB:oB + wB], start=st_,
                                     stop=sp_)
                            e.matmul(pb[2][:, 0:wA], xqT[:, t % 2, d, :],
                                     wuT[:, d, oA:oA + wA], start=st_,
                                     stop=sp_)
                            last = e.matmul(pb[3][:, 0:wB],
                                            xqT[:, t % 2, d, :],
                                            wuT[:, d, oB:oB + wB], start=st_,
                                            stop=sp_)
                        mmG.inc(last)
                        if gi == 0 and t % CT != 0 and t >= 1:
                            htp(t - 1)
                    if t % CT != 0 and t >= 1:
                        down(t - 1)
                    if t + 1 < T and (t + 1) % CT != 0:
                        xtp(t + 1)
                htp(T - 1)
                down(T - 1)


def _build_l0f_program():
    stack = ExitStack()
    nc = bass.Bass("TRN2", target_bir_lowering=False)
    x_in = nc.dram_tensor("x_in", [N_TOK, D], F32, kind="ExternalInput").ap()
    W = {
        "w_gate": nc.dram_tensor("w_gate", [2816, 1024], F32, kind="ExternalInput").ap(),
        "w_up": nc.dram_tensor("w_up", [2816, 1024], F32, kind="ExternalInput").ap(),
        "w_down": nc.dram_tensor("w_down", [1024, 2816], F32, kind="ExternalInput").ap(),
    }
    x_out = nc.dram_tensor("x_out", [N_TOK, D], F32, kind="ExternalOutput").ap()
    p = Prog(nc, stack)
    setup_common(p)
    build_l0_ffn(p, x_in, x_out, W)
    stack.close()
    nc.finalize()
    return nc




def build_l1_attn(p: Prog, x_in, x_out, W):
    nc = p.nc
    H, HD, LAT, ROPE = 16, 64, 128, 32
    CH = 8
    CT = NT // CH
    CN = CT * 128

    ld = p.sem("l1a_ld")
    xd = p.sem("l1a_xd")
    st = p.sem("l1a_st")

    with ExitStack() as sa:
        with nc.Block() as b:
            @b.gpsimd
            def _(e):
                p.wkv = load_weight_bf16(p, "l1_wkv", W["Wkv"], D, LAT, ld, e, sa)
                p.wk1 = p.sbuf("l1_wk", [128, 512], BF16, sa)
                ld.inc(e.dma_start(p.wk1[:], W["Wk"]), 16)
                p.wv1 = p.sbuf("l1_wv", [128, 512], BF16, sa)
                ld.inc(e.dma_start(p.wv1[:], W["Wv"]), 16)
                p.woT = p.sbuf("l1_woT", [128, 6, D], BF16, sa)
                p.qg_col = p.sbuf("l1_qg", [128, 1], F32, sa)
                ld.inc(e.dma_start(p.qg_col[0:64, 0:1], W["q_g"][:, None]), 16)
                ld.inc(e.dma_start(p.qg_col[64:128, 0:1], W["q_g"][:, None]), 16)
                p.kg_col = p.sbuf("l1_kg", [128, 1], F32, sa)
                for m3 in range(3):
                    ld.inc(e.dma_start(p.kg_col[32 * m3:32 * m3 + 32, 0:1],
                                       W["k_g"][:, None]), 16)
                e.memset(p.woT[:], 0.0)
                for cc in range(6):
                    for m in range(3):
                        h = 3 * cc + m
                        if h >= H:
                            break
                        ld.inc(e.dma_start(
                            p.woT[32 * m:32 * m + 32, cc, :],
                            W["Wo"][h * 64:h * 64 + 32, :]), 16)
        with nc.Block() as b:
            @b.vector
            def _(e):
                ld.wait(e)
                e.tensor_scalar(p.kg_col[:], p.kg_col[:], 0.125, None, ALU.mult)

        iop = p.sbuf("l1_iop", [128, 1], F32, sa)
        pairsel = p.sbuf("l1_pairsel", [128, 2], BF16, sa)
        trisel = p.sbuf("l1_trisel", [96, 3], BF16, sa)
        triselT = p.sbuf("l1_triselT", [3, 96], BF16, sa)
        tri_i = p.sbuf("l1_tri_i", [3, 96], F32, sa)
        cmask = p.sbuf("l1_cmask", [128, 4, CN], BF16, sa)
        ones33 = p.sbuf("l1_ones33", [33, 32], BF16, sa)
        sc_const = ExitStack()
        cmask_i = p.sbuf("l1_cmask_i", [128, 4, CN], F32, sc_const)
        scrT = p.sbuf("l1_scrT", [96, 3], F32, sc_const)
        with nc.Block() as b:
            @b.gpsimd
            def _(e):
                e.iota(iop[:], [[1, 1]], base=0, channel_multiplier=1,
                       allow_small_or_imprecise_dtypes=True)
                e.iota(tri_i[:], [[1, 96]], base=0, channel_multiplier=-32,
                       allow_small_or_imprecise_dtypes=True)
                e.iota(cmask_i[:], [[-128, 4], [1, CN]], base=0,
                       channel_multiplier=-1,
                       allow_small_or_imprecise_dtypes=True)
                e.memset(ones33[:], 1.0)
        with nc.Block() as b:
            @b.vector
            def _(e):
                e.tensor_single_scalar(pairsel[:, 0:1], iop[:], 64.0, ALU.is_lt)
                e.tensor_single_scalar(pairsel[:, 1:2], iop[:], 63.0, ALU.is_ge)
                for g in range(3):
                    p.seq(e, e.tensor_scalar(scrT[0:96, g:g + 1], iop[0:96, :],
                                             32.0 * g - 1.0, 32.0 * g + 32.0,
                                             ALU.is_gt, ALU.min))
                    p.seq(e, e.tensor_single_scalar(trisel[0:96, g:g + 1],
                                                    iop[0:96, :],
                                                    32.0 * g + 32.0, ALU.is_lt))
                    p.seq(e, e.tensor_tensor(trisel[0:96, g:g + 1],
                                             trisel[0:96, g:g + 1],
                                             scrT[0:96, g:g + 1], ALU.mult))
                p.seq(e, e.tensor_scalar(triselT[:], tri_i[:], 0.0, 64.0,
                                         ALU.is_ge, ALU.min))
                p.seq(e, e.scalar_tensor_tensor(triselT[:], tri_i[:], 32.0,
                                                triselT[:], ALU.is_lt, ALU.mult))
                e.tensor_single_scalar(cmask[:], cmask_i[:], 0.0, ALU.is_ge)
        sc_const.close()

        x_ch = p.sbuf("l1_x_ch", [128, CT, D], F32, sa)
        xn_t = p.sbuf("l1_xn_t", [128, CT, D], BF16, sa)
        xnT = p.sbuf("l1_xnT", [128, DC, CN], BF16, sa)
        s1 = p.sbuf("l1_s1", [128, CT], F32, sa)
        ssq = p.sbuf("l1_ssq", [128, CT], F32, sa)
        mean = p.sbuf("l1_mean", [128, CT], F32, sa)
        rstd = p.sbuf("l1_rstd", [128, CT], F32, sa)
        scr = p.sbuf("l1_scr", [128, CN], F32, sa)
        stats = (s1, ssq, mean, rstd, scr)
        kvT = p.sbuf("l1_kvT", [128, CN], BF16, sa)
        raw = p.sbuf("l1_raw", [128, DC, CN], BF16, sa)
        sqb = p.sbuf("l1_sqb", [128, DC, CN], BF16, sa)
        rrmsK = p.sbuf("l1_rrmsK", [3, 6, CN], BF16, sa)
        rrmsQ = p.sbuf("l1_rrmsQ", [2, DC, CN], BF16, sa)
        ktT = p.sbuf("l1_ktT", [128, 6, N_TOK], BF16, sa)
        vtok = p.sbuf("l1_vtok", [128, NT, H, 33], BF16, sa)
        qtmp = p.sbuf("l1_qtmp", [128, CN], BF16, sa)
        wcol_a = p.sbuf("l1_wcol", [128, 2, DC, 128], BF16, sa)
        wda = p.sem("l1a_wd")
        qrope = p.sbuf("l1_qrope", [128, 6, CN], BF16, sa)
        Ptile = p.sbuf("l1_Pt", [128, 2, CN], BF16, sa)
        oT = p.sbuf("l1_oT", [128, 6, CN], BF16, sa)
        Drb = p.sbuf("l1_Drb", [33, 6, CN], BF16, sa)

        with nc.Block() as b:
            @b.gpsimd
            def _(e):
                e.memset(vtok[:], 0.0)
                e.memset(oT[:], 0.0)

        with nc.Block() as b:
            @b.vector
            def _(e):
                e.memset(vtok[:, :, :, 32:33], 1.0)

        for c in range(CH):
            with ExitStack() as sp:
                pb = [p.psum(f"x{c}pb{i}", [128, 512], F32, sp) for i in range(4)]
                ptr = [p.psum(f"x{c}ptr{i}", [128, 128], BF16, sp) for i in range(2)]
                rms_ps = p.psum(f"x{c}rms", [4, 512], F32, sp)

                rq = scr
                ln_phase(p, x_in, x_ch, xn_t, xnT, stats, CT, xd, ptr, c)

                def mm_kv(e, bank, c=c):
                    if c == 0:
                        ld.wait(e)
                    last = None
                    for d in range(DC):
                        last = e.matmul(bank[:], p.wkv[:, d, :], xnT[:, d, :],
                                        start=(d == 0), stop=(d == DC - 1))
                    return last

                def ev_kv(e, bank):
                    return e.tensor_copy(kvT[:], bank[:])
                p.pipeline([(mm_kv, [Cons("vector", ev_kv)])], pb[0:1])

                G6 = [(g, 3 * g, min(3, H - 3 * g)) for g in range(6)]
                items = []
                for g, h0, nh3 in G6:
                    R = 32 * nh3
                    def mmfn(e, bank, g=g, h0=h0, R=R):
                        return e.matmul(bank[0:R, :],
                                        p.wk1[:, 32 * h0:32 * h0 + R],
                                        kvT[:], start=True, stop=True)

                    def evfn(e, bank, g=g, R=R):
                        e.activation(sqb[0:R, g, :], bank[0:R, :], AF.Square)
                        return e.activation(raw[0:R, g, :], bank[0:R, :],
                                            AF.Copy)
                    items.append((mmfn, [Cons("scalar", evfn)]))
                p.pipeline(items, pb)
                items = []
                for g, h0, nh3 in G6:
                    R = 32 * nh3
                    def mmfn(e, bank, g=g, R=R, nh3=nh3):
                        return e.matmul(bank[0:nh3, :], trisel[0:R, 0:nh3],
                                        sqb[0:R, g, :], start=True, stop=True)

                    def ev1(e, bank, g=g, nh3=nh3):
                        return e.tensor_scalar(rq[0:nh3, 0:CN], bank[0:nh3, :],
                                               1.0 / 32.0, 1e-6, ALU.mult,
                                               ALU.add)

                    def ev2(e, bank, g=g, nh3=nh3):
                        return e.activation(rq[0:nh3, 0:CN], rq[0:nh3, 0:CN],
                                            AF.Sqrt)

                    def ev3(e, bank, g=g, nh3=nh3):
                        p.seq(e, e.reciprocal(rq[0:nh3, 0:CN], rq[0:nh3, 0:CN]))
                        return e.tensor_copy(rrmsK[0:nh3, g, :], rq[0:nh3, 0:CN])
                    items.append((mmfn, [Cons("vector", ev1),
                                         Cons("scalar", ev2, deps=[(0, 0)]),
                                         Cons("vector", ev3, deps=[(1, 0)],
                                              reads_bank=False)]))
                p.pipeline(items, [rms_ps])
                items = []
                for g, h0, nh3 in G6:
                    R = 32 * nh3
                    def mmfn(e, bank, g=g, nh3=nh3, R=R):
                        return e.matmul(bank[0:R, :], triselT[0:nh3, 0:R],
                                        rrmsK[0:nh3, g, :],
                                        start=True, stop=True)

                    def evfn(e, bank, g=g, c=c, R=R):
                        p.seq(e, e.tensor_scalar(scr[0:R, 0:CN],
                                                 raw[0:R, g, :],
                                                 p.kg_col[0:R, 0:1], None,
                                                 ALU.mult))
                        return e.tensor_tensor(
                            ktT[0:R, g, c * CN:(c + 1) * CN],
                            scr[0:R, 0:CN], bank[0:R, :], ALU.mult)
                    items.append((mmfn, [Cons("vector", evfn)]))
                p.pipeline(items, pb)
                items = []
                for g, h0, nh3 in G6:
                    R = 32 * nh3
                    def mmfn(e, bank, g=g, h0=h0, R=R):
                        return e.matmul(bank[0:R, :],
                                        p.wv1[:, 32 * h0:32 * h0 + R],
                                        kvT[:], start=True, stop=True)

                    def evfn(e, bank, g=g, R=R):
                        return e.tensor_copy(raw[0:R, g, :], bank[0:R, :])
                    items.append((mmfn, [Cons("vector", evfn)]))
                p.pipeline(items, pb)
                items = []
                for g, h0, nh3 in G6:
                    R = 32 * nh3
                    for jt in range(CT):
                        def mmfn(e, bank, g=g, jt=jt, R=R):
                            return e.transpose(bank[:, 0:R],
                                               raw[0:R, g, jt * 128:(jt + 1) * 128],
                                               p.ident_bf[0:R, 0:R])

                        def evfn(e, bank, g=g, h0=h0, nh3=nh3, jt=jt, c=c):
                            last = None
                            for m in range(nh3):
                                last = e.tensor_copy(
                                    vtok[:, c * CT + jt, h0 + m, 0:32],
                                    bank[:, 32 * m:32 * m + 32])
                            return last
                        items.append((mmfn, [Cons("vector", evfn)]))
                p.pipeline(items, ptr)

                with nc.Block() as b:
                    @b.gpsimd
                    def _(e):
                        src = W["Wq"][:, 0:128].rearrange("(c p) f -> p c f",
                                                          p=128)
                        wda.inc(e.dma_start(wcol_a[:, 0, :, :], src), 16)
                items = []
                hold_a = {}
                prefsem_a = p.sem("pl_s1")
                base_wda = wda.n
                for dout in range(DC):
                    def mmfn(e, bank, dout=dout, base_wda=base_wda):
                        if dout == 0:
                            e.wait_ge(wda.h, base_wda)
                        else:
                            e.wait_ge(prefsem_a.h,
                                      hold_a["B"] + 16 * dout)
                        last = None
                        for d in range(DC):
                            last = e.matmul(bank[:],
                                            wcol_a[:, dout % 2, d, :],
                                            xnT[:, d, :],
                                            start=(d == 0), stop=(d == DC - 1))
                        return last

                    def evfn(e, bank, dout=dout):
                        e.activation(sqb[:, dout, :], bank[:], AF.Square)
                        return e.activation(raw[:, dout, :], bank[:], AF.Copy)

                    def pref(e, bank, dout=dout):
                        nxt = dout + 1
                        if nxt >= DC:
                            return e.nop()
                        src = W["Wq"][:, nxt * 128:(nxt + 1) * 128].rearrange(
                            "(c p) f -> p c f", p=128)
                        return e.dma_start(wcol_a[:, nxt % 2, :, :], src)
                    items.append((mmfn, [Cons("scalar", evfn),
                                         Cons("gpsimd", pref, reads_bank=False,
                                              dma=True)]))
                hold_a["B"] = prefsem_a.n
                p.pipeline(items, pb)
                items = []
                for dout in range(DC):
                    def mmfn(e, bank, dout=dout):
                        return e.matmul(bank[0:2, :], pairsel[:],
                                        sqb[:, dout, :], start=True, stop=True)

                    def ev1(e, bank, dout=dout):
                        return e.tensor_scalar(rq[0:2, :], bank[0:2, :],
                                               1.0 / 64.0, 1e-6, ALU.mult,
                                               ALU.add)

                    def ev2(e, bank, dout=dout):
                        return e.activation(rq[0:2, :], rq[0:2, :], AF.Sqrt)

                    def ev3(e, bank, dout=dout):
                        p.seq(e, e.reciprocal(rq[0:2, :], rq[0:2, :]))
                        return e.tensor_copy(rrmsQ[0:2, dout, :], rq[0:2, :])
                    items.append((mmfn, [Cons("vector", ev1),
                                         Cons("scalar", ev2, deps=[(0, 0)]),
                                         Cons("vector", ev3, deps=[(1, 0)],
                                              reads_bank=False)]))
                p.pipeline(items, [rms_ps])
                items = []
                for dout in range(DC):
                    def mmfn(e, bank, dout=dout):
                        return e.matmul(bank[:], p.zsel[:],
                                        rrmsQ[0:2, dout, :],
                                        start=True, stop=True)

                    def evfn(e, bank, dout=dout):
                        p.seq(e, e.tensor_scalar(scr[:, 0:CN], raw[:, dout, :],
                                                 p.qg_col[:, 0:1], None,
                                                 ALU.mult))
                        p.seq(e, e.tensor_tensor(qtmp[:, 0:CN], scr[:, 0:CN],
                                                 bank[:], ALU.mult))
                        h0, h1 = 2 * dout, 2 * dout + 1
                        p.seq(e, e.tensor_copy(
                            qrope[32 * (h0 % 3):32 * (h0 % 3) + 32, h0 // 3, :],
                            qtmp[0:32, 0:CN]))
                        return e.tensor_copy(
                            qrope[32 * (h1 % 3):32 * (h1 % 3) + 32, h1 // 3, :],
                            qtmp[64:96, 0:CN])
                    items.append((mmfn, [Cons("vector", evfn)]))
                p.pipeline(items, pb)

            nj = 4 * c + 4
            with ExitStack() as sp:
                sb = [p.psum(f"a{c}sb{i}", [128, 512], F32, sp) for i in range(2)]
                pv = [p.psum(f"a{c}pv{i}", [33, 512], F32, sp) for i in range(6)]
                hgroups = [list(range(0, 6)), list(range(6, 12)),
                           list(range(12, 16))]
                for hg in hgroups:
                    items = []
                    first = {h: True for h in hg}
                    for j in range(nj):
                        for h in hg:
                            def mmfn(e, bank, j=j, h=h):
                                b3, c3 = 32 * (h % 3), h // 3
                                return e.matmul(
                                    bank[:],
                                    ktT[b3:b3 + 32, c3, j * 128:(j + 1) * 128],
                                    qrope[b3:b3 + 32, c3, :],
                                    start=True, stop=True)

                            def ev_exp(e, bank, idx=len(items)):
                                return e.activation(Ptile[:, idx % 2, :],
                                                    bank[:], AF.Exp)

                            diag = (j >= 4 * c)

                            def ev_msk(e, bank, j=j, c=c, idx=len(items),
                                       diag=diag):
                                if not diag:
                                    return e.engine_nop()
                                return e.tensor_tensor(
                                    Ptile[:, idx % 2, :], Ptile[:, idx % 2, :],
                                    cmask[:, j - 4 * c, :], ALU.mult)
                            items.append((mmfn, [
                                Cons("scalar", ev_exp, deps=[("post", 2)]),
                                Cons("vector", ev_msk, deps=[(0, 0)],
                                     reads_bank=False),
                            ]))

                    def mk_post(items_meta):
                        def post(e, i):
                            j, h = items_meta[i]
                            slot = hg.index(h)
                            stt = first[h]
                            first[h] = False
                            return e.matmul(pv[slot][:],
                                            vtok[:, j, h, :],
                                            Ptile[:, i % 2, :],
                                            start=stt, stop=False)
                        return post
                    meta = [(j, h) for j in range(nj) for h in hg]
                    p.pipeline(items, sb, post=mk_post(meta), post_lag=2,
                               post_wait_streams=[0, 1])
                    with nc.Block() as b:
                        @b.vector
                        def _(e, hg=hg):
                            for si, h in enumerate(hg):
                                p.seq(e, e.tensor_copy(scr[32:33, 0:CN],
                                                       pv[si][32:33, :]))
                                p.seq(e, e.reciprocal(scr[32:33, 0:CN],
                                                      scr[32:33, 0:CN]))
                                p.seq(e, e.tensor_copy(Drb[32:33, si, :],
                                                       scr[32:33, 0:CN]))
                    items = []
                    for si, h in enumerate(hg):
                        def mmfn(e, bank, si=si):
                            return e.matmul(bank[0:32, :], ones33[32:33, :],
                                            Drb[32:33, si, :],
                                            start=True, stop=True)

                        def evfn(e, bank, si=si, h=h):
                            p.seq(e, e.tensor_copy(scr[0:32, 0:CN], bank[0:32, :]))
                            p.seq(e, e.tensor_tensor(scr[0:32, 0:CN],
                                                     scr[0:32, 0:CN],
                                                     pv[si][0:32, :], ALU.mult))
                            return e.tensor_copy(
                                oT[32 * (h % 3):32 * (h % 3) + 32, h // 3, :],
                                scr[0:32, 0:CN])
                        items.append((mmfn, [Cons("vector", evfn)]))
                    p.pipeline(items, sb)
                items = []
                for ti in range(CT):
                    for half in range(2):
                        def mmfn(e, bank, ti=ti, half=half):
                            last = None
                            for cc in range(6):
                                last = e.matmul(
                                    bank[:], oT[:, cc, ti * 128:(ti + 1) * 128],
                                    p.woT[:, cc, half * 512:(half + 1) * 512],
                                    start=(cc == 0), stop=(cc == 5))
                            return last

                        def evfn(e, bank, ti=ti, half=half):
                            sl = slice(half * 512, (half + 1) * 512)
                            return e.tensor_add(x_ch[:, ti, sl], x_ch[:, ti, sl],
                                                bank[:])
                        items.append((mmfn, [Cons("vector", evfn)]))
                p.pipeline(items, sb)
            with nc.Block() as b:
                @b.sync
                def _(e, c=c):
                    dst = x_out[c * CN:(c + 1) * CN, :].rearrange(
                        "(t p) f -> p t f", p=128)
                    st.inc(e.dma_start(dst, x_ch[:, 0:CT, :]), 16)
    with nc.Block() as b:
        @b.sync
        def _(e):
            st.wait(e)


def _build_l1a_program():
    stack = ExitStack()
    nc = bass.Bass("TRN2", target_bir_lowering=False)
    x_in = nc.dram_tensor("x_in", [N_TOK, D], F32, kind="ExternalInput").ap()
    W = {
        "Wq": nc.dram_tensor("Wq", [D, D], F32, kind="ExternalInput").ap(),
        "Wkv": nc.dram_tensor("Wkv", [D, 128], F32, kind="ExternalInput").ap(),
        "Wk": nc.dram_tensor("Wk", [128, 512], F32, kind="ExternalInput").ap(),
        "Wv": nc.dram_tensor("Wv", [128, 512], F32, kind="ExternalInput").ap(),
        "Wo": nc.dram_tensor("Wo", [D, D], F32, kind="ExternalInput").ap(),
        "q_g": nc.dram_tensor("q_g", [64], F32, kind="ExternalInput").ap(),
        "k_g": nc.dram_tensor("k_g", [32], F32, kind="ExternalInput").ap(),
    }
    x_out = nc.dram_tensor("x_out", [N_TOK, D], F32, kind="ExternalOutput").ap()
    p = Prog(nc, stack)
    setup_common(p)
    setup_zsel(p)
    build_l1_attn(p, x_in, x_out, W)
    stack.close()
    nc.finalize()
    return nc



CAPACITY = int(1.2 * 32768 / 8)



def prolog_wT_multi(p, tag, sources):
    nc = p.nc
    ns = len(sources)
    with ExitStack() as stk:
        nat = [p.sbuf(f"{tag}_nat{i}", [128, DC, D], BF16, stk)
               for i in range(2)]
        dstT = [p.sbuf(f"{tag}_dst{i}", [128, DC, D], BF16, stk)
                for i in range(2)]
        ptr = [p.psum(f"{tag}_tp{i}", [128, 128], BF16, stk) for i in range(2)]
        ldw = p.sem(tag + "_ld")
        tpw = p.sem(tag + "_tp")
        evw = p.sem(tag + "_ev")
        otw = p.sem(tag + "_ot")
        b_ld, b_tp, b_ev, b_ot = ldw.n, tpw.n, evw.n, otw.n
        with nc.Block() as b:
            @b.gpsimd
            def _(e):
                for si, (src, dst) in enumerate(sources):
                    if si >= 2:
                        e.wait_ge(tpw.h, b_tp + 64 * (si - 1))
                    s2 = src.rearrange("(c p) d -> p c d", p=128)
                    ldw.inc(e.dma_start(nat[si % 2][:], s2), 16)

            @b.tensor
            def _(e):
                for si in range(ns):
                    e.wait_ge(ldw.h, b_ld + 16 * (si + 1))
                    for rc in range(DC):
                        for dcc in range(DC):
                            tau = si * 64 + rc * DC + dcc
                            if tau >= 2:
                                e.wait_ge(evw.h, b_ev + tau - 1)
                            tpw.inc(e.transpose(
                                ptr[tau % 2][:],
                                nat[si % 2][:, rc, dcc * 128:(dcc + 1) * 128],
                                p.ident_bf[:]))

            @b.scalar
            def _(e):
                for si in range(ns):
                    if si >= 2:
                        e.wait_ge(otw.h, b_ot + 16 * (si - 1))
                    for rc in range(DC):
                        for dcc in range(DC):
                            tau = si * 64 + rc * DC + dcc
                            e.wait_ge(tpw.h, b_tp + tau + 1)
                            evw.inc(e.activation(
                                dstT[si % 2][:, dcc, rc * 128:(rc + 1) * 128],
                                ptr[tau % 2][:], AF.Copy))

            @b.sync
            def _(e):
                for si, (src, dst) in enumerate(sources):
                    e.wait_ge(evw.h, b_ev + 64 * (si + 1))
                    d2 = dst.rearrange("(c p) he -> p c he", p=128)
                    otw.inc(e.dma_start(d2, dstT[si % 2][:]), 16)
                e.wait_ge(otw.h, b_ot + 16 * ns)


def build_l1_ffn(p: Prog, x_in, x_out, W, rank_mask, hid_d, cc_in, cc_out):
    nc = p.nc
    H, DH, E = 16, 64, 8
    CH = 8
    CT = NT // CH
    CN = CT * 128

    ld = p.sem("l1f_ld")
    xd = p.sem("l1f_xd")
    st = p.sem("l1f_st")
    cs = p.sem("l1f_cs")

    with ExitStack() as sa:

        def prolog_wT(name, dram, wT_d):
            with ExitStack() as stk:
                nat = p.sbuf(name + "_nat", [128, DC, D], BF16, stk)
                dstT = p.sbuf(name + "_T", [128, DC, D], BF16, stk)
                with nc.Block() as b:
                    @b.gpsimd
                    def _(e):
                        src = dram.rearrange("h e d -> (h e) d").rearrange(
                            "(c p) d -> p c d", p=128)
                        ld.inc(e.dma_start(nat[:], src), 16)
                items = []
                for rc in range(DC):
                    for dcc in range(DC):
                        def mmfn(e, bank, rc=rc, dcc=dcc):
                            if rc == 0 and dcc == 0:
                                ld.wait(e)
                            return e.transpose(bank[:],
                                               nat[:, rc, dcc * 128:(dcc + 1) * 128],
                                               p.ident_bf[:])

                        def evfn(e, bank, rc=rc, dcc=dcc):
                            return e.tensor_copy(dstT[:, dcc, rc * 128:(rc + 1) * 128],
                                                 bank[:])
                        items.append((mmfn, [Cons("vector", evfn)]))
                with ExitStack() as sp:
                    ptr = [p.psum(f"{name}_tp{i}", [128, 128], BF16, sp)
                           for i in range(2)]
                    p.pipeline(items, ptr)
                with nc.Block() as b:
                    @b.sync
                    def _(e):
                        dst = wT_d.rearrange("(c p) he -> p c he", p=128)
                        xd.inc(e.dma_start(dst, dstT[:]), 16)
            with nc.Block() as b:
                @b.sync
                def _(e):
                    xd.wait(e)

        wqT_d = nc.dram_tensor("l1f_wqT", [D, D], BF16, kind="Internal").ap()
        wkT_d = nc.dram_tensor("l1f_wkT", [D, D], BF16, kind="Internal").ap()
        wvT_d = nc.dram_tensor("l1f_wvT", [D, D], BF16, kind="Internal").ap()
        fl = lambda w: w.rearrange("h e d -> (h e) d")
        prolog_wT_multi(p, "f_wt3", [(fl(W["Wq"]), wqT_d),
                                     (fl(W["Wk"]), wkT_d),
                                     (fl(W["Wv"]), wvT_d)])

        with nc.Block() as b:
            @b.gpsimd
            def _(e):
                p.fbq = load_bias_col_f32(p, "f_wq_b",
                                          W["bq"].rearrange("h e -> (h e)"),
                                          D, ld, e, sa)
                p.fbk = load_bias_col_f32(p, "f_wk_b",
                                          W["bk"].rearrange("h e -> (h e)"),
                                          D, ld, e, sa)
                p.fbv = load_bias_col_f32(p, "f_wv_b",
                                          W["bv"].rearrange("h e -> (h e)"),
                                          D, ld, e, sa)
        with nc.Block() as b:
            @b.gpsimd
            def _(e):
                p.wswT = p.sbuf("f_wsw", [128, DC, E], BF16, sa)
                p.bsw_row = load_bias_row_bf16(p, "f_bsw", W["bsw"], E, ld, e, sa)
                p.rkmask = p.sbuf("f_rkm", [128, 1], F32, sa)
                ld.inc(e.dma_start(p.rkmask[:], rank_mask), 16)
        s_rows = ExitStack()
        with nc.Block() as b:
            @b.gpsimd
            def _(e):
                p.wsw_nat = p.sbuf("f_wswn", [E, D], BF16, s_rows)
                ld.inc(e.dma_start(p.wsw_nat[:], W["Wsw"]), 16)
        with ExitStack() as sp:
            ptrw = [p.psum(f"f_wswt{i}", [128, 128], BF16, sp) for i in range(2)]
            items = []
            for ccc in range(DC):
                def mmfn(e, bank, ccc=ccc):
                    if ccc == 0:
                        ld.wait(e)
                    return e.transpose(bank[0:128, 0:E],
                                       p.wsw_nat[0:E, ccc * 128:(ccc + 1) * 128],
                                       p.ident_bf[0:E, 0:E])

                def evfn(e, bank, ccc=ccc):
                    return e.tensor_copy(p.wswT[:, ccc, :], bank[:, 0:E])
                items.append((mmfn, [Cons("vector", evfn)]))
            p.pipeline(items, ptrw)
        with nc.Block() as b:
            @b.vector
            def _(e):
                ld.wait(e)
                e.tensor_scalar(p.fbk[:], p.fbk[:], 1.0 / 32.0, None, ALU.mult)
        s_rows.close()

        x_ch = p.sbuf("f_x_ch", [128, CT, D], F32, sa)
        xn_t = p.sbuf("f_xn_t", [128, CT, D], BF16, sa)
        xnT = p.sbuf("f_xnT", [128, DC, CN], BF16, sa)
        s1 = p.sbuf("f_s1", [128, CT], F32, sa)
        ssq = p.sbuf("f_ssq", [128, CT], F32, sa)
        mean = p.sbuf("f_mean", [128, CT], F32, sa)
        rstd = p.sbuf("f_rstd", [128, CT], F32, sa)
        scr = p.sbuf("f_scr", [128, CN], F32, sa)
        stats = (s1, ssq, mean, rstd, scr)
        umask = p.sbuf("f_umask", [128, 4, CN], BF16, sa)
        with nc.Block() as b:
            @b.vector
            def _(e):
                e.tensor_scalar(umask[:], p.cmask_l1f[:] if False else umask[:],
                                0.0, None, ALU.mult) if False else None
                e.scalar_tensor_tensor(umask[:], p.l1f_cmask[:], -1.0,
                                       p.cc[1e-5][:].broadcast_to([128, 1])
                                       if False else None, ALU.mult, ALU.add) \
                    if False else None

        with ExitStack() as s_att:
            ktT = p.sbuf("f_ktT", [128, DC, N_TOK], BF16, s_att)
            vtok = p.sbuf("f_vtok", [128, NT, H, DH + 1], BF16, s_att)
            qT2 = p.sbuf("f_qT2", [128, DC, CN], BF16, s_att)
            Ptile = p.sbuf("f_Pt", [128, 2, CN], BF16, s_att)
            hT_blk = p.sbuf("f_hT", [128, DC, CN], BF16, s_att)
            vsuf = p.sbuf("f_vsuf", [1, H, DH + 1], BF16, s_att)
            Drb = p.sbuf("f_Drb", [DH + 1, 6, CN], BF16, s_att)
            ones65 = p.sbuf("f_ones65", [DH + 1, DH], BF16, s_att)
            cmask = p.sbuf("f_cmask", [128, 4, CN], BF16, s_att)
            with nc.Block() as b:
                @b.gpsimd
                def _(e):
                    e.memset(vtok[:], 0.0)
                    e.memset(ones65[:], 1.0)
                    e.iota(scr.bitcast(F32)[:, 0:D] if False else scr[:, 0:D],
                           [[1, 1024]], base=0, channel_multiplier=0,
                           allow_small_or_imprecise_dtypes=True) \
                        if False else None
            with nc.Block() as b:
                @b.gpsimd
                def _(e):
                    e.iota(cmask[:], [[-128, 4], [1, CN]], base=0,
                           channel_multiplier=-1,
                           allow_small_or_imprecise_dtypes=True)
            with nc.Block() as b:
                @b.vector
                def _(e):
                    p.seq(e, e.tensor_single_scalar(umask[:], cmask[:], 0.0,
                                                    ALU.is_lt))
                    e.tensor_single_scalar(cmask[:], cmask[:], 0.0, ALU.is_ge)
            with nc.Block() as b:
                @b.vector
                def _(e):
                    e.memset(vtok[:, :, :, DH:DH + 1], 1.0)

            wcol = p.sbuf("f_wcol", [128, 2, DC, 128], BF16, s_att)
            wdsem = p.sem("l1f_wd")

            def stream_proj(wT_d, xnT, evgen, pb, tag):
                with nc.Block() as b:
                    @b.sync
                    def _(e):
                        src = wT_d[:, 0:128].rearrange("(c p) he -> p c he", p=128)
                        wdsem.inc(e.dma_start(wcol[:, 0, :, :], src), 16)
                items = []
                base0 = wdsem.n
                hold = {}
                prefsem = p.sem("pl_s1")
                for dout in range(DC):
                    def mmfn(e, bank, dout=dout):
                        if dout == 0:
                            e.wait_ge(wdsem.h, base0)
                        else:
                            e.wait_ge(prefsem.h, hold["B"] + 16 * dout)
                        last = None
                        for d in range(DC):
                            last = e.matmul(
                                bank[:], wcol[:, dout % 2, d, :],
                                xnT[:, d, :], start=(d == 0), stop=(d == DC - 1))
                        return last

                    def pref(e, bank, dout=dout):
                        nxt = dout + 1
                        if nxt >= DC:
                            return e.nop()
                        src = wT_d[:, nxt * 128:(nxt + 1) * 128].rearrange(
                            "(c p) he -> p c he", p=128)
                        return e.dma_start(wcol[:, nxt % 2, :, :], src)
                    items.append((mmfn, [evgen(dout),
                                         Cons("sync", pref, reads_bank=False,
                                              dma=True)]))
                hold["B"] = prefsem.n
                p.pipeline(items, pb)

            for c in range(CH):
                with ExitStack() as sp:
                    pb = [p.psum(f"fa{c}p{i}", [128, 512], F32, sp) for i in range(4)]
                    ptr = [p.psum(f"fa{c}t{i}", [128, 128], BF16, sp) for i in range(2)]
                    ln_phase(p, x_in, x_ch, xn_t, xnT, stats, CT, xd, ptr, c)
                    def evgen_k(dout, c=c):
                        def evfn(e, bank, dout=dout, c=c):
                            return e.tensor_scalar(
                                ktT[:, dout, c * CN:(c + 1) * CN], bank[:],
                                p.fbk[:, dout:dout + 1], 1.0 / 32.0,
                                ALU.add, ALU.mult)
                        return Cons("vector", evfn)
                    stream_proj(wkT_d, xnT, evgen_k, pb, "k")
                    def evgen_v(dout):
                        def evfn(e, bank, dout=dout):
                            return e.tensor_scalar(hT_blk[:, dout, :], bank[:],
                                                   p.fbv[:, dout:dout + 1],
                                                   None, ALU.add)
                        return Cons("vector", evfn)
                    stream_proj(wvT_d, xnT, evgen_v, pb, "v")
                    items = []
                    for dout in range(DC):
                        for jt in range(CT):
                            def mmfn(e, bank, dout=dout, jt=jt):
                                return e.transpose(
                                    bank[:], hT_blk[:, dout, jt * 128:(jt + 1) * 128],
                                    p.ident_bf[:])

                            def evfn(e, bank, dout=dout, jt=jt, c=c):
                                h0, h1 = 2 * dout, 2 * dout + 1
                                e.tensor_copy(vtok[:, c * CT + jt, h0, 0:DH],
                                              bank[:, 0:64])
                                return e.tensor_copy(vtok[:, c * CT + jt, h1, 0:DH],
                                                     bank[:, 64:128])
                            items.append((mmfn, [Cons("vector", evfn)]))
                    p.pipeline(items, ptr)

            for c in range(CH):
                nj = 4 * c + 4
                with ExitStack() as sp:
                    pb = [p.psum(f"fb{c}p{i}", [128, 512], F32, sp) for i in range(2)]
                    ptr = [p.psum(f"fb{c}t{i}", [128, 128], BF16, sp) for i in range(2)]
                    vs_ps = p.psum(f"fb{c}vs", [1, DH + 1], F32, sp)
                    ln_phase(p, x_in, x_ch, xn_t, xnT, stats, CT, xd, ptr, c)
                    def evgen_q(dout):
                        def evfn(e, bank, dout=dout):
                            return e.tensor_scalar(qT2[:, dout, :], bank[:],
                                                   p.fbq[:, dout:dout + 1],
                                                   None, ALU.add)
                        return Cons("vector", evfn)
                    stream_proj(wqT_d, xnT, evgen_q, pb, "q")
                    sufj = list(range(4 * (c + 1), NT))
                    if sufj:
                        items = []
                        for h in range(H):
                            def mmfn(e, bank, h=h):
                                last = None
                                for ji, j in enumerate(sufj):
                                    last = e.matmul(bank[0:1, :],
                                                    p.ones_col_bf[:],
                                                    vtok[:, j, h, :],
                                                    start=(ji == 0),
                                                    stop=(ji == len(sufj) - 1))
                                return last

                            def evfn(e, bank, h=h):
                                return e.tensor_copy(vsuf[0:1, h, :], bank[0:1, :])
                            items.append((mmfn, [Cons("vector", evfn)]))
                        p.pipeline(items, [vs_ps])
                    else:
                        with nc.Block() as b:
                            @b.vector
                            def _(e):
                                e.memset(vsuf[:], 0.0)
                with ExitStack() as sp:
                    sb = [p.psum(f"fb{c}s{i}", [128, 512], F32, sp) for i in range(2)]
                    pv = [p.psum(f"fb{c}v{i}", [DH + 1, 512], F32, sp)
                          for i in range(6)]
                    hgroups = [list(range(0, 6)), list(range(6, 12)),
                               list(range(12, 16))]
                    for hg in hgroups:
                        items = []
                        first = {h: True for h in hg}
                        for j in range(nj):
                            for h in hg:
                                def mmfn(e, bank, j=j, h=h):
                                    b2, c2 = 64 * (h % 2), h // 2
                                    return e.matmul(
                                        bank[:],
                                        ktT[b2:b2 + 64, c2, j * 128:(j + 1) * 128],
                                        qT2[b2:b2 + 64, c2, :],
                                        start=True, stop=True)

                                def ev_exp(e, bank, idx=len(items)):
                                    return e.activation(Ptile[:, idx % 2, :],
                                                        bank[:], AF.Exp)

                                diag = (j >= 4 * c)

                                def ev_msk(e, bank, j=j, c=c, idx=len(items),
                                           diag=diag):
                                    if not diag:
                                        return e.engine_nop()
                                    return e.tensor_tensor(
                                        Ptile[:, idx % 2, :], Ptile[:, idx % 2, :],
                                        cmask[:, j - 4 * c, :], ALU.mult)
                                items.append((mmfn, [
                                    Cons("scalar", ev_exp, deps=[("post", 2)]),
                                    Cons("vector", ev_msk, deps=[(0, 0)],
                                         reads_bank=False),
                                ]))

                        def mk_post(meta, hg=hg, first=first):
                            def post(e, i):
                                j, h = meta[i]
                                slot = hg.index(h)
                                stt_ = first[h]
                                first[h] = False
                                return e.matmul(pv[slot][:],
                                                vtok[:, j, h, :],
                                                Ptile[:, i % 2, :],
                                                start=stt_, stop=False)
                            return post
                        meta = [(j, h) for j in range(nj) for h in hg]
                        p.pipeline(items, sb, post=mk_post(meta), post_lag=2,
                                   post_wait_streams=[0, 1])
                        items = []
                        for si, h in enumerate(hg):
                            def mmfn(e, bank, si=si, h=h, c=c):
                                for jo in range(4):
                                    e.matmul(bank[:], vtok[:, 4 * c + jo, h, :],
                                             umask[:, jo, :],
                                             start=False, stop=False)
                                return e.matmul(bank[:],
                                                vsuf[0:1, h, :],
                                                p.ones_row_bf[0:1, 0:CN],
                                                start=False, stop=True)

                            def evfn(e, bank, si=si):
                                p.seq(e, e.tensor_copy(scr[64:65, 0:CN],
                                                       bank[64:65, :]))
                                p.seq(e, e.reciprocal(scr[64:65, 0:CN],
                                                      scr[64:65, 0:CN]))
                                return e.tensor_copy(Drb[64:65, si, :],
                                                     scr[64:65, 0:CN])
                            items.append((mmfn, [Cons("vector", evfn)]))
                        p.pipeline(items, [pv[si] for si in range(len(hg))])
                        items = []
                        for si, h in enumerate(hg):
                            def mmfn(e, bank, si=si):
                                return e.matmul(bank[0:64, :], ones65[64:65, :],
                                                Drb[64:65, si, :],
                                                start=True, stop=True)

                            def evfn(e, bank, si=si, h=h):
                                p.seq(e, e.tensor_copy(scr[0:64, 0:CN],
                                                       bank[0:64, :]))
                                p.seq(e, e.tensor_tensor(scr[0:64, 0:CN],
                                                         scr[0:64, 0:CN],
                                                         pv[si][0:64, :],
                                                         ALU.mult))
                                return e.tensor_copy(
                                    hT_blk[64 * (h % 2):64 * (h % 2) + 64,
                                           h // 2, :],
                                    scr[0:64, 0:CN])
                            items.append((mmfn, [Cons("vector", evfn)]))
                        p.pipeline(items, sb)
                with nc.Block() as b:
                    @b.sync
                    def _(e, c=c):
                        dst = hid_d[:, c * CN:(c + 1) * CN].rearrange(
                            "(c2 p) n -> p c2 n", p=128)
                        st.inc(e.dma_start(dst, hT_blk[:]), 16)

        ohf = p.sbuf("f_ohf", [128, NT, E], BF16, sa)
        ohf32 = p.sbuf("f_ohf32", [128, NT, E], F32, sa)
        rp_all = p.sbuf("f_rp", [128, NT], F32, sa)
        gloc = p.sbuf("f_gloc", [128, NT], F32, sa)
        Rsb = p.sbuf("f_Rsb", [1, E], F32, sa)
        offrow = p.sbuf("f_off", [1, E], F32, sa)
        cntp = p.sbuf("f_cntp", [16, 128], F32, sa)
        cnt_all = p.sbuf("f_cnta", [128, 128], F32, sa)
        hidT_c = p.sbuf("f_hidc", [128, DC, CN], BF16, sa)
        m_col = p.sbuf("f_mcol", [128, 1], F32, sa)
        negm = p.sbuf("f_negm", [128, 1], F32, sa)
        se_col = p.sbuf("f_se", [128, 1], F32, sa)
        oh0 = p.sbuf("f_oh0", [128, E], BF16, sa)
        t2b = p.sbuf("f_t2b", [128, E], F32, sa)
        zer8 = p.sbuf("f_zer8", [128, E], F32, sa)
        scr8 = p.sbuf("f_scr8", [128, E], F32, sa)
        scr8b = p.sbuf("f_scr8b", [128, E], F32, sa)
        with nc.Block() as b:
            @b.gpsimd
            def _(e):
                e.memset(zer8[:], 0.0)
                e.memset(Rsb[:], 0.0)
                e.memset(cntp[:], 0.0)
        with ExitStack() as sp:
            lg_ps = p.psum("fc_lg", [128, E], F32, sp)
            cs_ps = p.psum("fc_cs", [128, E], F32, sp)
            R_ps = p.psum("fc_R", [1, E], F32, sp)
            off_ps = p.psum("fc_off", [1, E], F32, sp)
            pe_s = p.sem("fc_pe")
            v_s = p.sem("fc_v")
            sc_s = p.sem("fc_sc")
            for c in range(CH):
                with nc.Block() as b:
                    @b.sync
                    def _(e, c=c):
                        src = hid_d[:, c * CN:(c + 1) * CN].rearrange(
                            "(c2 p) n -> p c2 n", p=128)
                        xd.inc(e.dma_start(hidT_c[:], src), 16)
                base_v_pre = v_s.n
                base_sc_pre = sc_s.n
                with nc.Block() as b:
                    gt0 = c * CT

                    @b.tensor
                    def _(e, c=c, gt0=gt0):
                        xd.wait(e)
                        for t in range(CT):
                            gt = gt0 + t
                            if gt > 0:
                                e.wait_ge(sc_s.h, gt)
                            for d in range(DC):
                                e.matmul(lg_ps[:], hidT_c[:, d, t * 128:(t + 1) * 128],
                                         p.wswT[:, d, :], start=(d == 0),
                                         stop=False)
                            e.matmul(lg_ps[:], p.ones_row_bf[0:1, 0:128],
                                     p.bsw_row[0:1, :], start=False, stop=True) \
                                .then_inc(pe_s.h, 1)
                            pe_s.n += 1
                            e.wait_ge(v_s.h, base_v_pre + 2 * t + 1)
                            e.matmul(cs_ps[:], cmask[:, 0, 0:128],
                                     ohf[:, gt, :], start=True, stop=False)
                            e.matmul(cs_ps[:], p.ones_row_f32[0:1, 0:128],
                                     Rsb[0:1, :], start=False, stop=True) \
                                .then_inc(pe_s.h, 1)
                            pe_s.n += 1
                            e.matmul(R_ps[:], p.ones_col_bf[:], ohf[:, gt, :],
                                     start=(gt == 0), stop=(gt == NT - 1)) \
                                .then_inc(pe_s.h, 1)
                            pe_s.n += 1

                    @b.vector
                    def _(e, c=c, gt0=gt0):
                        base_pe = pe_s.n - 3 * CT
                        base_sc = base_sc_pre
                        for t in range(CT):
                            gt = gt0 + t
                            e.wait_ge(pe_s.h, base_pe + 3 * t + 1)
                            p.seq(e, e.reduce_max(m_col[:], lg_ps[:], AX.X))
                            p.seq(e, e.tensor_scalar(negm[:], m_col[:], -1.0,
                                                     None, ALU.mult))
                            p.seq(e, e.tensor_scalar(oh0[:], lg_ps[:], m_col[:, 0:1],
                                                     None, ALU.is_equal))
                            p.seq(e, e.tensor_tensor_scan(t2b[:], oh0[:], zer8[:],
                                                          0.0, ALU.add, ALU.add))
                            p.seq(e, e.tensor_single_scalar(t2b[:], t2b[:], 1.0,
                                                            ALU.is_equal))
                            v_s.inc(e.tensor_tensor(ohf[:, gt, :], oh0[:], t2b[:],
                                                    ALU.mult))
                            e.wait_ge(v_s.h, v_s.n)
                            p.seq(e, e.tensor_copy(ohf32[:, gt, :], ohf[:, gt, :]))
                            e.wait_ge(sc_s.h, base_sc + t + 1)
                            p.seq(e, e.reciprocal(rp_all[:, gt:gt + 1], se_col[:]))
                            e.wait_ge(pe_s.h, base_pe + 3 * t + 2)
                            p.seq(e, e.tensor_tensor(scr8[:], cs_ps[:],
                                                     ohf32[:, gt, :], ALU.mult))
                            p.seq(e, e.reduce_sum(gloc[:, gt:gt + 1], scr8[:],
                                                  AX.X))
                            e.wait_ge(pe_s.h, base_pe + 3 * t + 3)
                            v_s.inc(e.tensor_copy(Rsb[0:1, :], R_ps[0:1, :]))

                    @b.scalar
                    def _(e, c=c, gt0=gt0):
                        base_v = v_s.n - 2 * CT
                        for t in range(CT):
                            e.wait_ge(v_s.h, base_v + 2 * t + 1)
                            sc_s.inc(e.activation(scr8b[:], lg_ps[:], AF.Exp,
                                                  bias=negm[:, 0:1],
                                                  accum_out=se_col[:]))
                            sc_s.n += 0
            with nc.Block() as b:
                @b.vector
                def _(e):
                    e.tensor_copy(cntp[0:1, 0:E], R_ps[0:1, :])
            with nc.Block() as b:
                @b.sync
                def _(e):
                    xd.inc(e.dma_start(cc_in, cntp[:]), 16)
            with nc.Block() as b:
                @b.gpsimd
                def _(e):
                    xd.wait(e)
                    cs.inc(e.collective_compute(
                        "AllGather", ALU.bypass, [[0, 1, 2, 3, 4, 5, 6, 7]],
                        [cc_in.opt()], [cc_out.opt()]), 1)
            with nc.Block() as b:
                @b.sync
                def _(e):
                    cs.wait(e)
                    xd.inc(e.dma_start(cnt_all[:], cc_out), 16)
            with nc.Block() as b:
                @b.tensor
                def _(e):
                    xd.wait(e)
                    e.matmul(off_ps[:], p.rkmask[:], cnt_all[:, 0:E],
                             start=True, stop=True)
            with nc.Block() as b:
                @b.vector
                def _(e):
                    e.tensor_copy(offrow[0:1, :], off_ps[0:1, :])

        p.dbg_dump("ohf32", ohf32[:], [128, NT, E])
        p.dbg_dump("gloc", gloc[:], [128, NT])
        p.dbg_dump("off", offrow[:], [1, E])
        with ExitStack() as sd:
            weT_d = nc.dram_tensor("l1f_weT", [E * D, D], BF16, kind="Internal").ap()
            prolog_wT_multi(p, "f_wte",
                            [(W["We"][ei], weT_d[ei * D:(ei + 1) * D, :])
                             for ei in range(E)])

            acc = p.sbuf("f_acc", [128, CT, D], F32, sd)
            scrd = p.sbuf("f_scrd", [128, D], F32, sd)
            gam_b = p.sbuf("f_gam_b", [128, D], BF16, sd)
            bet_b = p.sbuf("f_bet_b", [128, D], BF16, sd)
            s_gb = ExitStack()
            with nc.Block() as b:
                @b.gpsimd
                def _(e):
                    p.be_sb = p.sbuf("f_be", [8, D], BF16, sd)
                    ld.inc(e.dma_start(p.be_sb[:], W["be"]), 16)
                    p.gam_row = load_bias_row_bf16(p, "f_gam", W["gamma"], D,
                                                   ld, e, s_gb)
                    p.bet_row = load_bias_row_bf16(p, "f_bet", W["beta"], D,
                                                   ld, e, s_gb)
            with ExitStack() as sp:
                gb_ps = p.psum("f_gbps", [128, 512], F32, sp)
                items = []
                for which in range(4):
                    def mmfn(e, bank, which=which):
                        if which == 0:
                            ld.wait(e)
                        row = p.gam_row if which < 2 else p.bet_row
                        return e.matmul(bank[:], p.ones_row_bf[0:1, 0:128],
                                        row[0:1, (which % 2) * 512:(which % 2 + 1) * 512],
                                        start=True, stop=True)

                    def evfn(e, bank, which=which):
                        dst = gam_b if which < 2 else bet_b
                        return e.tensor_copy(
                            dst[:, (which % 2) * 512:(which % 2 + 1) * 512],
                            bank[:])
                    items.append((mmfn, [Cons("vector", evfn)]))
                p.pipeline(items, [gb_ps])
            s_gb.close()

            hid_tok = p.sbuf("f_hidtok", [128, CT, D], BF16, sd)
            kept = p.sbuf("f_kept", [128, CT], F32, sd)
            kinv = p.sbuf("f_kinv", [128, CT], F32, sd)
            ohfT = p.sbuf("f_ohfT", [8, CT, 128], BF16, sd)
            m2 = p.sbuf("f_m2", [128, CT], F32, sd)
            r2 = p.sbuf("f_r2", [128, CT], F32, sd)
            web = p.sbuf("f_web", [128, 2, DC, D], BF16, sd)
            wes = p.sem("l1f_we")

            for c in range(CH):
                with ExitStack() as sp:
                    pb = [p.psum(f"fd{c}p{i}", [128, 512], F32, sp) for i in range(4)]
                    ptr = [p.psum(f"fd{c}t{i}", [128, 128], BF16, sp) for i in range(2)]
                    ob_ps = p.psum(f"fd{c}ob", [128, E], F32, sp)
                    with nc.Block() as b:
                        @b.sync
                        def _(e, c=c):
                            src = hid_d[:, c * CN:(c + 1) * CN].rearrange(
                                "(c2 p) n -> p c2 n", p=128)
                            xd.inc(e.dma_start(hidT_c[:], src), 16)
                            src0 = weT_d[0:D, :].rearrange("(cp p) f -> p cp f",
                                                           p=128)
                            wes.inc(e.dma_start(web[:, 0, :, :], src0), 16)
                    ln_phase(p, x_in, x_ch, xn_t, xnT, stats, CT, xd, ptr, c)
                    with nc.Block() as b:
                        @b.tensor
                        def _(e, c=c):
                            e.matmul(ob_ps[:], p.ones_row_f32[0:1, 0:128],
                                     offrow[0:1, :], start=True, stop=True)
                    with nc.Block() as b:
                        @b.vector
                        def _(e, c=c):
                            for t in range(CT):
                                gt = c * CT + t
                                p.seq(e, e.tensor_tensor(scr8[:], ob_ps[:],
                                                         ohf32[:, gt, :],
                                                         ALU.mult))
                                p.seq(e, e.reduce_sum(kept[:, t:t + 1], scr8[:],
                                                      AX.X))
                                p.seq(e, e.scalar_tensor_tensor(
                                    kept[:, t:t + 1], gloc[:, gt:gt + 1], -1.0,
                                    kept[:, t:t + 1], ALU.add, ALU.add))
                                p.seq(e, e.tensor_single_scalar(
                                    kept[:, t:t + 1], kept[:, t:t + 1],
                                    CAPACITY - 0.5, ALU.is_lt))
                                p.seq(e, e.tensor_scalar(
                                    kinv[:, t:t + 1], kept[:, t:t + 1], -1.0, 1.0,
                                    ALU.mult, ALU.add))
                    items = []
                    for t in range(CT):
                        for d in range(DC):
                            def mmfn(e, bank, t=t, d=d):
                                return e.transpose(
                                    bank[:], hidT_c[:, d, t * 128:(t + 1) * 128],
                                    p.ident_bf[:])

                            def evfn(e, bank, t=t, d=d):
                                return e.tensor_copy(
                                    hid_tok[:, t, d * 128:(d + 1) * 128], bank[:])
                            items.append((mmfn, [Cons("vector", evfn)]))
                    p.pipeline(items, ptr)
                    items = []
                    for t in range(CT):
                        def mm_oht(e, bank, t=t, c=c):
                            return e.transpose(bank[0:E, 0:128],
                                               ohf[:, c * CT + t, :],
                                               p.ident_bf[:])

                        def ev_oht(e, bank, t=t):
                            return e.tensor_copy(ohfT[0:E, t, :], bank[0:E, 0:128])
                        items.append((mm_oht, [Cons("vector", ev_oht)]))
                    p.pipeline(items, ptr)
                    items = []
                    hold_e = {}
                    prefsem = p.sem("pl_s1")
                    for ei in range(E):
                        for t in range(CT):
                            def mmfn(e, bg, t=t, ei=ei):
                                if ei == 0:
                                    e.wait_ge(wes.h, hold_e["W"])
                                else:
                                    e.wait_ge(prefsem.h,
                                              hold_e["B"] + 16 * ((ei - 1) * CT + 1))
                                last = None
                                for half in range(2):
                                    for d in range(DC):
                                        last = e.matmul(
                                            bg[half][:],
                                            hidT_c[:, d, t * 128:(t + 1) * 128],
                                            web[:, ei % 2, d,
                                                half * 512:(half + 1) * 512],
                                            start=(d == 0), stop=(d == DC - 1))
                                return last

                            def evfn(e, bg, t=t, ei=ei, c=c):
                                gt = c * CT + t
                                last = None
                                for half in range(2):
                                    sl = slice(half * 512, (half + 1) * 512)
                                    if ei == 0:
                                        last = e.tensor_scalar(
                                            acc[:, t, sl], bg[half][:],
                                            ohf32[:, gt, ei:ei + 1], None,
                                            ALU.mult)
                                    else:
                                        last = e.scalar_tensor_tensor(
                                            acc[:, t, sl], bg[half][:],
                                            ohf32[:, gt, ei:ei + 1],
                                            acc[:, t, sl], ALU.mult, ALU.add)
                                    if half == 0:
                                        p.seq(e, last)
                                return last

                            def pref(e, bg, t=t, ei=ei):
                                if t != 0 or ei >= E - 1:
                                    return e.nop()
                                src = weT_d[(ei + 1) * D:(ei + 2) * D, :].rearrange(
                                    "(cp p) f -> p cp f", p=128)
                                return e.dma_start(web[:, (ei + 1) % 2, :, :], src)
                            items.append((mmfn, [Cons("vector", evfn),
                                                 Cons("sync", pref,
                                                      reads_bank=False,
                                                      dma=True)]))
                    hold_e["B"] = prefsem.n
                    hold_e["W"] = wes.n
                    p.pipeline(items, [(pb[0], pb[1]), (pb[2], pb[3])])
                    items = []
                    for t in range(CT):
                        def mm_be(e, bg, t=t):
                            e.matmul(bg[0][:], ohfT[0:E, t, :], p.be_sb[:, 0:512],
                                     start=True, stop=True)
                            return e.matmul(bg[1][:], ohfT[0:E, t, :],
                                            p.be_sb[:, 512:1024],
                                            start=True, stop=True)

                        def ev_be(e, bg, t=t, c=c):
                            gt = c * CT + t
                            for half in range(2):
                                sl = slice(half * 512, (half + 1) * 512)
                                p.seq(e, e.tensor_add(acc[:, t, sl], acc[:, t, sl],
                                                      bg[half][:]))
                            p.seq(e, e.tensor_scalar(acc[:, t, :], acc[:, t, :],
                                                     kept[:, t:t + 1], None,
                                                     ALU.mult))
                            p.seq(e, e.scalar_tensor_tensor(
                                scrd[:, 0:D], hid_tok[:, t, :],
                                kinv[:, t:t + 1], acc[:, t, :],
                                ALU.mult, ALU.add))
                            p.seq(e, e.tensor_scalar(scrd[:, 0:D], scrd[:, 0:D],
                                                     rp_all[:, gt:gt + 1],
                                                     None, ALU.mult))
                            p.seq(e, e.tensor_scalar(acc[:, t, :], x_ch[:, t, :],
                                                     mean[:, t:t + 1],
                                                     rstd[:, t:t + 1],
                                                     ALU.subtract, ALU.mult))
                            return e.tensor_add(acc[:, t, :], acc[:, t, :],
                                                scrd[:, 0:D])
                        items.append((mm_be, [Cons("vector", ev_be)]))
                    p.pipeline(items, [(pb[0], pb[1]), (pb[2], pb[3])])
                    with nc.Block() as b:
                        @b.scalar
                        def _(e):
                            for t in range(CT):
                                e.activation(scrd[:, 0:D], acc[:, t, :], AF.Square,
                                             accum_out=ssq[:, t:t + 1])
                                e.activation(scrd[:, 0:D], acc[:, t, :], AF.Copy,
                                             accum_out=s1[:, t:t + 1])
                    with nc.Block() as b:
                        @b.vector
                        def _(e):
                            p.seq(e, e.tensor_scalar(m2[:, 0:CT], s1[:, 0:CT],
                                                     1.0 / D, None, ALU.mult))
                            p.seq(e, e.tensor_tensor(r2[:, 0:CT], m2[:, 0:CT],
                                                     m2[:, 0:CT], ALU.mult))
                            e.scalar_tensor_tensor(r2[:, 0:CT], ssq[:, 0:CT],
                                                   1.0 / D, r2[:, 0:CT],
                                                   ALU.mult, ALU.subtract)
                    with nc.Block() as b:
                        @b.scalar
                        def _(e):
                            e.activation(r2[:, 0:CT], r2[:, 0:CT], AF.Sqrt,
                                         bias=p.cc[1e-5][:])
                    with nc.Block() as b:
                        @b.vector
                        def _(e):
                            e.reciprocal(r2[:, 0:CT], r2[:, 0:CT])
                    with nc.Block() as b:
                        @b.vector
                        def _(e, c=c):
                            for t in range(CT):
                                p.seq(e, e.tensor_scalar(scrd[:, 0:D], acc[:, t, :],
                                                         m2[:, t:t + 1],
                                                         r2[:, t:t + 1],
                                                         ALU.subtract, ALU.mult))
                                p.seq(e, e.tensor_tensor(scrd[:, 0:D], scrd[:, 0:D],
                                                         gam_b[:], ALU.mult))
                                p.seq(e, e.tensor_add(scrd[:, 0:D], scrd[:, 0:D],
                                                      bet_b[:]))
                                p.seq(e, e.tensor_add(x_ch[:, t, :], x_ch[:, t, :],
                                                      scrd[:, 0:D]))
                    with nc.Block() as b:
                        @b.sync
                        def _(e, c=c):
                            dst = x_out[c * CN:(c + 1) * CN, :].rearrange(
                                "(t p) f -> p t f", p=128)
                            st.inc(e.dma_start(dst, x_ch[:, 0:CT, :]), 16)
    with nc.Block() as b:
        @b.sync
        def _(e):
            st.wait(e)


def _build_l1f_program():
    stack = ExitStack()
    nc = bass.Bass("TRN2", target_bir_lowering=False)
    x_in = nc.dram_tensor("x_in", [N_TOK, D], F32, kind="ExternalInput").ap()
    W = {
        "Wq": nc.dram_tensor("Wq", [16, 64, D], F32, kind="ExternalInput").ap(),
        "bq": nc.dram_tensor("bq", [16, 64], F32, kind="ExternalInput").ap(),
        "Wk": nc.dram_tensor("Wk", [16, 64, D], F32, kind="ExternalInput").ap(),
        "bk": nc.dram_tensor("bk", [16, 64], F32, kind="ExternalInput").ap(),
        "Wv": nc.dram_tensor("Wv", [16, 64, D], F32, kind="ExternalInput").ap(),
        "bv": nc.dram_tensor("bv", [16, 64], F32, kind="ExternalInput").ap(),
        "We": nc.dram_tensor("We", [8, D, D], F32, kind="ExternalInput").ap(),
        "be": nc.dram_tensor("be", [8, D], F32, kind="ExternalInput").ap(),
        "Wsw": nc.dram_tensor("Wsw", [8, D], F32, kind="ExternalInput").ap(),
        "bsw": nc.dram_tensor("bsw", [8], F32, kind="ExternalInput").ap(),
        "gamma": nc.dram_tensor("gamma", [D], F32, kind="ExternalInput").ap(),
        "beta": nc.dram_tensor("beta", [D], F32, kind="ExternalInput").ap(),
    }
    rank_mask = nc.dram_tensor("rank_mask", [128, 1], F32, kind="ExternalInput").ap()
    x_out = nc.dram_tensor("x_out", [N_TOK, D], F32, kind="ExternalOutput").ap()
    hid_d = nc.dram_tensor("hid_d", [D, N_TOK], BF16, kind="Internal").ap()
    cc_in = nc.dram_tensor("cc_in", [16, 128], F32, kind="Internal").ap()
    cc_out = nc.dram_tensor("cc_out", [128, 128], F32, kind="Internal").ap()
    p = Prog(nc, stack)
    setup_common(p)
    setup_ones_f32(p)
    build_l1_ffn(p, x_in, x_out, W, rank_mask, hid_d, cc_in, cc_out)
    stack.close()
    nc.finalize()
    return nc




def _build_fused_program():
    stack = ExitStack()
    nc = bass.Bass("TRN2", target_bir_lowering=False)
    xin = nc.dram_tensor("L0_attn_x", [N_TOK, D], F32, kind="ExternalInput").ap()
    EI = lambda n, sh: nc.dram_tensor(n, sh, F32, kind="ExternalInput").ap()
    W0a = {"wg_w": EI("wg_w", [D, D]), "wg_b": EI("wg_b", [D]),
           "wq_w": EI("wq_w", [D, D]), "wq_b": EI("wq_b", [D]),
           "wk_w": EI("wk_w", [D, D]), "wk_b": EI("wk_b", [D]),
           "wv_w": EI("wv_w", [D, D]), "wv_b": EI("wv_b", [D]),
           "out_w": EI("out_w", [D, D]), "out_b": EI("out_b", [D])}
    W0f = {"w_gate": EI("w_gate", [2816, D]), "w_up": EI("w_up", [2816, D]),
           "w_down": EI("w_down", [D, 2816])}
    W1a = {"Wq": EI("a_Wq", [D, D]), "Wkv": EI("a_Wkv", [D, 128]),
           "Wk": EI("a_Wk", [128, 512]), "Wv": EI("a_Wv", [128, 512]),
           "Wo": EI("a_Wo", [D, D]), "q_g": EI("a_qg", [64]),
           "k_g": EI("a_kg", [32])}
    W1f = {"Wq": EI("f_Wq", [16, 64, D]), "bq": EI("f_bq", [16, 64]),
           "Wk": EI("f_Wk", [16, 64, D]), "bk": EI("f_bk", [16, 64]),
           "Wv": EI("f_Wv", [16, 64, D]), "bv": EI("f_bv", [16, 64]),
           "We": EI("f_We", [8, D, D]), "be": EI("f_be_in", [8, D]),
           "Wsw": EI("f_Wsw", [8, D]), "bsw": EI("f_bsw_in", [8]),
           "gamma": EI("f_gamma", [D]), "beta": EI("f_beta", [D])}
    rank_mask = EI("rank_mask", [128, 1])
    x_out = nc.dram_tensor("x_out", [N_TOK, D], F32, kind="ExternalOutput").ap()
    x1 = nc.dram_tensor("x1_i", [N_TOK, D], F32, kind="Internal").ap()
    x2 = nc.dram_tensor("x2_i", [N_TOK, D], F32, kind="Internal").ap()
    x3 = nc.dram_tensor("x3_i", [N_TOK, D], F32, kind="Internal").ap()
    hid_d = nc.dram_tensor("hid_d", [D, N_TOK], BF16, kind="Internal").ap()
    cc_in = nc.dram_tensor("cc_in", [16, 128], F32, kind="Internal").ap()
    cc_out = nc.dram_tensor("cc_out", [128, 128], F32, kind="Internal").ap()

    p = Prog(nc, stack)
    setup_common(p)
    setup_zsel(p)
    setup_ones_f32(p)
    build_l0_attn(p, xin, x1, W0a)
    build_l0_ffn(p, x1, x2, W0f)
    build_l1_attn(p, x2, x3, W1a)
    build_l1_ffn(p, x3, x_out, W1f, rank_mask, hid_d, cc_in, cc_out)
    stack.close()
    nc.finalize()
    return nc


def kernel(**inputs):
    cores = list(range(N_CORES))
    x = np.ascontiguousarray(inputs["L0_attn_x"], dtype=np.float32)
    nc = _get_prog("fused", _build_fused_program)
    ren0a = {"wg_w": "L0_attn_wg_w", "wg_b": "L0_attn_wg_b",
             "wq_w": "L0_attn_wq_w", "wq_b": "L0_attn_wq_b",
             "wk_w": "L0_attn_wk_w", "wk_b": "L0_attn_wk_b",
             "wv_w": "L0_attn_wv_w", "wv_b": "L0_attn_wv_b",
             "out_w": "L0_attn_out_w", "out_b": "L0_attn_out_b",
             "w_gate": "L0_ffn_w_gate", "w_up": "L0_ffn_w_up",
             "w_down": "L0_ffn_w_down",
             "a_Wq": "L1_attn_Wq", "a_Wkv": "L1_attn_Wkv",
             "a_Wk": "L1_attn_Wk", "a_Wv": "L1_attn_Wv", "a_Wo": "L1_attn_Wo",
             "a_qg": "L1_attn_q_g", "a_kg": "L1_attn_k_g",
             "f_Wq": "L1_ffn_Wq", "f_bq": "L1_ffn_bq", "f_Wk": "L1_ffn_Wk",
             "f_bk": "L1_ffn_bk", "f_Wv": "L1_ffn_Wv", "f_bv": "L1_ffn_bv",
             "f_We": "L1_ffn_We", "f_be_in": "L1_ffn_be",
             "f_Wsw": "L1_ffn_Wsw", "f_bsw_in": "L1_ffn_bsw",
             "f_gamma": "L1_ffn_gamma", "f_beta": "L1_ffn_beta"}
    in_maps = []
    for c in cores:
        m = {"L0_attn_x": x[c]}
        for k, src in ren0a.items():
            m[k] = np.ascontiguousarray(inputs[src], np.float32)
        rk = np.zeros((128, 1), np.float32)
        for r in range(c):
            rk[16 * r, 0] = 1.0
        m["rank_mask"] = rk
        in_maps.append(m)
    res = run_bass_kernel_spmd(nc, in_maps, core_ids=cores)
    return np.stack([np.asarray(res.results[c]["x_out"], dtype=np.float32)
                     for c in cores])
